# version 04
# baseline (speedup 1.0000x reference)
import numpy as np
import ml_dtypes
from contextlib import ExitStack

import concourse.bass as bass
import concourse.mybir as mybir
from concourse.bass_utils import run_bass_kernel_spmd

F32 = mybir.dt.float32
BF16 = mybir.dt.bfloat16
I32 = mybir.dt.int32
AF = mybir.ActivationFunctionType
ALU = mybir.AluOpType
AX = mybir.AxisListType

T = 2048
D = 1024
NH = 16
DH = 64
DR = 32
KVR = 256
QR = 512
NE = 64
NEL = 8
CAP = 512
FF = 768
SF = 768
NC = 8
TT = T // 128
NEG = -1.0e30
SCALE = 1.0 / float(np.sqrt(DH + DR))


class Ch:
    def __init__(self, ctx, nc, name):
        self.sem = ctx.enter_context(nc.semaphore(name))
        self.n = 0

    def tag(self, inst, k=1):
        inst.then_inc(self.sem, k)
        self.n += k
        return self.n

    def dma(self, inst):
        return self.tag(inst, 16)


def wait_all(eng, chans):
    for c in chans:
        if c.n > 0:
            eng.wait_ge(c.sem, c.n)



def rope_tables():
    inv = 1.0 / 10000.0 ** (np.arange(0, DR, 2, dtype=np.float32) / DR)
    t = np.arange(T, dtype=np.float32)
    emb = np.concatenate([np.outer(t, inv)] * 2, -1)
    cosf = np.ascontiguousarray(np.cos(emb).astype(np.float32).T)
    sinf = np.ascontiguousarray(np.sin(emb).astype(np.float32).T)
    sinf_signed = sinf.copy()
    sinf_signed[0:16, :] *= -1.0
    return (np.ascontiguousarray(np.tile(cosf, (4, 1))),
            np.ascontiguousarray(np.tile(sinf_signed, (4, 1))))


def rope_shuffle_cols(w):
    perm = np.arange(w.shape[1]).reshape(-1, 2, 16)[:, ::-1, :].reshape(-1)
    return np.ascontiguousarray(w[:, perm])


def host_consts():
    c = {}
    c["IDENT"] = np.eye(128, dtype=np.float32)
    c["TRICUM"] = np.triu(np.ones((128, 128), np.float32))
    c["ONES"] = np.ones((128, 128), np.float32)
    kk, qq = np.meshgrid(np.arange(128), np.arange(128), indexing="ij")
    c["MASKTRI"] = np.where(kk > qq, np.float32(NEG), np.float32(0.0)).astype(np.float32)
    c["COS"], c["SIN"] = rope_tables()
    return c



def fm_pair(dst, src, width):
    n = src.shape[0] // 128
    return (dst.rearrange("p (c w) -> p c w", w=width),
            src.rearrange("(c p) w -> p c w", p=128))


def build_l1():
    nc = bass.Bass("TRN2", target_bir_lowering=False)

    def din(name, shape, dt=F32):
        return nc.dram_tensor(name, list(shape), dt, kind="ExternalInput").ap()

    xT = din("xT", (D, T))
    wkv = din("wkv", (D, KVR))
    gkv = din("gkv", (128, 2))
    wkdec = din("wkdec", (KVR, D))
    wvdec = din("wvdec", (KVR, D))
    wq = din("wq", (D, QR))
    wqdec = din("wqdec", (QR, D))
    wkrope = din("wkrope", (D, QR))
    wkropeS = din("wkropeS", (D, QR))
    wqrope = din("wqrope", (QR, QR))
    wqropeS = din("wqropeS", (QR, QR))
    wo = din("wo", (D, D))
    routerT = din("routerT", (D, NE))
    wsgu = din("wsgu", (D, 2 * SF))
    wsdn = din("wsdn", (SF, D))
    IDENT = din("IDENT", (128, 128))
    TRICUM = din("TRICUM", (128, 128))
    ONESC = din("ONES", (128, 128))
    MASKTRI = din("MASKTRI", (128, 128))
    COS = din("COS", (128, T))
    SIN = din("SIN", (128, T))

    o_xsh = nc.dram_tensor("xsh", [D, T], F32, kind="ExternalOutput").ap()
    o_ln2 = nc.dram_tensor("ln2b", [D, T], BF16, kind="ExternalOutput").ap()
    o_cum = nc.dram_tensor("cum", [128, TT * NE], F32, kind="ExternalOutput").ap()
    o_wn = nc.dram_tensor("wnorm", [128, TT * NE], F32, kind="ExternalOutput").ap()
    o_srow = nc.dram_tensor("srow", [1, NE], F32, kind="ExternalOutput").ap()

    x1d = nc.dram_tensor("x1d", [D, T], F32)

    ctx = ExitStack()
    with ctx:
        E = ctx.enter_context
        ch = {k: Ch(ctx, nc, k) for k in ("sld", "sr", "gc0", "gw3", "gw4", "gwo", "gsh",
                                          "gx1", "pe", "dve", "act", "sst")}

        lnb = E(nc.sbuf_tensor("lnb", [128, 8 * T], BF16))
        rows = E(nc.sbuf_tensor("rows", [128, T], F32))
        bcast = E(nc.sbuf_tensor("bcast", [128, T], F32))
        sq = E(nc.sbuf_tensor("sq", [128, T], BF16))
        identb = E(nc.sbuf_tensor("identb", [128, 128], F32))
        trib = E(nc.sbuf_tensor("trib", [128, 128], BF16))
        onesb = E(nc.sbuf_tensor("onesb", [128, 128], BF16))
        maskt = E(nc.sbuf_tensor("maskt", [128, 128], F32))
        gkvb = E(nc.sbuf_tensor("gkvb", [128, 2], F32))
        onesF = E(nc.sbuf_tensor("onesF", [128, 128], F32))
        PS1 = E(nc.psum_tensor("PS1", [128, T], F32))
        PS2 = E(nc.psum_tensor("PS2", [128, T], F32))
        block = E(nc.Block())


        st1 = ExitStack()
        xbf = st1.enter_context(nc.sbuf_tensor("xbf", [128, 8 * T], BF16))

        def p0g(g):
            ch["gc0"].dma(g.dma_start(out=xbf[:].rearrange("p (c w) -> p c w", c=8), in_=xT.rearrange("(c p) w -> p c w", p=128)))
            ch["gc0"].dma(g.dma_start(out=trib[:], in_=TRICUM))
            ch["gc0"].dma(g.dma_start(out=onesb[:], in_=ONESC))
            ch["gc0"].dma(g.dma_start(out=identb[:], in_=IDENT))
            ch["gc0"].dma(g.dma_start(out=maskt[:], in_=MASKTRI))
            ch["gc0"].dma(g.dma_start(out=gkvb[:], in_=gkv))
            ch["gc0"].dma(g.dma_start(out=onesF[:], in_=ONESC))
        block.gpsimd(p0g)
        gld_x = ch["gc0"].n
        sld_p0 = 0

        def emit_ln(src_bf, eps):
            a0 = ch["act"].n
            dve_at = ch["dve"].n
            def summ(t):
                t.wait_ge(ch["gc0"].sem, gld_x)
                if dve_at:
                    t.wait_ge(ch["dve"].sem, dve_at)
                for dc in range(8):
                    for ps in range(4):
                        i = t.matmul(PS1[0:1, ps * 512:ps * 512 + 512], onesb[:, 0:1],
                                     src_bf[:, dc * T + ps * 512: dc * T + ps * 512 + 512],
                                     start=(dc == 0), stop=(dc == 7))
                ch["pe"].tag(i)
            block.tensor(summ)
            pe_cons = []
            for dc in range(8):
                def sqf(s, dc=dc):
                    if dc == 0:
                        s.wait_ge(ch["gc0"].sem, gld_x)
                        if dve_at:
                            s.wait_ge(ch["dve"].sem, dve_at)
                    else:
                        s.wait_ge(ch["pe"].sem, pe_cons[dc - 1])
                    ch["act"].tag(s.activation(out=sq[:], in_=src_bf[:, dc * T:(dc + 1) * T],
                                               func=AF.Square))
                block.scalar(sqf)
                def mmf(t, dc=dc, anow=ch["act"].n):
                    t.wait_ge(ch["act"].sem, anow)
                    for ps in range(4):
                        i = t.matmul(PS2[0:1, ps * 512:ps * 512 + 512], onesb[:, 0:1],
                                     sq[:, ps * 512:ps * 512 + 512],
                                     start=(dc == 0), stop=(dc == 7))
                    ch["pe"].tag(i)
                block.tensor(mmf)
                pe_cons.append(ch["pe"].n)

            def rowmath(v):
                v.wait_ge(ch["pe"].sem, ch["pe"].n)
                ch["dve"].tag(v.tensor_scalar(out=rows[0:1, :], in0=PS1[0:1, 0:T],
                                              scalar1=1.0 / D, scalar2=None, op0=ALU.mult))
                ch["dve"].tag(v.tensor_scalar(out=rows[32:33, :], in0=PS1[0:1, 0:T],
                                              scalar1=-1.0 / D, scalar2=None, op0=ALU.mult))
                ch["dve"].tag(v.tensor_scalar(out=rows[64:65, :], in0=PS2[0:1, 0:T],
                                              scalar1=1.0 / D, scalar2=None, op0=ALU.mult))
                v.wait_ge(ch["dve"].sem, ch["dve"].n)
                ch["dve"].tag(v.tensor_mul(out=PS1[0:1, 0:T], in0=rows[0:1, :], in1=rows[0:1, :]))
                v.wait_ge(ch["dve"].sem, ch["dve"].n)
                ch["dve"].tag(v.tensor_sub(out=rows[64:65, :], in0=rows[64:65, :], in1=PS1[0:1, 0:T]))
                v.wait_ge(ch["dve"].sem, ch["dve"].n)
                ch["dve"].tag(v.tensor_scalar(out=rows[64:65, :], in0=rows[64:65, :],
                                              scalar1=eps, scalar2=None, op0=ALU.add))
            block.vector(rowmath)
            emit_rsqrt_row(64)

        def emit_rsqrt_row(prow):
            def f(s):
                s.wait_ge(ch["dve"].sem, ch["dve"].n)
                ch["act"].tag(s.activation(out=rows[prow:prow + 1, :],
                                           in_=rows[prow:prow + 1, :], func=AF.Ln))
                s.wait_ge(ch["act"].sem, ch["act"].n)
                ch["act"].tag(s.activation(out=rows[prow:prow + 1, :],
                                           in_=rows[prow:prow + 1, :], func=AF.Exp,
                                           scale=-0.5))
            block.scalar(f)

        def emit_bcast(prow, nrows=128):
            def mm(t):
                t.wait_ge(ch["gc0"].sem, gld_x)
                wait_all(t, [ch["dve"], ch["act"]])
                for ps in range(4):
                    i = t.matmul(PS2[0:nrows, ps * 512:ps * 512 + 512],
                                 onesF[prow:prow + 1, 0:nrows],
                                 rows[prow:prow + 1, ps * 512:ps * 512 + 512],
                                 start=True, stop=True)
                ch["pe"].tag(i)
            block.tensor(mm)
            def ev(v):
                v.wait_ge(ch["pe"].sem, ch["pe"].n)
                ch["dve"].tag(v.tensor_copy(out=bcast[0:nrows, :], in_=PS2[0:nrows, 0:T]))
            block.vector(ev)

        def emit_ln_apply(src_bf, dst_bf):
            emit_bcast(32)
            def add(v):
                v.wait_ge(ch["dve"].sem, ch["dve"].n)
                for dc in range(8):
                    i = v.tensor_add(out=dst_bf[:, dc * T:(dc + 1) * T],
                                     in0=src_bf[:, dc * T:(dc + 1) * T], in1=bcast[:])
                ch["dve"].tag(i)
            block.vector(add)
            emit_bcast(64)
            def mul(v):
                v.wait_ge(ch["dve"].sem, ch["dve"].n)
                for dc in range(8):
                    i = v.tensor_mul(out=dst_bf[:, dc * T:(dc + 1) * T],
                                     in0=dst_bf[:, dc * T:(dc + 1) * T], in1=bcast[:])
                ch["dve"].tag(i)
            block.vector(mul)

        emit_ln(xbf, 1e-5)
        emit_ln_apply(xbf, lnb)
        st1.close()

        st2 = ExitStack()
        E2 = st2.enter_context
        qc = E2(nc.sbuf_tensor("qc", [128, 4 * T], BF16))
        kvn = E2(nc.sbuf_tensor("kvn", [128, 2 * T], BF16))
        krp = E2(nc.sbuf_tensor("krp", [128, 4 * T], BF16))
        qrp = E2(nc.sbuf_tensor("qrp", [128, 4 * T], BF16))
        vaug = E2(nc.sbuf_tensor("vaug", [128, TT * NH * 65], BF16))

        st2t = ExitStack()
        E2t = st2t.enter_context
        wkvB = E2t(nc.sbuf_tensor("wkvB", [128, 8 * KVR], BF16))
        wqB = E2t(nc.sbuf_tensor("wqB", [128, 8 * QR], BF16))
        wkrB = E2t(nc.sbuf_tensor("wkrB", [128, 8 * QR], BF16))
        wkrSB = E2t(nc.sbuf_tensor("wkrSB", [128, 8 * QR], BF16))
        wqrB = E2t(nc.sbuf_tensor("wqrB", [128, 4 * QR], BF16))
        wqrSB = E2t(nc.sbuf_tensor("wqrSB", [128, 4 * QR], BF16))
        wvdB = E2t(nc.sbuf_tensor("wvdB", [128, 2 * D], BF16))
        kvc32 = E2t(nc.sbuf_tensor("kvc32", [128, 2 * T], F32))
        cosb = E2t(nc.sbuf_tensor("cosb", [128, T], BF16))
        sinb = E2t(nc.sbuf_tensor("sinb", [128, T], BF16))

        def p3g(g):
            def cast(dst, src, pat):
                ch["gw3"].dma(g.dma_start(out=dst.rearrange("p (c w) -> p c w", c=src.shape[0] // 128), in_=src.rearrange("(c p) w -> p c w", p=128)))
            ch["gw3"].dma(g.dma_start(out=cosb[:], in_=COS))
            ch["gw3"].dma(g.dma_start(out=sinb[:], in_=SIN))
            cast(wkvB[:], wkv, "(dc p) r -> p (dc r)")
            cast(wqB[:], wq, "(dc p) r -> p (dc r)")
            cast(wkrB[:], wkrope, "(dc p) r -> p (dc r)")
            cast(wkrSB[:], wkropeS, "(dc p) r -> p (dc r)")
            cast(wqrB[:], wqrope, "(qc p) r -> p (qc r)")
            cast(wqrSB[:], wqropeS, "(qc p) r -> p (qc r)")
            cast(wvdB[:], wvdec, "(kc p) d -> p (kc d)")
        block.gpsimd(p3g)
        gld_w = ch["gw3"].n

        def proj(dst_bf, wB, n_in, n_out, src_bf, dst_f32=None, wSB=None, wait_ld=None):
            for oc in range(n_out):
                for ps in range(4):
                    def mm(t, oc=oc, ps=ps):
                        if oc == 0 and ps == 0:
                            t.wait_ge(ch["gw3"].sem, wait_ld if wait_ld else ch["gw3"].n)
                            wait_all(t, [ch["dve"]])
                        for ic in range(n_in):
                            i = t.matmul(PS1[:, ps * 512:ps * 512 + 512],
                                         wB[:, ic * n_out * 128 + oc * 128: ic * n_out * 128 + oc * 128 + 128],
                                         src_bf[:, ic * T + ps * 512: ic * T + ps * 512 + 512],
                                         start=(ic == 0), stop=(ic == n_in - 1))
                        if wSB is not None:
                            for ic in range(n_in):
                                i = t.matmul(PS2[:, ps * 512:ps * 512 + 512],
                                             wSB[:, ic * n_out * 128 + oc * 128: ic * n_out * 128 + oc * 128 + 128],
                                             src_bf[:, ic * T + ps * 512: ic * T + ps * 512 + 512],
                                             start=(ic == 0), stop=(ic == n_in - 1))
                        ch["pe"].tag(i)
                    block.tensor(mm)
                    pe_now = ch["pe"].n

                    def ev(v, oc=oc, ps=ps, pe_now=pe_now):
                        v.wait_ge(ch["pe"].sem, pe_now)
                        dsl = slice(oc * T + ps * 512, oc * T + ps * 512 + 512)
                        ssl = slice(ps * 512, ps * 512 + 512)
                        if wSB is None:
                            if dst_f32 is not None:
                                ch["dve"].tag(v.tensor_copy(out=dst_f32[:, dsl], in_=PS1[:, ssl]))
                            ch["dve"].tag(v.tensor_copy(out=dst_bf[:, dsl], in_=PS1[:, ssl]))
                        else:
                            ch["dve"].tag(v.tensor_mul(out=sq[:, 0:512], in0=PS1[:, ssl], in1=cosb[:, ssl]))
                            ch["dve"].tag(v.tensor_mul(out=dst_bf[:, dsl], in0=PS2[:, ssl], in1=sinb[:, ssl]))
                            v.wait_ge(ch["dve"].sem, ch["dve"].n)
                            ch["dve"].tag(v.tensor_add(out=dst_bf[:, dsl], in0=dst_bf[:, dsl], in1=sq[:, 0:512]))
                    block.vector(ev)

                    def gate(t, now=ch["dve"].n):
                        t.wait_ge(ch["dve"].sem, now)
                    block.tensor(gate)

        proj(kvn, wkvB, 8, 2, lnb, dst_f32=kvc32, wait_ld=gld_w)
        kv_pe = []
        kv_dve = ch["dve"].n
        for c2 in range(2):
            def kvsq(s, c2=c2):
                if c2 == 0:
                    s.wait_ge(ch["dve"].sem, kv_dve)
                else:
                    s.wait_ge(ch["pe"].sem, kv_pe[c2 - 1])
                ch["act"].tag(s.activation(out=sq[:], in_=kvc32[:, c2 * T:(c2 + 1) * T],
                                           func=AF.Square))
            block.scalar(kvsq)
            def kvmm(t, c2=c2, anow=ch["act"].n):
                t.wait_ge(ch["act"].sem, anow)
                for ps in range(4):
                    i = t.matmul(PS1[0:1, ps * 512:ps * 512 + 512], onesb[:, 0:1],
                                 sq[:, ps * 512:ps * 512 + 512],
                                 start=(c2 == 0), stop=(c2 == 1))
                ch["pe"].tag(i)
            block.tensor(kvmm)
            kv_pe.append(ch["pe"].n)
        def kvrow(v):
            v.wait_ge(ch["pe"].sem, ch["pe"].n)
            ch["dve"].tag(v.tensor_scalar(out=rows[64:65, :], in0=PS1[0:1, 0:T],
                                          scalar1=1.0 / KVR, scalar2=None, op0=ALU.mult))
            v.wait_ge(ch["dve"].sem, ch["dve"].n)
            ch["dve"].tag(v.tensor_scalar(out=rows[64:65, :], in0=rows[64:65, :],
                                          scalar1=1e-6, scalar2=None, op0=ALU.add))
        block.vector(kvrow)
        emit_rsqrt_row(64)
        emit_bcast(64)
        def kvapply(v):
            v.wait_ge(ch["gc0"].sem, gld_x)
            v.wait_ge(ch["dve"].sem, ch["dve"].n)
            for c2 in range(2):
                ch["dve"].tag(v.tensor_mul(out=kvc32[:, c2 * T:(c2 + 1) * T],
                                           in0=kvc32[:, c2 * T:(c2 + 1) * T], in1=bcast[:]))
            v.wait_ge(ch["dve"].sem, ch["dve"].n)
            for c2 in range(2):
                ch["dve"].tag(v.tensor_scalar(out=kvn[:, c2 * T:(c2 + 1) * T],
                                              in0=kvc32[:, c2 * T:(c2 + 1) * T],
                                              scalar1=gkvb[:, c2:c2 + 1],
                                              scalar2=None, op0=ALU.mult))
        block.vector(kvapply)

        proj(qc, wqB, 8, 4, lnb)
        proj(krp, wkrB, 8, 4, lnb, wSB=wkrSB)
        proj(qrp, wqrB, 4, 4, qc, wSB=wqrSB)

        def vones(v):
            ch["dve"].tag(v.memset(vaug[:].rearrange("p (a b) -> p a b", b=65)[:, :, 64:65], 1.0))
        block.vector(vones)
        for kt in range(TT):
            def vmm(t, kt=kt):
                if kt == 0:
                    wait_all(t, [ch["dve"], ch["gw3"]])
                for half in range(2):
                    for c2 in range(2):
                        i = t.matmul(PS1[:, half * 512:half * 512 + 512],
                                     kvn[:, c2 * T + kt * 128: c2 * T + kt * 128 + 128],
                                     wvdB[:, c2 * D + half * 512: c2 * D + half * 512 + 512],
                                     start=(c2 == 0), stop=(c2 == 1))
                ch["pe"].tag(i)
            block.tensor(vmm)
            pe_now = ch["pe"].n
            def vev(v, kt=kt, pe_now=pe_now):
                v.wait_ge(ch["pe"].sem, pe_now)
                ch["dve"].tag(v.tensor_copy(
                    out=vaug[:, kt * NH * 65:(kt + 1) * NH * 65].rearrange("p (h c) -> p h c", c=65)[:, :, 0:64],
                    in_=PS1[:, 0:1024].rearrange("p (h c) -> p h c", c=64)))
            block.vector(vev)
            def vgate(t, now=ch["dve"].n):
                t.wait_ge(ch["dve"].sem, now)
            block.tensor(vgate)

        st2t.close()

        st3a = ExitStack()
        abuf = st3a.enter_context(nc.sbuf_tensor("abuf", [128, 8 * T], BF16))
        st3b = ExitStack()
        E3 = st3b.enter_context
        wkdB = E3(nc.sbuf_tensor("wkdB", [128, 2 * D], BF16))
        wqdB = E3(nc.sbuf_tensor("wqdB", [128, 4 * D], BF16))
        pt = E3(nc.sbuf_tensor("pt", [128, 2 * 1024], BF16))
        qh = E3(nc.sbuf_tensor("qh", [128, T], BF16))
        kh = E3(nc.sbuf_tensor("kh", [128, T], BF16))

        def p4g(g):
            wait_all(g, [ch["pe"], ch["dve"], ch["act"], ch["gw3"]])
            def cast(dst, src, pat):
                ch["gw4"].dma(g.dma_start(out=dst.rearrange("p (c w) -> p c w", c=src.shape[0] // 128), in_=src.rearrange("(c p) w -> p c w", p=128)))
            cast(wkdB[:], wkdec, "(kc p) d -> p (kc d)")
            cast(wqdB[:], wqdec, "(qc p) d -> p (qc d)")
        block.gpsimd(p4g)
        gld_hw = ch["gw4"].n

        for h in range(NH):
            for ps in range(4):
                def qmm(t, h=h, ps=ps):
                    if h == 0 and ps == 0:
                        t.wait_ge(ch["gw4"].sem, gld_hw)
                        wait_all(t, [ch["dve"]])
                    for ic in range(4):
                        i = t.matmul(PS2[0:64, ps * 512:ps * 512 + 512],
                                     wqdB[:, ic * D + h * 64: ic * D + h * 64 + 64],
                                     qc[:, ic * T + ps * 512: ic * T + ps * 512 + 512],
                                     start=(ic == 0), stop=(ic == 3))
                    for ic in range(2):
                        i = t.matmul(PS1[0:64, ps * 512:ps * 512 + 512],
                                     wkdB[:, ic * D + h * 64: ic * D + h * 64 + 64],
                                     kvn[:, ic * T + ps * 512: ic * T + ps * 512 + 512],
                                     start=(ic == 0), stop=(ic == 1))
                    ch["pe"].tag(i)
                block.tensor(qmm)
                pe_now = ch["pe"].n
                def qev(v, h=h, ps=ps, pe_now=pe_now):
                    v.wait_ge(ch["pe"].sem, pe_now)
                    ssl = slice(ps * 512, ps * 512 + 512)
                    ch["dve"].tag(v.tensor_copy(out=qh[0:64, ssl], in_=PS2[0:64, ssl]))
                    ch["dve"].tag(v.tensor_copy(out=kh[0:64, ssl], in_=PS1[0:64, ssl]))
                block.vector(qev)
                def qgate(t, now=ch["dve"].n):
                    t.wait_ge(ch["dve"].sem, now)
                block.tensor(qgate)
            def rcp(v, h=h):
                rb = (h % 4) * 32
                dc = h // 4
                ch["dve"].tag(v.tensor_copy(out=qh[64:96, :], in_=qrp[rb:rb + 32, dc * T:(dc + 1) * T]))
                ch["dve"].tag(v.tensor_copy(out=kh[64:96, :], in_=krp[rb:rb + 32, dc * T:(dc + 1) * T]))
            block.vector(rcp)
            dve_rope = ch["dve"].n

            secs = []
            for kt in range(TT):
                q0 = kt * 128
                secs.append((kt, q0, min(q0 + 1024, T)))
                if q0 + 1024 < T:
                    secs.append((kt, q0 + 1024, T))
            ns = len(secs)
            act_sec = [None] * ns
            pe_pv = [None] * ns
            def emit_pv(si):
                kt, qlo, qhi = secs[si]
                par = si % 2
                def pvmm(t, si=si, kt=kt, qlo=qlo, qhi=qhi, par=par, h=h):
                    t.wait_ge(ch["act"].sem, act_sec[si])
                    b0 = qlo
                    while b0 < qhi:
                        b1 = min((b0 // 512 + 1) * 512, qhi)
                        lastkt = min(TT - 1, (b1 - 1) // 128)
                        i = t.matmul(PS2[0:65, b0:b1],
                                     vaug[:, kt * NH * 65 + h * 65: kt * NH * 65 + h * 65 + 65],
                                     pt[:, par * 1024 + (b0 - qlo): par * 1024 + (b1 - qlo)],
                                     start=(kt == 0), stop=(kt == lastkt))
                        b0 = b1
                    ch["pe"].tag(i)
                block.tensor(pvmm)
                pe_pv[si] = ch["pe"].n
            for si, (kt, qlo, qhi) in enumerate(secs):
                par = si % 2
                ln = qhi - qlo
                def smm(t, h=h, si=si, kt=kt, qlo=qlo, qhi=qhi, par=par, ln=ln):
                    if si >= 2 and act_sec[si - 2] is not None:
                        t.wait_ge(ch["act"].sem, act_sec[si - 2])
                    if si == 0:
                        t.wait_ge(ch["dve"].sem, dve_rope)
                    nps = (ln + 511) // 512
                    for ps in range(nps):
                        w = min(512, ln - ps * 512)
                        i = t.matmul(PS1[:, par * 1024 + ps * 512: par * 1024 + ps * 512 + w],
                                     kh[0:96, kt * 128:kt * 128 + 128],
                                     qh[0:96, qlo + ps * 512: qlo + ps * 512 + w],
                                     start=True, stop=True)
                    ch["pe"].tag(i)
                block.tensor(smm)
                pe_sc = ch["pe"].n

                diag = (qlo == kt * 128)
                if diag:
                    def msk(v, par=par, pe_sc=pe_sc):
                        v.wait_ge(ch["pe"].sem, pe_sc)
                        v.wait_ge(ch["gc0"].sem, gld_x)
                        ch["dve"].tag(v.tensor_add(out=PS1[:, par * 1024: par * 1024 + 128],
                                                   in0=PS1[:, par * 1024: par * 1024 + 128],
                                                   in1=maskt[:]))
                    block.vector(msk)
                dve_m = ch["dve"].n

                def expf(s, si=si, par=par, ln=ln, pe_sc=pe_sc, dve_m=dve_m, diag=diag):
                    if si >= 2 and pe_pv[si - 2] is not None:
                        s.wait_ge(ch["pe"].sem, pe_pv[si - 2])
                    if diag:
                        s.wait_ge(ch["dve"].sem, dve_m)
                    else:
                        s.wait_ge(ch["pe"].sem, pe_sc)
                    ch["act"].tag(s.activation(out=pt[:, par * 1024: par * 1024 + ln],
                                               in_=PS1[:, par * 1024: par * 1024 + ln],
                                               func=AF.Exp, scale=SCALE))
                block.scalar(expf)
                act_sec[si] = ch["act"].n
                if si >= 1:
                    emit_pv(si - 1)
            emit_pv(ns - 1)

            def nrm_row(s, pv_done=pe_pv[ns - 1]):
                s.wait_ge(ch["pe"].sem, pv_done)
                ch["act"].tag(s.activation(out=rows[0:1, :], in_=PS2[64:65, 0:T], func=AF.Ln))
                s.wait_ge(ch["act"].sem, ch["act"].n)
                ch["act"].tag(s.activation(out=rows[0:1, :], in_=rows[0:1, :],
                                           func=AF.Exp, scale=-1.0))
            block.scalar(nrm_row)
            def nrm_bc(t):
                t.wait_ge(ch["act"].sem, ch["act"].n)
                t.wait_ge(ch["gc0"].sem, gld_x)
                for ps in range(4):
                    i = t.matmul(PS1[0:64, ps * 512:ps * 512 + 512], onesF[0:1, 0:64],
                                 rows[0:1, ps * 512:ps * 512 + 512], start=True, stop=True)
                ch["pe"].tag(i)
            block.tensor(nrm_bc)
            def nrm_ev(v, h=h):
                v.wait_ge(ch["pe"].sem, ch["pe"].n)
                ch["dve"].tag(v.tensor_copy(out=bcast[0:64, :], in_=PS1[0:64, 0:T]))
                v.wait_ge(ch["dve"].sem, ch["dve"].n)
                dc = h // 2
                rb = (h % 2) * 64
                ch["dve"].tag(v.tensor_mul(out=abuf[rb:rb + 64, dc * T:(dc + 1) * T],
                                           in0=PS2[0:64, 0:T], in1=bcast[0:64, :]))
            block.vector(nrm_ev)
            def nrm_gate(t, now=ch["dve"].n):
                t.wait_ge(ch["dve"].sem, now)
            block.tensor(nrm_gate)

        st3b.close()

        st4 = ExitStack()
        woB = st4.enter_context(nc.sbuf_tensor("woB", [128, 8 * D], BF16))
        xstr = st4.enter_context(nc.sbuf_tensor("xstr", [128, T], F32))
        x1f = st4.enter_context(nc.sbuf_tensor("x1f", [128, T], F32))

        def p5g(g):
            wait_all(g, [ch["pe"], ch["dve"], ch["act"], ch["gw4"]])
            ch["gwo"].dma(g.dma_start(out=woB[:].rearrange("p (c w) -> p c w", c=8),
                                      in_=wo.rearrange("(c p) w -> p c w", p=128)))
        block.gpsimd(p5g)
        gld_wo = ch["gwo"].n

        def woph(oc):
            def mm(t, oc=oc):
                if oc == 0:
                    wait_all(t, [ch["dve"]])
                    t.wait_ge(ch["gwo"].sem, gld_wo)
                for ps in range(4):
                    for ic in range(8):
                        i = t.matmul(PS1[:, ps * 512:ps * 512 + 512],
                                     woB[:, ic * D + oc * 128: ic * D + oc * 128 + 128],
                                     abuf[:, ic * T + ps * 512: ic * T + ps * 512 + 512],
                                     start=(ic == 0), stop=(ic == 7))
                ch["pe"].tag(i)
            block.tensor(mm)
            pe_now = ch["pe"].n
            def ld(s, oc=oc):
                if oc == 0:
                    wait_all(s, [ch["pe"], ch["dve"], ch["act"]])
                else:
                    s.wait_ge(ch["dve"].sem, x1tag[oc - 1][0])
                ch["sld"].dma(s.dma_start(out=xstr[:], in_=xT[oc * 128:(oc + 1) * 128, :]))
            block.sync(ld)
            sld_now = ch["sld"].n
            def ev(v, oc=oc, pe_now=pe_now, sld_now=sld_now):
                v.wait_ge(ch["pe"].sem, pe_now)
                v.wait_ge(ch["sld"].sem, sld_now)
                if oc >= 1:
                    v.wait_ge(ch["gx1"].sem, x1tag[oc - 1][1])
                ch["dve"].tag(v.tensor_add(out=x1f[:], in0=PS1[:, 0:T], in1=xstr[:]))
                v.wait_ge(ch["dve"].sem, ch["dve"].n)
                ch["dve"].tag(v.tensor_copy(out=lnb[:, oc * T:(oc + 1) * T], in_=x1f[:]))
            block.vector(ev)
            def st(g, oc=oc, now=ch["dve"].n):
                g.wait_ge(ch["dve"].sem, now)
                ch["gx1"].dma(g.dma_start(out=x1d.ap()[oc * 128:(oc + 1) * 128, :],
                                          in_=x1f[:]))
                x1tag.append((ch["dve"].n, ch["gx1"].n))
            block.gpsimd(st)
            def gate(t, now=ch["dve"].n):
                t.wait_ge(ch["dve"].sem, now)
            block.tensor(gate)
        x1tag = []
        for oc in range(8):
            woph(oc)
        st4.close()
        st3a.close()
        st2.close()

        emit_ln(lnb, 1e-5)
        st5 = ExitStack()
        E5 = st5.enter_context
        xs2 = E5(nc.sbuf_tensor("xs2", [128, T], F32))
        ln2f = E5(nc.sbuf_tensor("ln2f", [128, T], F32))
        routB = E5(nc.sbuf_tensor("routB", [128, 8 * NE], F32))
        nbc = E5(nc.sbuf_tensor("nbc", [128, T], F32))
        rbc = E5(nc.sbuf_tensor("rbc", [128, T], F32))
        logitsT = E5(nc.sbuf_tensor("logitsT", [128, TT * NE], F32))
        lgf = E5(nc.sbuf_tensor("lgf", [128, T], F32))

        def rload(s):
            wait_all(s, [ch["pe"], ch["dve"], ch["act"], ch["gwo"], ch["gx1"]])
            ch["sr"].dma(s.dma_start(out=routB[:].rearrange("p (c w) -> p c w", c=8), in_=routerT.rearrange("(c p) w -> p c w", p=128)))
        block.sync(rload)
        emit_bcast(32)
        def cpn(v):
            v.wait_ge(ch["dve"].sem, ch["dve"].n)
            ch["dve"].tag(v.tensor_copy(out=nbc[:], in_=bcast[:]))
        block.vector(cpn)
        emit_bcast(64)
        def cpr(v):
            v.wait_ge(ch["dve"].sem, ch["dve"].n)
            ch["dve"].tag(v.tensor_copy(out=rbc[:], in_=bcast[:]))
        block.vector(cpr)

        l2tags = []
        for dc in range(8):
            def l2ld(s, dc=dc):
                if dc == 0:
                    wait_all(s, [ch["dve"]])
                else:
                    s.wait_ge(ch["dve"].sem, l2tags[dc - 1])
                ch["sld"].dma(s.dma_start(out=xs2[:], in_=x1d.ap()[dc * 128:(dc + 1) * 128, :]))
            block.sync(l2ld)
            sld_now = ch["sld"].n
            def l2f(v, dc=dc, sld_now=sld_now):
                v.wait_ge(ch["sld"].sem, sld_now)
                if dc > 0:
                    v.wait_ge(ch["pe"].sem, ch["pe"].n)
                ch["dve"].tag(v.tensor_add(out=ln2f[:], in0=xs2[:], in1=nbc[:]))
                l2tags.append(ch["dve"].n)
                v.wait_ge(ch["dve"].sem, ch["dve"].n)
                ch["dve"].tag(v.tensor_mul(out=ln2f[:], in0=ln2f[:], in1=rbc[:]))
            block.vector(l2f)
            def l2mm(t, dc=dc, now=ch["dve"].n):
                t.wait_ge(ch["dve"].sem, now)
                if dc == 0:
                    t.wait_ge(ch["sr"].sem, 16)
                for ps in range(4):
                    i = t.matmul(PS2[0:NE, ps * 512:ps * 512 + 512],
                                 routB[:, dc * NE:(dc + 1) * NE],
                                 ln2f[:, ps * 512:ps * 512 + 512],
                                 start=(dc == 0), stop=(dc == 7))
                ch["pe"].tag(i)
            block.tensor(l2mm)
        def lgev2(v):
            v.wait_ge(ch["pe"].sem, ch["pe"].n)
            ch["dve"].tag(v.tensor_copy(out=lgf[0:NE, 0:T], in_=PS2[0:NE, 0:T]))
        block.vector(lgev2)
        for tt in range(TT):
            def ltr(t, tt=tt, now=ch["dve"].n):
                t.wait_ge(ch["dve"].sem, now)
                t.wait_ge(ch["gc0"].sem, gld_x)
                i = t.transpose(PS1[:, (tt % 8) * 128:(tt % 8) * 128 + NE],
                                lgf[0:NE, tt * 128:(tt + 1) * 128], identb[0:NE, 0:NE])
                ch["pe"].tag(i)
            block.tensor(ltr)
            pe_now = ch["pe"].n
            def lte(v, tt=tt, pe_now=pe_now):
                v.wait_ge(ch["pe"].sem, pe_now)
                ch["dve"].tag(v.tensor_copy(out=logitsT[:, tt * NE:(tt + 1) * NE],
                                            in_=PS1[:, (tt % 8) * 128:(tt % 8) * 128 + NE]))
            block.vector(lte)
            def ltg(t, now=ch["dve"].n):
                t.wait_ge(ch["dve"].sem, now)
            block.tensor(ltg)

        def l2a(v):
            for dc in range(8):
                ch["dve"].tag(v.tensor_add(out=lnb[:, dc * T:(dc + 1) * T],
                                           in0=lnb[:, dc * T:(dc + 1) * T], in1=nbc[:]))
            v.wait_ge(ch["dve"].sem, ch["dve"].n)
            for dc in range(8):
                ch["dve"].tag(v.tensor_mul(out=lnb[:, dc * T:(dc + 1) * T],
                                           in0=lnb[:, dc * T:(dc + 1) * T], in1=rbc[:]))
        block.vector(l2a)

        mx8 = E5(nc.sbuf_tensor("mx8", [128, TT * 8], F32))
        selb = E5(nc.sbuf_tensor("selb", [128, TT * NE], BF16))
        wn = E5(nc.sbuf_tensor("wn", [128, TT * NE], F32))
        wsum = E5(nc.sbuf_tensor("wsum", [128, 2], F32))
        cumo = E5(nc.sbuf_tensor("cumo", [128, TT * NE], F32))
        selw = E5(nc.sbuf_tensor("selw", [128, NE], F32))

        for tt in range(TT):
            lsl = slice(tt * NE, (tt + 1) * NE)
            def r1(v, tt=tt, lsl=lsl, dnow=ch["dve"].n):
                v.wait_ge(ch["dve"].sem, dnow)
                ch["dve"].tag(v.max(out=mx8[:, tt * 8:(tt + 1) * 8], in_=logitsT[:, lsl]))
                v.wait_ge(ch["dve"].sem, ch["dve"].n)
                ch["dve"].tag(v.tensor_scalar(out=selw[:], in0=logitsT[:, lsl],
                                              scalar1=mx8[:, tt * 8 + 7: tt * 8 + 8],
                                              scalar2=None, op0=ALU.is_ge))
                ch["dve"].tag(v.tensor_scalar(out=wsum[:, 1:2], in0=mx8[:, tt * 8: tt * 8 + 1],
                                              scalar1=-1.0, scalar2=None, op0=ALU.mult))
            block.vector(r1)
            def r2(s, tt=tt, lsl=lsl, now=ch["dve"].n):
                s.wait_ge(ch["dve"].sem, now)
                ch["act"].tag(s.activation(out=wn[:, lsl], in_=logitsT[:, lsl],
                                           func=AF.Exp, bias=wsum[:, 1:2]))
            block.scalar(r2)
            def r3(v, tt=tt, lsl=lsl, now=ch["act"].n):
                v.wait_ge(ch["act"].sem, now)
                ch["dve"].tag(v.scalar_tensor_tensor(out=wn[:, lsl], in0=wn[:, lsl],
                                                     scalar=1.0, in1=selw[:],
                                                     op0=ALU.mult, op1=ALU.mult,
                                                     accum_out=wsum[:, 0:1]))
                ch["dve"].tag(v.tensor_copy(out=selb[:, lsl], in_=selw[:]))
            block.vector(r3)
            def r4(s, now=ch["dve"].n):
                s.wait_ge(ch["dve"].sem, now)
                ch["act"].tag(s.activation(out=wsum[:, 1:2], in_=wsum[:, 0:1], func=AF.Ln))
                s.wait_ge(ch["act"].sem, ch["act"].n)
                ch["act"].tag(s.activation(out=wsum[:, 1:2], in_=wsum[:, 1:2],
                                           func=AF.Exp, scale=-1.0))
            block.scalar(r4)
            def r5(v, tt=tt, lsl=lsl, now=ch["act"].n):
                v.wait_ge(ch["act"].sem, now)
                ch["dve"].tag(v.tensor_scalar(out=wn[:, lsl], in0=wn[:, lsl],
                                              scalar1=wsum[:, 1:2], scalar2=None, op0=ALU.mult))
            block.vector(r5)

        def cums(t):
            wait_all(t, [ch["dve"]])
            for qt in range(TT):
                mms = [("tri", qt)] + [("ones", kt) for kt in range(qt)]
                for j, (kind, kt) in enumerate(mms):
                    lhs = trib[:] if kind == "tri" else onesb[:]
                    i = t.matmul(PS1[:, qt * NE:(qt + 1) * NE], lhs,
                                 selb[:, kt * NE:(kt + 1) * NE],
                                 start=(j == 0), stop=(j == len(mms) - 1))
            ch["pe"].tag(i)
        block.tensor(cums)
        def cume(v):
            v.wait_ge(ch["pe"].sem, ch["pe"].n)
            ch["dve"].tag(v.tensor_copy(out=cumo[:], in_=PS1[:, 0:TT * NE]))
        block.vector(cume)

        def outs(s, now=ch["dve"].n):
            s.wait_ge(ch["dve"].sem, now)
            ch["sst"].dma(s.dma_start(out=o_cum, in_=cumo[:]))
            ch["sst"].dma(s.dma_start(out=o_wn, in_=wn[:]))
            ch["sst"].dma(s.dma_start(out=o_srow, in_=cumo[127:128, 15 * NE:16 * NE]))
            ch["sst"].dma(s.dma_start(out=o_ln2.rearrange("(c p) w -> p c w", p=128), in_=lnb[:].rearrange("p (c w) -> p c w", c=8)))
        block.sync(outs)

        wsguB = E5(nc.sbuf_tensor("wsguB", [128, 8 * 2 * SF], BF16))
        wsdnB = E5(nc.sbuf_tensor("wsdnB", [128, 6 * D], BF16))
        hbuf = E5(nc.sbuf_tensor("hbuf", [128, 6 * T], BF16))

        def shg(g):
            wait_all(g, [ch["pe"], ch["dve"], ch["act"]])
            ch["gsh"].dma(g.dma_start(out=wsguB[:].rearrange("p (c w) -> p c w", c=8), in_=wsgu.rearrange("(c p) w -> p c w", p=128)))
            ch["gsh"].dma(g.dma_start(out=wsdnB[:].rearrange("p (c w) -> p c w", c=6), in_=wsdn.rearrange("(c p) w -> p c w", p=128)))
        block.gpsimd(shg)
        gld_sh = ch["gsh"].n

        for fc in range(6):
            for ps in range(4):
                def gmm(t, fc=fc, ps=ps):
                    if fc == 0 and ps == 0:
                        t.wait_ge(ch["gsh"].sem, gld_sh)
                        wait_all(t, [ch["dve"]])
                    for ic in range(8):
                        t.matmul(PS1[:, ps * 512:ps * 512 + 512],
                                 wsguB[:, ic * 2 * SF + fc * 128: ic * 2 * SF + fc * 128 + 128],
                                 lnb[:, ic * T + ps * 512: ic * T + ps * 512 + 512],
                                 start=(ic == 0), stop=(ic == 7))
                    for ic in range(8):
                        i = t.matmul(PS2[:, ps * 512:ps * 512 + 512],
                                     wsguB[:, ic * 2 * SF + SF + fc * 128: ic * 2 * SF + SF + fc * 128 + 128],
                                     lnb[:, ic * T + ps * 512: ic * T + ps * 512 + 512],
                                     start=(ic == 0), stop=(ic == 7))
                    ch["pe"].tag(i)
                block.tensor(gmm)
                pe_now = ch["pe"].n
                def gsl(s, pe_now=pe_now, ps=ps):
                    s.wait_ge(ch["pe"].sem, pe_now)
                    ch["act"].tag(s.activation(out=sq[:, 0:512], in_=PS1[:, ps * 512:ps * 512 + 512],
                                               func=AF.Silu))
                block.scalar(gsl)
                def gmu(v, fc=fc, ps=ps, now=ch["act"].n):
                    v.wait_ge(ch["act"].sem, now)
                    ch["dve"].tag(v.tensor_mul(out=hbuf[:, fc * T + ps * 512: fc * T + ps * 512 + 512],
                                               in0=PS2[:, ps * 512:ps * 512 + 512], in1=sq[:, 0:512]))
                block.vector(gmu)
                def ggate(t, now=ch["dve"].n, na=ch["act"].n):
                    t.wait_ge(ch["dve"].sem, now)
                    t.wait_ge(ch["act"].sem, na)
                block.tensor(ggate)

        devtags = []
        for oc in range(8):
            def dmm(t, oc=oc):
                for ps in range(4):
                    for ic in range(6):
                        i = t.matmul(PS1[:, ps * 512:ps * 512 + 512],
                                     wsdnB[:, ic * D + oc * 128: ic * D + oc * 128 + 128],
                                     hbuf[:, ic * T + ps * 512: ic * T + ps * 512 + 512],
                                     start=(ic == 0), stop=(ic == 5))
                ch["pe"].tag(i)
            block.tensor(dmm)
            pe_now = ch["pe"].n
            def dld(s, oc=oc):
                if oc >= 1:
                    s.wait_ge(ch["dve"].sem, devtags[oc - 1])
                ch["sld"].dma(s.dma_start(out=xs2[:], in_=x1d.ap()[oc * 128:(oc + 1) * 128, :]))
            block.sync(dld)
            sld_now = ch["sld"].n
            def dev(v, oc=oc, pe_now=pe_now, sld_now=sld_now):
                v.wait_ge(ch["pe"].sem, pe_now)
                v.wait_ge(ch["sld"].sem, sld_now)
                ch["dve"].tag(v.tensor_add(out=ln2f[:], in0=PS1[:, 0:T], in1=xs2[:]))
                devtags.append(ch["dve"].n)
            block.vector(dev)
            def dst(s, oc=oc, now=ch["dve"].n):
                s.wait_ge(ch["dve"].sem, now)
                ch["sst"].dma(s.dma_start(out=o_xsh[oc * 128:(oc + 1) * 128, :], in_=ln2f[:]))
            block.sync(dst)
            def dgate(t, now=ch["dve"].n):
                t.wait_ge(ch["dve"].sem, now)
            def dgate2(v, now=ch["sst"].n):
                v.wait_ge(ch["sst"].sem, now)
            block.tensor(dgate)
            block.vector(dgate2)

        st5.close()

        def fin(s):
            s.wait_ge(ch["sst"].sem, ch["sst"].n)
        block.sync(fin)

    return nc


def l1_inputs(x_core, w, consts):
    return {
        "xT": np.ascontiguousarray(x_core.T),
        "wkv": w["L0_attn_W_kv"], "gkv": np.ascontiguousarray(w["L0_attn_g_kv"].reshape(2, 128).T),
        "wkdec": w["L0_attn_W_kdec"], "wvdec": w["L0_attn_W_vdec"],
        "wq": w["L0_attn_W_q"], "wqdec": w["L0_attn_W_qdec"],
        "wkrope": w["L0_attn_W_krope"], "wkropeS": rope_shuffle_cols(w["L0_attn_W_krope"]),
        "wqrope": w["L0_attn_W_qrope"], "wqropeS": rope_shuffle_cols(w["L0_attn_W_qrope"]),
        "wo": w["L0_attn_W_o"],
        "routerT": np.ascontiguousarray(w["L0_ffn_router_w"].T),
        "wsgu": w["L0_ffn_ws_gate_up"], "wsdn": w["L0_ffn_ws_down"],
        "IDENT": consts["IDENT"], "TRICUM": consts["TRICUM"], "ONES": consts["ONES"],
        "MASKTRI": consts["MASKTRI"], "COS": consts["COS"], "SIN": consts["SIN"],
    }


MAPN = 32896


def build_l2():
    nc = bass.Bass("TRN2", target_bir_lowering=False)
    def din(name, shape, dt=F32):
        return nc.dram_tensor(name, list(shape), dt, kind="ExternalInput").ap()
    cum = din("cum", (128, TT * NE))
    wn = din("wnorm", (128, TT * NE))
    srowA = din("srowA", (8, NE))
    cmask = din("cmask", (8, 1))
    tokid1 = din("tokid1", (128, TT))
    E1024W = din("E1024W", (128, TT * NE))
    E8W = din("E8W", (128, TT * NE))
    TRASHW = din("TRASHW", (128, TT * 8))
    ONESC = din("ONES", (128, 128))

    o_map = nc.dram_tensor("mapc", [MAPN, 1], F32, kind="ExternalOutput").ap()
    o_s8 = nc.dram_tensor("slot8", [128, TT * 8], F32, kind="ExternalOutput").ap()
    o_w8 = nc.dram_tensor("w8v", [128, TT * 8], F32, kind="ExternalOutput").ap()

    with ExitStack() as ctx:
        E = ctx.enter_context
        ch = {k: Ch(ctx, nc, k) for k in ("sld", "gsc", "pe", "dve", "act", "sst")}
        scum = E(nc.sbuf_tensor("scum", [128, TT * NE], F32))
        swn = E(nc.sbuf_tensor("swn", [128, TT * NE], F32))
        ssrow = E(nc.sbuf_tensor("ssrow", [8, NE], F32))
        scm = E(nc.sbuf_tensor("scm", [8, 1], F32))
        stok = E(nc.sbuf_tensor("stok", [128, TT], F32))
        se1024 = E(nc.sbuf_tensor("se1024", [128, TT * NE], F32))
        se8 = E(nc.sbuf_tensor("se8", [128, TT * NE], F32))
        strash = E(nc.sbuf_tensor("strash", [128, TT * 8], F32))
        sonesF = E(nc.sbuf_tensor("sonesF", [128, 128], F32))
        obb = E(nc.sbuf_tensor("obb", [128, NE], F32))
        orow = E(nc.sbuf_tensor("orow", [1, NE], F32))
        zet = E(nc.sbuf_tensor("zet", [128, 257], F32))
        t1 = E(nc.sbuf_tensor("t1", [128, TT * NE], F32))
        t2 = E(nc.sbuf_tensor("t2", [128, TT * NE], F32))
        t3 = E(nc.sbuf_tensor("t3", [128, TT * NE], F32))
        sv8a = E(nc.sbuf_tensor("sv8a", [128, TT * 8], F32))
        su8a = E(nc.sbuf_tensor("su8a", [128, TT * 8], F32))
        svi = E(nc.sbuf_tensor("svi", [128, TT * 8], I32))
        svb = E(nc.sbuf_tensor("svb", [128, TT * 8], I32))
        validf = E(nc.sbuf_tensor("validf", [128, TT * 8], F32))
        sslot = E(nc.sbuf_tensor("sslot", [128, TT * 8], F32))
        sw8 = E(nc.sbuf_tensor("sw8", [128, TT * 8], F32))
        sidx = E(nc.sbuf_tensor("sidx", [128, TT * 8], I32))
        PS = E(nc.psum_tensor("PS", [128, 512], F32))
        block = E(nc.Block())

        def lds(s):
            for dst, srcp in ((scum[:], cum), (swn[:], wn), (ssrow[:], srowA),
                              (scm[:], cmask), (stok[:], tokid1), (se1024[:], E1024W),
                              (se8[:], E8W), (strash[:], TRASHW), (sonesF[:], ONESC)):
                ch["sld"].dma(s.dma_start(out=dst, in_=srcp))
        block.sync(lds)
        sld0 = ch["sld"].n

        def zeroV(v):
            ch["dve"].tag(v.memset(zet[:], 0.0))
        block.vector(zeroV)
        dve_z = ch["dve"].n
        def zero(g):
            g.wait_ge(ch["dve"].sem, dve_z)
            ch["gsc"].dma(g.dma_start(out=o_map.rearrange("(p b) v -> p (b v)", p=128), in_=zet[:]))
        block.gpsimd(zero)

        def off(t):
            t.wait_ge(ch["sld"].sem, sld0)
            i = t.matmul(PS[0:1, 0:NE], scm[:], ssrow[:], start=True, stop=True)
            ch["pe"].tag(i)
        block.tensor(off)
        def offe(v):
            v.wait_ge(ch["pe"].sem, ch["pe"].n)
            ch["dve"].tag(v.tensor_copy(out=orow[:], in_=PS[0:1, 0:NE]))
        block.vector(offe)
        def offb(t, now=ch["dve"].n):
            t.wait_ge(ch["dve"].sem, now)
            i = t.matmul(PS[:, 0:NE], sonesF[0:1, :], orow[:], start=True, stop=True)
            ch["pe"].tag(i)
        block.tensor(offb)
        def offbe(v):
            v.wait_ge(ch["pe"].sem, ch["pe"].n)
            ch["dve"].tag(v.tensor_copy(out=obb[:], in_=PS[:, 0:NE]))
        block.vector(offbe)

        def chain(v):
            C = ch["dve"]
            def step(i):
                C.tag(i)
                v.wait_ge(C.sem, C.n)
            v.wait_ge(C.sem, C.n)
            for tt in range(TT):
                C.tag(v.tensor_add(out=t1[:, tt * NE:(tt + 1) * NE],
                                   in0=scum[:, tt * NE:(tt + 1) * NE], in1=obb[:]))
            v.wait_ge(C.sem, C.n)
            step(v.tensor_scalar(out=t1[:], in0=t1[:], scalar1=1.0, scalar2=None, op0=ALU.subtract))
            step(v.tensor_scalar(out=t3[:], in0=swn[:], scalar1=0.0, scalar2=None, op0=ALU.is_gt))
            step(v.tensor_scalar(out=t2[:], in0=t1[:], scalar1=512.0, scalar2=None, op0=ALU.is_lt))
            step(v.tensor_scalar(out=t1[:], in0=t1[:], scalar1=511.0, scalar2=None, op0=ALU.min))
            step(v.tensor_scalar(out=t1[:], in0=t1[:], scalar1=0.0, scalar2=None, op0=ALU.max))
            step(v.tensor_scalar(out=t1[:], in0=t1[:], scalar1=2.0, scalar2=None, op0=ALU.mult))
            step(v.tensor_add(out=t1[:], in0=t1[:], in1=se1024[:]))
            step(v.tensor_add(out=t1[:], in0=t1[:], in1=t2[:]))
            step(v.tensor_scalar(out=t1[:], in0=t1[:], scalar1=65536.0, scalar2=None, op0=ALU.add))
            step(v.tensor_mul(out=t1[:], in0=t1[:], in1=t3[:]))
            step(v.tensor_scalar(out=t1[:], in0=t1[:], scalar1=65536.0, scalar2=None, op0=ALU.subtract))
            for tt in range(TT):
                C.tag(v.max(out=sv8a[:, tt * 8:(tt + 1) * 8], in_=t1[:, tt * NE:(tt + 1) * NE]))
            v.wait_ge(C.sem, C.n)
            step(v.tensor_scalar(out=t2[:], in0=swn[:], scalar1=4.0, scalar2=None, op0=ALU.mult))
            step(v.tensor_add(out=t2[:], in0=t2[:], in1=se8[:]))
            step(v.tensor_scalar(out=t2[:], in0=t2[:], scalar1=1024.0, scalar2=None, op0=ALU.add))
            step(v.tensor_mul(out=t2[:], in0=t2[:], in1=t3[:]))
            step(v.tensor_scalar(out=t2[:], in0=t2[:], scalar1=1024.0, scalar2=None, op0=ALU.subtract))
            for tt in range(TT):
                C.tag(v.max(out=su8a[:, tt * 8:(tt + 1) * 8], in_=t2[:, tt * NE:(tt + 1) * NE]))
            v.wait_ge(C.sem, C.n)
            step(v.tensor_copy(out=svi[:], in_=sv8a[:]))
            step(v.tensor_scalar(out=svb[:], in0=svi[:], scalar1=1, scalar2=None, op0=ALU.bitwise_and))
            step(v.tensor_scalar(out=svi[:], in0=svi[:], scalar1=1, scalar2=None, op0=ALU.arith_shift_right))
            step(v.tensor_copy(out=sslot[:], in_=svi[:]))
            step(v.tensor_copy(out=validf[:], in_=svb[:]))
            step(v.tensor_scalar(out=t1[:, 0:TT * 8], in0=su8a[:], scalar1=0.125, scalar2=None, op0=ALU.mult))
            step(v.tensor_copy(out=svi[:], in_=t1[:, 0:TT * 8]))
            step(v.tensor_copy(out=t1[:, 0:TT * 8], in_=svi[:]))
            step(v.tensor_scalar(out=t1[:, 0:TT * 8], in0=t1[:, 0:TT * 8], scalar1=-8.0, scalar2=None, op0=ALU.mult))
            step(v.tensor_add(out=t1[:, 0:TT * 8], in0=t1[:, 0:TT * 8], in1=su8a[:]))
            step(v.tensor_scalar(out=t1[:, 0:TT * 8], in0=t1[:, 0:TT * 8], scalar1=0.25, scalar2=None, op0=ALU.mult))
            step(v.tensor_scalar(out=t3[:, 0:TT * 8], in0=t1[:, 0:TT * 8], scalar1=0.0, scalar2=None, op0=ALU.is_lt))
            step(v.tensor_scalar(out=t3[:, 0:TT * 8], in0=t3[:, 0:TT * 8], scalar1=2.0, scalar2=None, op0=ALU.mult))
            step(v.tensor_add(out=t1[:, 0:TT * 8], in0=t1[:, 0:TT * 8], in1=t3[:, 0:TT * 8]))
            step(v.tensor_mul(out=sw8[:], in0=t1[:, 0:TT * 8], in1=validf[:]))
            step(v.tensor_scalar(out=t3[:, 0:TT * 8], in0=validf[:], scalar1=-1.0, scalar2=None, op0=ALU.mult))
            step(v.tensor_scalar(out=t3[:, 0:TT * 8], in0=t3[:, 0:TT * 8], scalar1=1.0, scalar2=None, op0=ALU.add))
            step(v.tensor_mul(out=t3[:, 0:TT * 8], in0=t3[:, 0:TT * 8], in1=strash[:]))
            step(v.tensor_mul(out=t2[:, 0:TT * 8], in0=sslot[:], in1=validf[:]))
            step(v.tensor_add(out=t2[:, 0:TT * 8], in0=t2[:, 0:TT * 8], in1=t3[:, 0:TT * 8]))
            step(v.tensor_copy(out=sidx[:], in_=t2[:, 0:TT * 8]))
        block.vector(chain)

        dnow = ch["dve"].n
        def scat(g):
            g.wait_ge(ch["dve"].sem, dnow)
            g.wait_ge(ch["gsc"].sem, 16)
            for tt in range(TT):
                for k in range(8):
                    ch["gsc"].dma(g.indirect_dma_start(
                        out=o_map, out_offset=bass.IndirectOffsetOnAxis(sidx[:, tt * 8 + k: tt * 8 + k + 1], 0),
                        in_=stok[:, tt:tt + 1], in_offset=None))
            g.wait_ge(ch["gsc"].sem, ch["gsc"].n)
        block.gpsimd(scat)

        def outs(s):
            s.wait_ge(ch["dve"].sem, dnow)
            ch["sst"].dma(s.dma_start(out=o_s8, in_=sslot[:]))
            ch["sst"].dma(s.dma_start(out=o_w8, in_=sw8[:]))
            s.wait_ge(ch["sst"].sem, ch["sst"].n)
        block.sync(outs)
    return nc


def build_l3():
    nc = bass.Bass("TRN2", target_bir_lowering=False)
    def din(name, shape, dt=F32):
        return nc.dram_tensor(name, list(shape), dt, kind="ExternalInput").ap()
    maps = din("maps", (8, MAPN))
    ln2r = din("ln2r", (NC * T, D), BF16)
    myrows = din("myrows", (32, 1), I32)
    wg = din("wg", (NEL, D, FF))
    wu = din("wu", (NEL, D, FF))
    wd = din("wd", (NEL, FF, D))
    IDENT = din("IDENT", (128, 128))
    o_y = nc.dram_tensor("y", [NEL * CAP, D], BF16, kind="ExternalOutput").ap()
    mapd = nc.dram_tensor("mapd", [MAPN, 1], F32)

    with ExitStack() as ctx:
        E = ctx.enter_context
        ch = {k: Ch(ctx, nc, k) for k in ("sld", "gw", "gw2", "gg", "gg2", "pe", "dve", "act", "sst")}
        gwch = [ch["gw"], ch["gw2"]]
        ggx = [ch["gg"], ch["gg2"]]
        smaps = E(nc.sbuf_tensor("smaps", [128, 8 * 257], F32))
        identb = E(nc.sbuf_tensor("identb", [128, 128], BF16))
        identF = E(nc.sbuf_tensor("identF", [128, 128], F32))
        trtags = []
        trtags2 = []
        wfree = []
        smyr = E(nc.sbuf_tensor("smyr", [32, 1], I32))
        mext = E(nc.sbuf_tensor("mext", [32, 128], F32))
        mtr = E(nc.sbuf_tensor("mtr", [128, 32], F32))
        xidx = E(nc.sbuf_tensor("xidx", [128, 32], I32))
        xg = [E(nc.sbuf_tensor(f"xg{i}", [128, 1024], BF16)) for i in range(2)]
        xbT = E(nc.sbuf_tensor("xbT", [128, 8 * (NEL * CAP)], BF16))
        hbuf = E(nc.sbuf_tensor("hbuf", [128, 6 * CAP], BF16))
        sqs = E(nc.sbuf_tensor("sqs", [128, 512], BF16))
        yev = E(nc.sbuf_tensor("yev", [128, D], BF16))
        wgB = [E(nc.sbuf_tensor(f"wgB{i}", [128, 8 * FF], BF16)) for i in range(2)]
        wuB = [E(nc.sbuf_tensor(f"wuB{i}", [128, 8 * FF], BF16)) for i in range(2)]
        wdB = [E(nc.sbuf_tensor(f"wdB{i}", [128, 6 * D], BF16)) for i in range(2)]
        PS1 = E(nc.psum_tensor("PS1", [128, 1024], F32))
        PS2 = E(nc.psum_tensor("PS2", [128, 1024], F32))
        PSB = [E(nc.psum_tensor("PSBa", [128, 128], BF16)),
               E(nc.psum_tensor("PSBb", [128, 128], BF16))]
        block = E(nc.Block())

        def lds(s):
            ch["sld"].dma(s.dma_start(out=smaps[:].rearrange("p (c b) -> p c b", c=8),
                                      in_=maps.rearrange("c (p b) -> p c b", p=128)))
            ch["sld"].dma(s.dma_start(out=smyr[:], in_=myrows))
        block.sync(lds)
        sld0 = ch["sld"].n
        def wload(g, e, par):
            for dst, srcp, nch in ((wgB[par], wg, 8), (wuB[par], wu, 8), (wdB[par], wd, 6)):
                gwch[par].dma(g.dma_start(
                    out=dst[:].rearrange("p (c w) -> p c w", c=nch),
                    in_=srcp[e].rearrange("(c p) w -> p c w", p=128)))
        def ldg0(g):
            ch["gg"].dma(g.dma_start(out=identb[:], in_=IDENT))
            ch["gg"].dma(g.dma_start(out=identF[:], in_=IDENT))
            wload(g, 0, 0)
        block.gpsimd(ldg0)
        gw_id = ch["gg"].n

        def msum(v):
            v.wait_ge(ch["sld"].sem, sld0)
            for c in range(1, 8):
                ch["dve"].tag(v.tensor_add(out=smaps[:, 0:257], in0=smaps[:, 0:257],
                                           in1=smaps[:, c * 257:(c + 1) * 257]))
                v.wait_ge(ch["dve"].sem, ch["dve"].n)
        block.vector(msum)

        def mst(g, now=ch["dve"].n):
            g.wait_ge(ch["dve"].sem, now)
            ch["gg"].dma(g.dma_start(out=mapd.ap().rearrange("(p b) v -> p (b v)", p=128),
                                     in_=smaps[:, 0:257]))
            g.wait_ge(ch["gg"].sem, ch["gg"].n)
            ch["gg"].dma(g.indirect_dma_start(
                out=mext[:], out_offset=None,
                in_=mapd.ap()[0:32768, :].rearrange("(r q) v -> r (q v)", q=128),
                in_offset=bass.IndirectOffsetOnAxis(smyr[:, 0:1], 0)))
            g.wait_ge(ch["gg"].sem, ch["gg"].n)
        block.gpsimd(mst)
        gg_ext = ch["gg"].n

        def mtp(t):
            t.wait_ge(ch["gg"].sem, gg_ext)
            i = t.transpose(PS1[:, 0:32], mext[:], identF[0:32, 0:32])
            ch["pe"].tag(i)
        block.tensor(mtp)
        def mte(v):
            v.wait_ge(ch["pe"].sem, ch["pe"].n)
            ch["dve"].tag(v.tensor_copy(out=mtr[:], in_=PS1[:, 0:32]))
            v.wait_ge(ch["dve"].sem, ch["dve"].n)
            ch["dve"].tag(v.tensor_scalar(out=mtr[:], in0=mtr[:], scalar1=1.0, scalar2=None, op0=ALU.max))
            v.wait_ge(ch["dve"].sem, ch["dve"].n)
            ch["dve"].tag(v.tensor_scalar(out=mtr[:], in0=mtr[:], scalar1=1.0, scalar2=None, op0=ALU.subtract))
            v.wait_ge(ch["dve"].sem, ch["dve"].n)
            ch["dve"].tag(v.tensor_copy(out=xidx[:], in_=mtr[:]))
        block.vector(mte)
        dve_idx = ch["dve"].n

        for j in range(32):
            jp = j % 2
            def gat(g, j=j, jp=jp, dve_idx=dve_idx):
                g.wait_ge(ggx[jp].sem, ggx[jp].n)
                if j == 0:
                    g.wait_ge(ch["dve"].sem, dve_idx)
                    g.wait_ge(ch["gg"].sem, ch["gg"].n)
                elif j >= 2:
                    g.wait_ge(ch["pe"].sem, trtags[j - 2])
                ggx[jp].dma(g.indirect_dma_start(
                    out=xg[jp][:], out_offset=None, in_=ln2r,
                    in_offset=bass.IndirectOffsetOnAxis(xidx[:, j:j + 1], 0)))
            block.gpsimd(gat)
            gg_now = ggx[jp].n
            for dc in range(8):
                gi = j * 8 + dc
                def trp(t, j=j, jp=jp, dc=dc, gg_now=gg_now, gi=gi):
                    t.wait_ge(ggx[jp].sem, gg_now)
                    if gi >= 2:
                        t.wait_ge(ch["dve"].sem, trtags2[gi - 2])
                    i = t.transpose(PSB[gi % 2][:, 0:128],
                                    xg[jp][:, dc * 128:(dc + 1) * 128], identb[:])
                    ch["pe"].tag(i)
                block.tensor(trp)
                pe_now = ch["pe"].n
                def tre(v, j=j, dc=dc, pe_now=pe_now, gi=gi):
                    v.wait_ge(ch["pe"].sem, pe_now)
                    ch["dve"].tag(v.tensor_copy(
                        out=xbT[:, dc * (NEL * CAP) + j * 128: dc * (NEL * CAP) + j * 128 + 128],
                        in_=PSB[gi % 2][:, 0:128]))
                block.vector(tre)
                trtags2.append(ch["dve"].n)
            trtags.append(ch["pe"].n)

        gwtags = [ch["gw"].n]

        ysttags = []
        for e in range(NEL):
            par = e % 2
            if e + 1 < NEL:
                def wln(g, e=e, par=par):
                    if e >= 1:
                        g.wait_ge(ch["pe"].sem, wfree[e - 1])
                    wload(g, e + 1, 1 - par)
                block.gpsimd(wln)
                gwtags.append(gwch[(e + 1) % 2].n)
            base = e * CAP
            for fc in range(6):
                def gmm(t, e=e, par=par, fc=fc):
                    if fc == 0:
                        t.wait_ge(gwch[par].sem, gwtags[e])
                        wait_all(t, [ch["dve"]])
                    for ic in range(8):
                        t.matmul(PS1[:, 0:512],
                                 wgB[par][:, ic * FF + fc * 128: ic * FF + fc * 128 + 128],
                                 xbT[:, ic * (NEL * CAP) + base: ic * (NEL * CAP) + base + 512],
                                 start=(ic == 0), stop=(ic == 7))
                    for ic in range(8):
                        i = t.matmul(PS2[:, 0:512],
                                     wuB[par][:, ic * FF + fc * 128: ic * FF + fc * 128 + 128],
                                     xbT[:, ic * (NEL * CAP) + base: ic * (NEL * CAP) + base + 512],
                                     start=(ic == 0), stop=(ic == 7))
                    ch["pe"].tag(i)
                block.tensor(gmm)
                pe_now = ch["pe"].n
                def gsl(s, pe_now=pe_now):
                    s.wait_ge(ch["pe"].sem, pe_now)
                    ch["act"].tag(s.activation(out=sqs[:], in_=PS1[:, 0:512], func=AF.Silu))
                block.scalar(gsl)
                def gmu(v, fc=fc, now=ch["act"].n):
                    v.wait_ge(ch["act"].sem, now)
                    ch["dve"].tag(v.tensor_mul(out=hbuf[:, fc * CAP: fc * CAP + 512],
                                               in0=PS2[:, 0:512], in1=sqs[:]))
                block.vector(gmu)
                def ggate(t, nd=ch["dve"].n):
                    t.wait_ge(ch["dve"].sem, nd)
                block.tensor(ggate)
            for sc in range(4):
                for dp in range(2):
                    def dmm(t, e=e, par=par, sc=sc, dp=dp):
                        for ic in range(6):
                            i = t.matmul(PS1[0:128, dp * 512:dp * 512 + 512],
                                         hbuf[:, ic * CAP + sc * 128: ic * CAP + sc * 128 + 128],
                                         wdB[par][:, ic * D + dp * 512: ic * D + dp * 512 + 512],
                                         start=(ic == 0), stop=(ic == 5))
                        ch["pe"].tag(i)
                    block.tensor(dmm)
                pe_now = ch["pe"].n
                def yevf(v, pe_now=pe_now, snow=len(ysttags)):
                    v.wait_ge(ch["pe"].sem, pe_now)
                    if snow >= 1:
                        v.wait_ge(ch["sst"].sem, ysttags[snow - 1])
                    ch["dve"].tag(v.tensor_copy(out=yev[:], in_=PS1[:, 0:1024]))
                block.vector(yevf)
                def yst(s, e=e, sc=sc, now=ch["dve"].n):
                    s.wait_ge(ch["dve"].sem, now)
                    ch["sst"].dma(s.dma_start(out=o_y[e * CAP + sc * 128: e * CAP + sc * 128 + 128, :],
                                              in_=yev[:]))
                block.sync(yst)
                ysttags.append(ch["sst"].n)
                def dgate(t, nd=ch["dve"].n):
                    t.wait_ge(ch["dve"].sem, nd)
                block.tensor(dgate)
            wfree.append(ch["pe"].n)

        def fin(s):
            s.wait_ge(ch["sst"].sem, ch["sst"].n)
        block.sync(fin)
    return nc


def build_l4():
    nc = bass.Bass("TRN2", target_bir_lowering=False)
    def din(name, shape, dt=F32):
        return nc.dram_tensor(name, list(shape), dt, kind="ExternalInput").ap()
    yfull = din("yfull", (NE * CAP, D), BF16)
    slot8 = din("slot8", (128, TT * 8))
    w8v = din("w8v", (128, TT * 8))
    xsht = din("xsht", (T, D))
    o_out = nc.dram_tensor("out", [T, D], F32, kind="ExternalOutput").ap()

    with ExitStack() as ctx:
        E = ctx.enter_context
        ch = {k: Ch(ctx, nc, k) for k in ("sld", "sx", "gg", "gg2", "pe", "dve", "act", "sst")}
        ggch = [ch["gg"], ch["gg2"]]
        ss8 = E(nc.sbuf_tensor("ss8", [128, TT * 8], F32))
        sw8 = E(nc.sbuf_tensor("sw8", [128, TT * 8], F32))
        sidx = E(nc.sbuf_tensor("sidx", [128, TT * 8], I32))
        yg = [[E(nc.sbuf_tensor(f"yg{i}_{k}", [128, D], BF16)) for k in range(8)]
              for i in range(2)]
        sxsh = [E(nc.sbuf_tensor(f"sxsh{i}", [128, D], F32)) for i in range(2)]
        acc = E(nc.sbuf_tensor("acc", [128, D], F32))
        oval = [E(nc.sbuf_tensor(f"oval{i}", [128, D], F32)) for i in range(2)]
        block = E(nc.Block())

        def lds(s):
            ch["sld"].dma(s.dma_start(out=ss8[:], in_=slot8))
            ch["sld"].dma(s.dma_start(out=sw8[:], in_=w8v))
        block.sync(lds)
        sld0 = ch["sld"].n
        def cvt(v):
            v.wait_ge(ch["sld"].sem, sld0)
            ch["dve"].tag(v.tensor_copy(out=sidx[:], in_=ss8[:]))
        block.vector(cvt)
        dve_idx = ch["dve"].n

        gtags = []
        xtags = []
        dtags = []
        stags = []
        for tt in range(TT):
            par = tt % 2
            def gat(g, tt=tt, par=par):
                if tt == 0:
                    g.wait_ge(ch["dve"].sem, dve_idx)
                elif tt >= 2:
                    g.wait_ge(ch["dve"].sem, dtags[tt - 2][0])
                for k in range(8):
                    ggch[par].dma(g.indirect_dma_start(
                        out=yg[par][k][:], out_offset=None, in_=yfull,
                        in_offset=bass.IndirectOffsetOnAxis(sidx[:, tt * 8 + k: tt * 8 + k + 1], 0)))
            block.gpsimd(gat)
            gtags.append(ggch[par].n)
            def xld(s, tt=tt, par=par):
                if tt >= 2:
                    s.wait_ge(ch["dve"].sem, dtags[tt - 2][1])
                ch["sx"].dma(s.dma_start(out=sxsh[par][:], in_=xsht[tt * 128:(tt + 1) * 128, :]))
            block.sync(xld)
            xtags.append(ch["sx"].n)
            def comb(v, tt=tt, par=par, gnow=gtags[tt], xnow=xtags[tt], dprev=ch["dve"].n):
                v.wait_ge(ggch[par].sem, gnow)
                v.wait_ge(ch["dve"].sem, dprev)
                ch["dve"].tag(v.tensor_scalar(out=acc[:], in0=yg[par][0][:],
                                              scalar1=sw8[:, tt * 8: tt * 8 + 1],
                                              scalar2=None, op0=ALU.mult))
                for k in range(1, 8):
                    v.wait_ge(ch["dve"].sem, ch["dve"].n)
                    ch["dve"].tag(v.scalar_tensor_tensor(out=acc[:], in0=yg[par][k][:],
                                                         scalar=sw8[:, tt * 8 + k: tt * 8 + k + 1],
                                                         in1=acc[:], op0=ALU.mult, op1=ALU.add))
                v.wait_ge(ch["dve"].sem, ch["dve"].n)
                v.wait_ge(ch["sx"].sem, xnow)
                if tt >= 1:
                    v.wait_ge(ch["sst"].sem, stags[tt - 1])
                yg_free = ch["dve"].tag(v.tensor_add(out=oval[par][:], in0=acc[:], in1=sxsh[par][:]))
                dtags.append((yg_free, yg_free))
            block.vector(comb)
            def ost(s, tt=tt, par=par, now=ch["dve"].n):
                s.wait_ge(ch["dve"].sem, now)
                ch["sst"].dma(s.dma_start(out=o_out[tt * 128:(tt + 1) * 128, :], in_=oval[par][:]))
            block.sync(ost)
            stags.append(ch["sst"].n)

        def fin(s):
            s.wait_ge(ch["sst"].sem, ch["sst"].n)
        block.sync(fin)
    return nc


def l2_inputs(r1c, srowA, c, consts):
    cmask = np.zeros((8, 1), np.float32); cmask[:c, 0] = 1.0
    tokid1 = (c * T + 1 + np.arange(128, dtype=np.float32)[:, None]
              + 128.0 * np.arange(TT, dtype=np.float32)[None, :])
    e1024 = np.tile((np.arange(NE, dtype=np.float32) * 1024.0)[None, :], (128, TT))
    e8 = np.tile((np.arange(NE, dtype=np.float32) * 8.0)[None, :], (128, TT))
    trashw = np.tile((32768.0 + np.arange(TT * 8, dtype=np.float32))[None, :], (128, 1))
    return {"cum": r1c["cum"], "wnorm": r1c["wnorm"], "srowA": srowA,
            "cmask": cmask, "tokid1": np.ascontiguousarray(tokid1),
            "E1024W": np.ascontiguousarray(e1024), "E8W": np.ascontiguousarray(e8),
            "TRASHW": np.ascontiguousarray(trashw), "ONES": consts["ONES"]}


def kernel(**inputs):
    x = inputs["L0_attn_x"]
    consts = host_consts()
    consts["E1024"] = np.tile((np.arange(NE, dtype=np.float32) * 1024.0)[None, :], (128, 1))
    consts["E8"] = np.tile((np.arange(NE, dtype=np.float32) * 8.0)[None, :], (128, 1))
    cores = list(range(NC))

    nc1 = build_l1()
    in1 = [l1_inputs(np.ascontiguousarray(x[c]), inputs, consts) for c in cores]
    r1 = run_bass_kernel_spmd(nc1, in1, core_ids=cores).results

    srowA = np.ascontiguousarray(np.stack([r1[c]["srow"][0] for c in cores]))
    nc2 = build_l2()
    in2 = [l2_inputs(r1[c], srowA, c, consts) for c in cores]
    r2 = run_bass_kernel_spmd(nc2, in2, core_ids=cores).results

    maps = np.ascontiguousarray(np.stack([r2[c]["mapc"][:, 0] for c in cores]))
    ln2r = np.ascontiguousarray(
        np.concatenate([r1[c]["ln2b"].T for c in cores], axis=0))
    nc3 = build_l3()
    in3 = []
    for c in cores:
        in3.append({
            "maps": maps, "ln2r": ln2r,
            "myrows": np.arange(c * 32, c * 32 + 32, dtype=np.int32).reshape(32, 1),
            "wg": np.ascontiguousarray(inputs["L0_ffn_w_gate"][c * NEL:(c + 1) * NEL]),
            "wu": np.ascontiguousarray(inputs["L0_ffn_w_up"][c * NEL:(c + 1) * NEL]),
            "wd": np.ascontiguousarray(inputs["L0_ffn_w_down"][c * NEL:(c + 1) * NEL]),
            "IDENT": consts["IDENT"],
        })
    r3 = run_bass_kernel_spmd(nc3, in3, core_ids=cores).results

    yfull = np.ascontiguousarray(np.concatenate([r3[c]["y"] for c in cores], axis=0))
    nc4 = build_l4()
    in4 = []
    for c in cores:
        in4.append({
            "yfull": yfull, "slot8": r2[c]["slot8"], "w8v": r2[c]["w8v"],
            "xsht": np.ascontiguousarray(r1[c]["xsh"].T),
        })
    r4 = run_bass_kernel_spmd(nc4, in4, core_ids=cores).results
    return np.stack([r4[c]["out"] for c in cores]).astype(np.float32)


# version 05
# speedup vs baseline: 1.0574x; 1.0574x over previous
import numpy as np
import ml_dtypes
from contextlib import ExitStack

import concourse.bass as bass
import concourse.mybir as mybir
from concourse.bass_utils import run_bass_kernel_spmd

F32 = mybir.dt.float32
BF16 = mybir.dt.bfloat16
I32 = mybir.dt.int32
AF = mybir.ActivationFunctionType
ALU = mybir.AluOpType
AX = mybir.AxisListType

T = 2048
D = 1024
NH = 16
DH = 64
DR = 32
KVR = 256
QR = 512
NE = 64
NEL = 8
CAP = 512
FF = 768
SF = 768
NC = 8
TT = T // 128
NEG = -1.0e30
SCALE = 1.0 / float(np.sqrt(DH + DR))


class Ch:
    def __init__(self, ctx, nc, name):
        self.sem = ctx.enter_context(nc.semaphore(name))
        self.n = 0

    def tag(self, inst, k=1):
        inst.then_inc(self.sem, k)
        self.n += k
        return self.n

    def dma(self, inst):
        return self.tag(inst, 16)


def wait_all(eng, chans):
    for c in chans:
        if c.n > 0:
            eng.wait_ge(c.sem, c.n)



def rope_tables():
    inv = 1.0 / 10000.0 ** (np.arange(0, DR, 2, dtype=np.float32) / DR)
    t = np.arange(T, dtype=np.float32)
    emb = np.concatenate([np.outer(t, inv)] * 2, -1)
    cosf = np.ascontiguousarray(np.cos(emb).astype(np.float32).T)
    sinf = np.ascontiguousarray(np.sin(emb).astype(np.float32).T)
    sinf_signed = sinf.copy()
    sinf_signed[0:16, :] *= -1.0
    return (np.ascontiguousarray(np.tile(cosf, (4, 1))),
            np.ascontiguousarray(np.tile(sinf_signed, (4, 1))))


def rope_shuffle_cols(w):
    perm = np.arange(w.shape[1]).reshape(-1, 2, 16)[:, ::-1, :].reshape(-1)
    return np.ascontiguousarray(w[:, perm])


def host_consts():
    c = {}
    c["IDENT"] = np.eye(128, dtype=np.float32)
    c["TRICUM"] = np.triu(np.ones((128, 128), np.float32))
    c["ONES"] = np.ones((128, 128), np.float32)
    kk, qq = np.meshgrid(np.arange(128), np.arange(128), indexing="ij")
    c["MASKTRI"] = np.where(kk > qq, np.float32(NEG), np.float32(0.0)).astype(np.float32)
    c["COS"], c["SIN"] = rope_tables()
    return c



def fm_pair(dst, src, width):
    n = src.shape[0] // 128
    return (dst.rearrange("p (c w) -> p c w", w=width),
            src.rearrange("(c p) w -> p c w", p=128))


def build_l1():
    nc = bass.Bass("TRN2", target_bir_lowering=False)

    def din(name, shape, dt=F32):
        return nc.dram_tensor(name, list(shape), dt, kind="ExternalInput").ap()

    xT = din("xT", (D, T))
    wkv = din("wkv", (D, KVR))
    gkv = din("gkv", (128, 2))
    wkdec = din("wkdec", (KVR, D))
    wvdec = din("wvdec", (KVR, D))
    wq = din("wq", (D, QR))
    wqdec = din("wqdec", (QR, D))
    wkrope = din("wkrope", (D, QR))
    wkropeS = din("wkropeS", (D, QR))
    wqrope = din("wqrope", (QR, QR))
    wqropeS = din("wqropeS", (QR, QR))
    wo = din("wo", (D, D))
    routerT = din("routerT", (D, NE))
    wsgu = din("wsgu", (D, 2 * SF))
    wsdn = din("wsdn", (SF, D))
    IDENT = din("IDENT", (128, 128))
    TRICUM = din("TRICUM", (128, 128))
    ONESC = din("ONES", (128, 128))
    MASKTRI = din("MASKTRI", (128, 128))
    COS = din("COS", (128, T))
    SIN = din("SIN", (128, T))

    o_xsh = nc.dram_tensor("xsh", [D, T], F32, kind="ExternalOutput").ap()
    o_ln2 = nc.dram_tensor("ln2b", [D, T], BF16, kind="ExternalOutput").ap()
    o_cum = nc.dram_tensor("cum", [128, TT * NE], F32, kind="ExternalOutput").ap()
    o_wn = nc.dram_tensor("wnorm", [128, TT * NE], F32, kind="ExternalOutput").ap()
    o_srow = nc.dram_tensor("srow", [1, NE], F32, kind="ExternalOutput").ap()

    x1d = nc.dram_tensor("x1d", [D, T], F32)

    ctx = ExitStack()
    with ctx:
        E = ctx.enter_context
        ch = {k: Ch(ctx, nc, k) for k in ("sld", "sr", "gc0", "gw3", "gw4", "gwo", "gsh",
                                          "gx1", "pe", "dve", "act", "sst")}

        lnb = E(nc.sbuf_tensor("lnb", [128, 8 * T], BF16))
        rows = E(nc.sbuf_tensor("rows", [128, T], F32))
        rowsb = E(nc.sbuf_tensor("rowsb", [128, T], BF16))
        bcast = E(nc.sbuf_tensor("bcast", [128, T], BF16))
        sq = E(nc.sbuf_tensor("sq", [128, T], BF16))
        identb = E(nc.sbuf_tensor("identb", [128, 128], F32))
        trib = E(nc.sbuf_tensor("trib", [128, 128], BF16))
        onesb = E(nc.sbuf_tensor("onesb", [128, 128], BF16))
        maskt = E(nc.sbuf_tensor("maskt", [128, 128], F32))
        gkvb = E(nc.sbuf_tensor("gkvb", [128, 2], F32))
        onesF = E(nc.sbuf_tensor("onesF", [128, 128], F32))
        PS1 = E(nc.psum_tensor("PS1", [128, T], F32))
        PS2 = E(nc.psum_tensor("PS2", [128, T], F32))
        block = E(nc.Block())


        st1 = ExitStack()
        xbf = st1.enter_context(nc.sbuf_tensor("xbf", [128, 8 * T], BF16))

        def p0g(g):
            ch["gc0"].dma(g.dma_start(out=xbf[:].rearrange("p (c w) -> p c w", c=8), in_=xT.rearrange("(c p) w -> p c w", p=128)))
            ch["gc0"].dma(g.dma_start(out=trib[:], in_=TRICUM))
            ch["gc0"].dma(g.dma_start(out=onesb[:], in_=ONESC))
            ch["gc0"].dma(g.dma_start(out=identb[:], in_=IDENT))
            ch["gc0"].dma(g.dma_start(out=maskt[:], in_=MASKTRI))
            ch["gc0"].dma(g.dma_start(out=gkvb[:], in_=gkv))
            ch["gc0"].dma(g.dma_start(out=onesF[:], in_=ONESC))
        block.gpsimd(p0g)
        gld_x = ch["gc0"].n
        sld_p0 = 0

        def emit_ln(src_bf, eps):
            a0 = ch["act"].n
            dve_at = ch["dve"].n
            def summ(t):
                t.wait_ge(ch["gc0"].sem, gld_x)
                if dve_at:
                    t.wait_ge(ch["dve"].sem, dve_at)
                for dc in range(8):
                    for ps in range(4):
                        i = t.matmul(PS1[0:1, ps * 512:ps * 512 + 512], onesb[:, 0:1],
                                     src_bf[:, dc * T + ps * 512: dc * T + ps * 512 + 512],
                                     start=(dc == 0), stop=(dc == 7))
                ch["pe"].tag(i)
            block.tensor(summ)
            pe_cons = []
            for dc in range(8):
                def sqf(s, dc=dc):
                    if dc == 0:
                        s.wait_ge(ch["gc0"].sem, gld_x)
                        if dve_at:
                            s.wait_ge(ch["dve"].sem, dve_at)
                    else:
                        s.wait_ge(ch["pe"].sem, pe_cons[dc - 1])
                    ch["act"].tag(s.activation(out=sq[:], in_=src_bf[:, dc * T:(dc + 1) * T],
                                               func=AF.Square))
                block.scalar(sqf)
                def mmf(t, dc=dc, anow=ch["act"].n):
                    t.wait_ge(ch["act"].sem, anow)
                    for ps in range(4):
                        i = t.matmul(PS2[0:1, ps * 512:ps * 512 + 512], onesb[:, 0:1],
                                     sq[:, ps * 512:ps * 512 + 512],
                                     start=(dc == 0), stop=(dc == 7))
                    ch["pe"].tag(i)
                block.tensor(mmf)
                pe_cons.append(ch["pe"].n)

            def rowmath(v):
                v.wait_ge(ch["pe"].sem, ch["pe"].n)
                ch["dve"].tag(v.tensor_scalar(out=rows[0:1, :], in0=PS1[0:1, 0:T],
                                              scalar1=1.0 / D, scalar2=None, op0=ALU.mult))
                ch["dve"].tag(v.tensor_scalar(out=rows[32:33, :], in0=PS1[0:1, 0:T],
                                              scalar1=-1.0 / D, scalar2=None, op0=ALU.mult))
                ch["dve"].tag(v.tensor_scalar(out=rows[64:65, :], in0=PS2[0:1, 0:T],
                                              scalar1=1.0 / D, scalar2=None, op0=ALU.mult))
                v.wait_ge(ch["dve"].sem, ch["dve"].n)
                ch["dve"].tag(v.tensor_mul(out=PS1[0:1, 0:T], in0=rows[0:1, :], in1=rows[0:1, :]))
                v.wait_ge(ch["dve"].sem, ch["dve"].n)
                ch["dve"].tag(v.tensor_sub(out=rows[64:65, :], in0=rows[64:65, :], in1=PS1[0:1, 0:T]))
                v.wait_ge(ch["dve"].sem, ch["dve"].n)
                ch["dve"].tag(v.tensor_scalar(out=rows[64:65, :], in0=rows[64:65, :],
                                              scalar1=eps, scalar2=None, op0=ALU.add))
            block.vector(rowmath)
            emit_rsqrt_row(64)

        def emit_rsqrt_row(prow):
            def f(s):
                s.wait_ge(ch["dve"].sem, ch["dve"].n)
                ch["act"].tag(s.activation(out=rows[prow:prow + 1, :],
                                           in_=rows[prow:prow + 1, :], func=AF.Ln))
                s.wait_ge(ch["act"].sem, ch["act"].n)
                ch["act"].tag(s.activation(out=rows[prow:prow + 1, :],
                                           in_=rows[prow:prow + 1, :], func=AF.Exp,
                                           scale=-0.5))
            block.scalar(f)

        def emit_bcast(prow, nrows=128):
            def cp(v):
                v.wait_ge(ch["dve"].sem, ch["dve"].n)
                wait_all(v, [ch["act"]])
                ch["dve"].tag(v.tensor_copy(out=rowsb[prow:prow + 1, :],
                                            in_=rows[prow:prow + 1, :]))
            block.vector(cp)
            def mm(t, now=ch["dve"].n):
                t.wait_ge(ch["gc0"].sem, gld_x)
                t.wait_ge(ch["dve"].sem, now)
                for ps in range(4):
                    i = t.matmul(PS2[0:nrows, ps * 512:ps * 512 + 512],
                                 onesb[prow:prow + 1, 0:nrows],
                                 rowsb[prow:prow + 1, ps * 512:ps * 512 + 512],
                                 start=True, stop=True)
                ch["pe"].tag(i)
            block.tensor(mm)
            def ev(v):
                v.wait_ge(ch["pe"].sem, ch["pe"].n)
                ch["dve"].tag(v.tensor_copy(out=bcast[0:nrows, :], in_=PS2[0:nrows, 0:T]))
            block.vector(ev)

        def emit_ln_apply(src_bf, dst_bf):
            emit_bcast(32)
            def add(v):
                v.wait_ge(ch["dve"].sem, ch["dve"].n)
                for dc in range(8):
                    i = v.tensor_add(out=dst_bf[:, dc * T:(dc + 1) * T],
                                     in0=src_bf[:, dc * T:(dc + 1) * T], in1=bcast[:])
                ch["dve"].tag(i)
            block.vector(add)
            emit_bcast(64)
            def mul(v):
                v.wait_ge(ch["dve"].sem, ch["dve"].n)
                for dc in range(8):
                    i = v.tensor_mul(out=dst_bf[:, dc * T:(dc + 1) * T],
                                     in0=dst_bf[:, dc * T:(dc + 1) * T], in1=bcast[:])
                ch["dve"].tag(i)
            block.vector(mul)

        emit_ln(xbf, 1e-5)
        emit_ln_apply(xbf, lnb)
        st1.close()

        st2 = ExitStack()
        E2 = st2.enter_context
        qc = E2(nc.sbuf_tensor("qc", [128, 4 * T], BF16))
        kvn = E2(nc.sbuf_tensor("kvn", [128, 2 * T], BF16))
        krp = E2(nc.sbuf_tensor("krp", [128, 4 * T], BF16))
        qrp = E2(nc.sbuf_tensor("qrp", [128, 4 * T], BF16))
        vaug = E2(nc.sbuf_tensor("vaug", [128, TT * NH * 65], BF16))

        st2t = ExitStack()
        E2t = st2t.enter_context
        wkvB = E2t(nc.sbuf_tensor("wkvB", [128, 8 * KVR], BF16))
        wqB = E2t(nc.sbuf_tensor("wqB", [128, 8 * QR], BF16))
        wkrB = E2t(nc.sbuf_tensor("wkrB", [128, 8 * QR], BF16))
        wkrSB = E2t(nc.sbuf_tensor("wkrSB", [128, 8 * QR], BF16))
        wqrB = E2t(nc.sbuf_tensor("wqrB", [128, 4 * QR], BF16))
        wqrSB = E2t(nc.sbuf_tensor("wqrSB", [128, 4 * QR], BF16))
        wvdB = E2t(nc.sbuf_tensor("wvdB", [128, 2 * D], BF16))
        kvc32 = E2t(nc.sbuf_tensor("kvc32", [128, 2 * T], F32))
        cosb = E2t(nc.sbuf_tensor("cosb", [128, T], BF16))
        sinb = E2t(nc.sbuf_tensor("sinb", [128, T], BF16))

        def p3g(g):
            def cast(dst, src, pat):
                ch["gw3"].dma(g.dma_start(out=dst.rearrange("p (c w) -> p c w", c=src.shape[0] // 128), in_=src.rearrange("(c p) w -> p c w", p=128)))
            ch["gw3"].dma(g.dma_start(out=cosb[:], in_=COS))
            ch["gw3"].dma(g.dma_start(out=sinb[:], in_=SIN))
            cast(wkvB[:], wkv, "(dc p) r -> p (dc r)")
            cast(wqB[:], wq, "(dc p) r -> p (dc r)")
            cast(wkrB[:], wkrope, "(dc p) r -> p (dc r)")
            cast(wkrSB[:], wkropeS, "(dc p) r -> p (dc r)")
            cast(wqrB[:], wqrope, "(qc p) r -> p (qc r)")
            cast(wqrSB[:], wqropeS, "(qc p) r -> p (qc r)")
            cast(wvdB[:], wvdec, "(kc p) d -> p (kc d)")
        block.gpsimd(p3g)
        gld_w = ch["gw3"].n

        def proj(dst_bf, wB, n_in, n_out, src_bf, dst_f32=None, wSB=None, wait_ld=None):
            for oc in range(n_out):
                for ps in range(4):
                    def mm(t, oc=oc, ps=ps):
                        if oc == 0 and ps == 0:
                            t.wait_ge(ch["gw3"].sem, wait_ld if wait_ld else ch["gw3"].n)
                            wait_all(t, [ch["dve"]])
                        for ic in range(n_in):
                            i = t.matmul(PS1[:, ps * 512:ps * 512 + 512],
                                         wB[:, ic * n_out * 128 + oc * 128: ic * n_out * 128 + oc * 128 + 128],
                                         src_bf[:, ic * T + ps * 512: ic * T + ps * 512 + 512],
                                         start=(ic == 0), stop=(ic == n_in - 1))
                        if wSB is not None:
                            for ic in range(n_in):
                                i = t.matmul(PS2[:, ps * 512:ps * 512 + 512],
                                             wSB[:, ic * n_out * 128 + oc * 128: ic * n_out * 128 + oc * 128 + 128],
                                             src_bf[:, ic * T + ps * 512: ic * T + ps * 512 + 512],
                                             start=(ic == 0), stop=(ic == n_in - 1))
                        ch["pe"].tag(i)
                    block.tensor(mm)
                    pe_now = ch["pe"].n

                    def ev(v, oc=oc, ps=ps, pe_now=pe_now):
                        v.wait_ge(ch["pe"].sem, pe_now)
                        dsl = slice(oc * T + ps * 512, oc * T + ps * 512 + 512)
                        ssl = slice(ps * 512, ps * 512 + 512)
                        if wSB is None:
                            if dst_f32 is not None:
                                ch["dve"].tag(v.tensor_copy(out=dst_f32[:, dsl], in_=PS1[:, ssl]))
                            ch["dve"].tag(v.tensor_copy(out=dst_bf[:, dsl], in_=PS1[:, ssl]))
                        else:
                            ch["dve"].tag(v.tensor_mul(out=sq[:, 0:512], in0=PS1[:, ssl], in1=cosb[:, ssl]))
                            ch["dve"].tag(v.tensor_mul(out=dst_bf[:, dsl], in0=PS2[:, ssl], in1=sinb[:, ssl]))
                            v.wait_ge(ch["dve"].sem, ch["dve"].n)
                            ch["dve"].tag(v.tensor_add(out=dst_bf[:, dsl], in0=dst_bf[:, dsl], in1=sq[:, 0:512]))
                    block.vector(ev)

                    def gate(t, now=ch["dve"].n):
                        t.wait_ge(ch["dve"].sem, now)
                    block.tensor(gate)

        proj(kvn, wkvB, 8, 2, lnb, dst_f32=kvc32, wait_ld=gld_w)
        kv_pe = []
        kv_dve = ch["dve"].n
        for c2 in range(2):
            def kvsq(s, c2=c2):
                if c2 == 0:
                    s.wait_ge(ch["dve"].sem, kv_dve)
                else:
                    s.wait_ge(ch["pe"].sem, kv_pe[c2 - 1])
                ch["act"].tag(s.activation(out=sq[:], in_=kvc32[:, c2 * T:(c2 + 1) * T],
                                           func=AF.Square))
            block.scalar(kvsq)
            def kvmm(t, c2=c2, anow=ch["act"].n):
                t.wait_ge(ch["act"].sem, anow)
                for ps in range(4):
                    i = t.matmul(PS1[0:1, ps * 512:ps * 512 + 512], onesb[:, 0:1],
                                 sq[:, ps * 512:ps * 512 + 512],
                                 start=(c2 == 0), stop=(c2 == 1))
                ch["pe"].tag(i)
            block.tensor(kvmm)
            kv_pe.append(ch["pe"].n)
        def kvrow(v):
            v.wait_ge(ch["pe"].sem, ch["pe"].n)
            ch["dve"].tag(v.tensor_scalar(out=rows[64:65, :], in0=PS1[0:1, 0:T],
                                          scalar1=1.0 / KVR, scalar2=None, op0=ALU.mult))
            v.wait_ge(ch["dve"].sem, ch["dve"].n)
            ch["dve"].tag(v.tensor_scalar(out=rows[64:65, :], in0=rows[64:65, :],
                                          scalar1=1e-6, scalar2=None, op0=ALU.add))
        block.vector(kvrow)
        emit_rsqrt_row(64)
        emit_bcast(64)
        def kvapply(v):
            v.wait_ge(ch["gc0"].sem, gld_x)
            v.wait_ge(ch["dve"].sem, ch["dve"].n)
            for c2 in range(2):
                ch["dve"].tag(v.tensor_mul(out=kvc32[:, c2 * T:(c2 + 1) * T],
                                           in0=kvc32[:, c2 * T:(c2 + 1) * T], in1=bcast[:]))
            v.wait_ge(ch["dve"].sem, ch["dve"].n)
            for c2 in range(2):
                ch["dve"].tag(v.tensor_scalar(out=kvn[:, c2 * T:(c2 + 1) * T],
                                              in0=kvc32[:, c2 * T:(c2 + 1) * T],
                                              scalar1=gkvb[:, c2:c2 + 1],
                                              scalar2=None, op0=ALU.mult))
        block.vector(kvapply)

        proj(qc, wqB, 8, 4, lnb)
        proj(krp, wkrB, 8, 4, lnb, wSB=wkrSB)
        proj(qrp, wqrB, 4, 4, qc, wSB=wqrSB)

        def vones(v):
            ch["dve"].tag(v.memset(vaug[:].rearrange("p (a b) -> p a b", b=65)[:, :, 64:65], 1.0))
        block.vector(vones)
        for kt in range(TT):
            def vmm(t, kt=kt):
                if kt == 0:
                    wait_all(t, [ch["dve"], ch["gw3"]])
                for half in range(2):
                    for c2 in range(2):
                        i = t.matmul(PS1[:, half * 512:half * 512 + 512],
                                     kvn[:, c2 * T + kt * 128: c2 * T + kt * 128 + 128],
                                     wvdB[:, c2 * D + half * 512: c2 * D + half * 512 + 512],
                                     start=(c2 == 0), stop=(c2 == 1))
                ch["pe"].tag(i)
            block.tensor(vmm)
            pe_now = ch["pe"].n
            def vev(v, kt=kt, pe_now=pe_now):
                v.wait_ge(ch["pe"].sem, pe_now)
                ch["dve"].tag(v.tensor_copy(
                    out=vaug[:, kt * NH * 65:(kt + 1) * NH * 65].rearrange("p (h c) -> p h c", c=65)[:, :, 0:64],
                    in_=PS1[:, 0:1024].rearrange("p (h c) -> p h c", c=64)))
            block.vector(vev)
            def vgate(t, now=ch["dve"].n):
                t.wait_ge(ch["dve"].sem, now)
            block.tensor(vgate)

        st2t.close()

        st3a = ExitStack()
        abuf = st3a.enter_context(nc.sbuf_tensor("abuf", [128, 8 * T], BF16))
        st3b = ExitStack()
        E3 = st3b.enter_context
        wkdB = E3(nc.sbuf_tensor("wkdB", [128, 2 * D], BF16))
        wqdB = E3(nc.sbuf_tensor("wqdB", [128, 4 * D], BF16))
        pt = E3(nc.sbuf_tensor("pt", [128, 2 * 1024], BF16))
        qh = E3(nc.sbuf_tensor("qh", [128, T], BF16))
        kh = E3(nc.sbuf_tensor("kh", [128, T], BF16))

        def p4g(g):
            wait_all(g, [ch["pe"], ch["dve"], ch["act"], ch["gw3"]])
            def cast(dst, src, pat):
                ch["gw4"].dma(g.dma_start(out=dst.rearrange("p (c w) -> p c w", c=src.shape[0] // 128), in_=src.rearrange("(c p) w -> p c w", p=128)))
            cast(wkdB[:], wkdec, "(kc p) d -> p (kc d)")
            cast(wqdB[:], wqdec, "(qc p) d -> p (qc d)")
        block.gpsimd(p4g)
        gld_hw = ch["gw4"].n

        for h in range(NH):
            for ps in range(4):
                def qmm(t, h=h, ps=ps):
                    if h == 0 and ps == 0:
                        t.wait_ge(ch["gw4"].sem, gld_hw)
                        wait_all(t, [ch["dve"]])
                    for ic in range(4):
                        i = t.matmul(PS2[0:64, ps * 512:ps * 512 + 512],
                                     wqdB[:, ic * D + h * 64: ic * D + h * 64 + 64],
                                     qc[:, ic * T + ps * 512: ic * T + ps * 512 + 512],
                                     start=(ic == 0), stop=(ic == 3))
                    for ic in range(2):
                        i = t.matmul(PS1[0:64, ps * 512:ps * 512 + 512],
                                     wkdB[:, ic * D + h * 64: ic * D + h * 64 + 64],
                                     kvn[:, ic * T + ps * 512: ic * T + ps * 512 + 512],
                                     start=(ic == 0), stop=(ic == 1))
                    ch["pe"].tag(i)
                block.tensor(qmm)
                pe_now = ch["pe"].n
                def qev(v, h=h, ps=ps, pe_now=pe_now):
                    v.wait_ge(ch["pe"].sem, pe_now)
                    ssl = slice(ps * 512, ps * 512 + 512)
                    ch["dve"].tag(v.tensor_copy(out=qh[0:64, ssl], in_=PS2[0:64, ssl]))
                    ch["dve"].tag(v.tensor_copy(out=kh[0:64, ssl], in_=PS1[0:64, ssl]))
                block.vector(qev)
                def qgate(t, now=ch["dve"].n):
                    t.wait_ge(ch["dve"].sem, now)
                block.tensor(qgate)
            def rcp(v, h=h):
                rb = (h % 4) * 32
                dc = h // 4
                ch["dve"].tag(v.tensor_copy(out=qh[64:96, :], in_=qrp[rb:rb + 32, dc * T:(dc + 1) * T]))
                ch["dve"].tag(v.tensor_copy(out=kh[64:96, :], in_=krp[rb:rb + 32, dc * T:(dc + 1) * T]))
            block.vector(rcp)
            dve_rope = ch["dve"].n

            secs = []
            for kt in range(TT):
                q0 = kt * 128
                secs.append((kt, q0, min(q0 + 1024, T)))
                if q0 + 1024 < T:
                    secs.append((kt, q0 + 1024, T))
            ns = len(secs)
            act_sec = [None] * ns
            pe_pv = [None] * ns
            def emit_pv(si):
                kt, qlo, qhi = secs[si]
                par = si % 2
                def pvmm(t, si=si, kt=kt, qlo=qlo, qhi=qhi, par=par, h=h):
                    t.wait_ge(ch["act"].sem, act_sec[si])
                    b0 = qlo
                    while b0 < qhi:
                        b1 = min((b0 // 512 + 1) * 512, qhi)
                        lastkt = min(TT - 1, (b1 - 1) // 128)
                        i = t.matmul(PS2[0:65, b0:b1],
                                     vaug[:, kt * NH * 65 + h * 65: kt * NH * 65 + h * 65 + 65],
                                     pt[:, par * 1024 + (b0 - qlo): par * 1024 + (b1 - qlo)],
                                     start=(kt == 0), stop=(kt == lastkt))
                        b0 = b1
                    ch["pe"].tag(i)
                block.tensor(pvmm)
                pe_pv[si] = ch["pe"].n
            for si, (kt, qlo, qhi) in enumerate(secs):
                par = si % 2
                ln = qhi - qlo
                def smm(t, h=h, si=si, kt=kt, qlo=qlo, qhi=qhi, par=par, ln=ln):
                    if si >= 2 and act_sec[si - 2] is not None:
                        t.wait_ge(ch["act"].sem, act_sec[si - 2])
                    if si == 0:
                        t.wait_ge(ch["dve"].sem, dve_rope)
                    nps = (ln + 511) // 512
                    for ps in range(nps):
                        w = min(512, ln - ps * 512)
                        i = t.matmul(PS1[:, par * 1024 + ps * 512: par * 1024 + ps * 512 + w],
                                     kh[0:96, kt * 128:kt * 128 + 128],
                                     qh[0:96, qlo + ps * 512: qlo + ps * 512 + w],
                                     start=True, stop=True)
                    ch["pe"].tag(i)
                block.tensor(smm)
                pe_sc = ch["pe"].n

                diag = (qlo == kt * 128)
                if diag:
                    def msk(v, par=par, pe_sc=pe_sc):
                        v.wait_ge(ch["pe"].sem, pe_sc)
                        v.wait_ge(ch["gc0"].sem, gld_x)
                        ch["dve"].tag(v.tensor_add(out=PS1[:, par * 1024: par * 1024 + 128],
                                                   in0=PS1[:, par * 1024: par * 1024 + 128],
                                                   in1=maskt[:]))
                    block.vector(msk)
                dve_m = ch["dve"].n

                def expf(s, si=si, par=par, ln=ln, pe_sc=pe_sc, dve_m=dve_m, diag=diag):
                    if si >= 2 and pe_pv[si - 2] is not None:
                        s.wait_ge(ch["pe"].sem, pe_pv[si - 2])
                    if diag:
                        s.wait_ge(ch["dve"].sem, dve_m)
                    else:
                        s.wait_ge(ch["pe"].sem, pe_sc)
                    ch["act"].tag(s.activation(out=pt[:, par * 1024: par * 1024 + ln],
                                               in_=PS1[:, par * 1024: par * 1024 + ln],
                                               func=AF.Exp, scale=SCALE))
                block.scalar(expf)
                act_sec[si] = ch["act"].n
                if si >= 1:
                    emit_pv(si - 1)
            emit_pv(ns - 1)

            def nrm_row(s, pv_done=pe_pv[ns - 1]):
                s.wait_ge(ch["pe"].sem, pv_done)
                ch["act"].tag(s.activation(out=rows[0:1, :], in_=PS2[64:65, 0:T], func=AF.Ln))
                s.wait_ge(ch["act"].sem, ch["act"].n)
                ch["act"].tag(s.activation(out=rowsb[0:1, :], in_=rows[0:1, :],
                                           func=AF.Exp, scale=-1.0))
            block.scalar(nrm_row)
            def nrm_bc(t):
                t.wait_ge(ch["act"].sem, ch["act"].n)
                t.wait_ge(ch["gc0"].sem, gld_x)
                for ps in range(4):
                    i = t.matmul(PS1[0:64, ps * 512:ps * 512 + 512], onesb[0:1, 0:64],
                                 rowsb[0:1, ps * 512:ps * 512 + 512], start=True, stop=True)
                ch["pe"].tag(i)
            block.tensor(nrm_bc)
            def nrm_ev(v, h=h):
                v.wait_ge(ch["pe"].sem, ch["pe"].n)
                ch["dve"].tag(v.tensor_copy(out=bcast[0:64, :], in_=PS1[0:64, 0:T]))
                v.wait_ge(ch["dve"].sem, ch["dve"].n)
                dc = h // 2
                rb = (h % 2) * 64
                ch["dve"].tag(v.tensor_mul(out=abuf[rb:rb + 64, dc * T:(dc + 1) * T],
                                           in0=PS2[0:64, 0:T], in1=bcast[0:64, :]))
            block.vector(nrm_ev)
            def nrm_gate(t, now=ch["dve"].n):
                t.wait_ge(ch["dve"].sem, now)
            block.tensor(nrm_gate)

        st3b.close()

        st4 = ExitStack()
        woB = st4.enter_context(nc.sbuf_tensor("woB", [128, 8 * D], BF16))
        xstr = st4.enter_context(nc.sbuf_tensor("xstr", [128, T], F32))
        x1f = st4.enter_context(nc.sbuf_tensor("x1f", [128, T], F32))

        def p5g(g):
            wait_all(g, [ch["pe"], ch["dve"], ch["act"], ch["gw4"]])
            ch["gwo"].dma(g.dma_start(out=woB[:].rearrange("p (c w) -> p c w", c=8),
                                      in_=wo.rearrange("(c p) w -> p c w", p=128)))
        block.gpsimd(p5g)
        gld_wo = ch["gwo"].n

        def woph(oc):
            def mm(t, oc=oc):
                if oc == 0:
                    wait_all(t, [ch["dve"]])
                    t.wait_ge(ch["gwo"].sem, gld_wo)
                for ps in range(4):
                    for ic in range(8):
                        i = t.matmul(PS1[:, ps * 512:ps * 512 + 512],
                                     woB[:, ic * D + oc * 128: ic * D + oc * 128 + 128],
                                     abuf[:, ic * T + ps * 512: ic * T + ps * 512 + 512],
                                     start=(ic == 0), stop=(ic == 7))
                ch["pe"].tag(i)
            block.tensor(mm)
            pe_now = ch["pe"].n
            def ld(s, oc=oc):
                if oc == 0:
                    wait_all(s, [ch["pe"], ch["dve"], ch["act"]])
                else:
                    s.wait_ge(ch["dve"].sem, x1tag[oc - 1][0])
                ch["sld"].dma(s.dma_start(out=xstr[:], in_=xT[oc * 128:(oc + 1) * 128, :]))
            block.sync(ld)
            sld_now = ch["sld"].n
            def ev(v, oc=oc, pe_now=pe_now, sld_now=sld_now):
                v.wait_ge(ch["pe"].sem, pe_now)
                v.wait_ge(ch["sld"].sem, sld_now)
                if oc >= 1:
                    v.wait_ge(ch["gx1"].sem, x1tag[oc - 1][1])
                ch["dve"].tag(v.tensor_add(out=x1f[:], in0=PS1[:, 0:T], in1=xstr[:]))
                v.wait_ge(ch["dve"].sem, ch["dve"].n)
                ch["dve"].tag(v.tensor_copy(out=lnb[:, oc * T:(oc + 1) * T], in_=x1f[:]))
            block.vector(ev)
            def st(g, oc=oc, now=ch["dve"].n):
                g.wait_ge(ch["dve"].sem, now)
                ch["gx1"].dma(g.dma_start(out=x1d.ap()[oc * 128:(oc + 1) * 128, :],
                                          in_=x1f[:]))
                x1tag.append((ch["dve"].n, ch["gx1"].n))
            block.gpsimd(st)
            def gate(t, now=ch["dve"].n):
                t.wait_ge(ch["dve"].sem, now)
            block.tensor(gate)
        x1tag = []
        for oc in range(8):
            woph(oc)
        st4.close()
        st3a.close()
        st2.close()

        emit_ln(lnb, 1e-5)
        st5 = ExitStack()
        E5 = st5.enter_context
        xs2 = E5(nc.sbuf_tensor("xs2", [128, T], F32))
        ln2f = E5(nc.sbuf_tensor("ln2f", [128, T], F32))
        routB = E5(nc.sbuf_tensor("routB", [128, 8 * NE], F32))
        nbc = E5(nc.sbuf_tensor("nbc", [128, T], F32))
        rbc = E5(nc.sbuf_tensor("rbc", [128, T], F32))
        logitsT = E5(nc.sbuf_tensor("logitsT", [128, TT * NE], F32))
        lgf = E5(nc.sbuf_tensor("lgf", [128, T], F32))

        def rload(s):
            wait_all(s, [ch["pe"], ch["dve"], ch["act"], ch["gwo"], ch["gx1"]])
            ch["sr"].dma(s.dma_start(out=routB[:].rearrange("p (c w) -> p c w", c=8), in_=routerT.rearrange("(c p) w -> p c w", p=128)))
        block.sync(rload)
        emit_bcast(32)
        def cpn(v):
            v.wait_ge(ch["dve"].sem, ch["dve"].n)
            ch["dve"].tag(v.tensor_copy(out=nbc[:], in_=bcast[:]))
        block.vector(cpn)
        emit_bcast(64)
        def cpr(v):
            v.wait_ge(ch["dve"].sem, ch["dve"].n)
            ch["dve"].tag(v.tensor_copy(out=rbc[:], in_=bcast[:]))
        block.vector(cpr)

        l2tags = []
        for dc in range(8):
            def l2ld(s, dc=dc):
                if dc == 0:
                    wait_all(s, [ch["dve"]])
                else:
                    s.wait_ge(ch["dve"].sem, l2tags[dc - 1])
                ch["sld"].dma(s.dma_start(out=xs2[:], in_=x1d.ap()[dc * 128:(dc + 1) * 128, :]))
            block.sync(l2ld)
            sld_now = ch["sld"].n
            def l2f(v, dc=dc, sld_now=sld_now):
                v.wait_ge(ch["sld"].sem, sld_now)
                if dc > 0:
                    v.wait_ge(ch["pe"].sem, ch["pe"].n)
                ch["dve"].tag(v.tensor_add(out=ln2f[:], in0=xs2[:], in1=nbc[:]))
                l2tags.append(ch["dve"].n)
                v.wait_ge(ch["dve"].sem, ch["dve"].n)
                ch["dve"].tag(v.tensor_mul(out=ln2f[:], in0=ln2f[:], in1=rbc[:]))
            block.vector(l2f)
            def l2mm(t, dc=dc, now=ch["dve"].n):
                t.wait_ge(ch["dve"].sem, now)
                if dc == 0:
                    t.wait_ge(ch["sr"].sem, 16)
                for ps in range(4):
                    i = t.matmul(PS2[0:NE, ps * 512:ps * 512 + 512],
                                 routB[:, dc * NE:(dc + 1) * NE],
                                 ln2f[:, ps * 512:ps * 512 + 512],
                                 start=(dc == 0), stop=(dc == 7))
                ch["pe"].tag(i)
            block.tensor(l2mm)
        def lgev2(v):
            v.wait_ge(ch["pe"].sem, ch["pe"].n)
            ch["dve"].tag(v.tensor_copy(out=lgf[0:NE, 0:T], in_=PS2[0:NE, 0:T]))
        block.vector(lgev2)
        for tt in range(TT):
            def ltr(t, tt=tt, now=ch["dve"].n):
                t.wait_ge(ch["dve"].sem, now)
                t.wait_ge(ch["gc0"].sem, gld_x)
                i = t.transpose(PS1[:, (tt % 8) * 128:(tt % 8) * 128 + NE],
                                lgf[0:NE, tt * 128:(tt + 1) * 128], identb[0:NE, 0:NE])
                ch["pe"].tag(i)
            block.tensor(ltr)
            pe_now = ch["pe"].n
            def lte(v, tt=tt, pe_now=pe_now):
                v.wait_ge(ch["pe"].sem, pe_now)
                ch["dve"].tag(v.tensor_copy(out=logitsT[:, tt * NE:(tt + 1) * NE],
                                            in_=PS1[:, (tt % 8) * 128:(tt % 8) * 128 + NE]))
            block.vector(lte)
            def ltg(t, now=ch["dve"].n):
                t.wait_ge(ch["dve"].sem, now)
            block.tensor(ltg)

        def l2a(v):
            for dc in range(8):
                ch["dve"].tag(v.tensor_add(out=lnb[:, dc * T:(dc + 1) * T],
                                           in0=lnb[:, dc * T:(dc + 1) * T], in1=nbc[:]))
            v.wait_ge(ch["dve"].sem, ch["dve"].n)
            for dc in range(8):
                ch["dve"].tag(v.tensor_mul(out=lnb[:, dc * T:(dc + 1) * T],
                                           in0=lnb[:, dc * T:(dc + 1) * T], in1=rbc[:]))
        block.vector(l2a)

        mx8 = E5(nc.sbuf_tensor("mx8", [128, TT * 8], F32))
        selb = E5(nc.sbuf_tensor("selb", [128, TT * NE], BF16))
        wn = E5(nc.sbuf_tensor("wn", [128, TT * NE], F32))
        wsum = E5(nc.sbuf_tensor("wsum", [128, 2], F32))
        cumo = E5(nc.sbuf_tensor("cumo", [128, TT * NE], F32))
        selw = E5(nc.sbuf_tensor("selw", [128, NE], F32))

        for tt in range(TT):
            lsl = slice(tt * NE, (tt + 1) * NE)
            def r1(v, tt=tt, lsl=lsl, dnow=ch["dve"].n):
                v.wait_ge(ch["dve"].sem, dnow)
                ch["dve"].tag(v.max(out=mx8[:, tt * 8:(tt + 1) * 8], in_=logitsT[:, lsl]))
                v.wait_ge(ch["dve"].sem, ch["dve"].n)
                ch["dve"].tag(v.tensor_scalar(out=selw[:], in0=logitsT[:, lsl],
                                              scalar1=mx8[:, tt * 8 + 7: tt * 8 + 8],
                                              scalar2=None, op0=ALU.is_ge))
                ch["dve"].tag(v.tensor_scalar(out=wsum[:, 1:2], in0=mx8[:, tt * 8: tt * 8 + 1],
                                              scalar1=-1.0, scalar2=None, op0=ALU.mult))
            block.vector(r1)
            def r2(s, tt=tt, lsl=lsl, now=ch["dve"].n):
                s.wait_ge(ch["dve"].sem, now)
                ch["act"].tag(s.activation(out=wn[:, lsl], in_=logitsT[:, lsl],
                                           func=AF.Exp, bias=wsum[:, 1:2]))
            block.scalar(r2)
            def r3(v, tt=tt, lsl=lsl, now=ch["act"].n):
                v.wait_ge(ch["act"].sem, now)
                ch["dve"].tag(v.scalar_tensor_tensor(out=wn[:, lsl], in0=wn[:, lsl],
                                                     scalar=1.0, in1=selw[:],
                                                     op0=ALU.mult, op1=ALU.mult,
                                                     accum_out=wsum[:, 0:1]))
                ch["dve"].tag(v.tensor_copy(out=selb[:, lsl], in_=selw[:]))
            block.vector(r3)
            def r4(s, now=ch["dve"].n):
                s.wait_ge(ch["dve"].sem, now)
                ch["act"].tag(s.activation(out=wsum[:, 1:2], in_=wsum[:, 0:1], func=AF.Ln))
                s.wait_ge(ch["act"].sem, ch["act"].n)
                ch["act"].tag(s.activation(out=wsum[:, 1:2], in_=wsum[:, 1:2],
                                           func=AF.Exp, scale=-1.0))
            block.scalar(r4)
            def r5(v, tt=tt, lsl=lsl, now=ch["act"].n):
                v.wait_ge(ch["act"].sem, now)
                ch["dve"].tag(v.tensor_scalar(out=wn[:, lsl], in0=wn[:, lsl],
                                              scalar1=wsum[:, 1:2], scalar2=None, op0=ALU.mult))
            block.vector(r5)

        def cums(t):
            wait_all(t, [ch["dve"]])
            for qt in range(TT):
                mms = [("tri", qt)] + [("ones", kt) for kt in range(qt)]
                for j, (kind, kt) in enumerate(mms):
                    lhs = trib[:] if kind == "tri" else onesb[:]
                    i = t.matmul(PS1[:, qt * NE:(qt + 1) * NE], lhs,
                                 selb[:, kt * NE:(kt + 1) * NE],
                                 start=(j == 0), stop=(j == len(mms) - 1))
            ch["pe"].tag(i)
        block.tensor(cums)
        def cume(v):
            v.wait_ge(ch["pe"].sem, ch["pe"].n)
            ch["dve"].tag(v.tensor_copy(out=cumo[:], in_=PS1[:, 0:TT * NE]))
        block.vector(cume)

        def outs(s, now=ch["dve"].n):
            s.wait_ge(ch["dve"].sem, now)
            ch["sst"].dma(s.dma_start(out=o_cum, in_=cumo[:]))
            ch["sst"].dma(s.dma_start(out=o_wn, in_=wn[:]))
            ch["sst"].dma(s.dma_start(out=o_srow, in_=cumo[127:128, 15 * NE:16 * NE]))
            ch["sst"].dma(s.dma_start(out=o_ln2.rearrange("(c p) w -> p c w", p=128), in_=lnb[:].rearrange("p (c w) -> p c w", c=8)))
        block.sync(outs)

        wsguB = E5(nc.sbuf_tensor("wsguB", [128, 8 * 2 * SF], BF16))
        wsdnB = E5(nc.sbuf_tensor("wsdnB", [128, 6 * D], BF16))
        hbuf = E5(nc.sbuf_tensor("hbuf", [128, 6 * T], BF16))

        def shg(g):
            wait_all(g, [ch["pe"], ch["dve"], ch["act"]])
            ch["gsh"].dma(g.dma_start(out=wsguB[:].rearrange("p (c w) -> p c w", c=8), in_=wsgu.rearrange("(c p) w -> p c w", p=128)))
            ch["gsh"].dma(g.dma_start(out=wsdnB[:].rearrange("p (c w) -> p c w", c=6), in_=wsdn.rearrange("(c p) w -> p c w", p=128)))
        block.gpsimd(shg)
        gld_sh = ch["gsh"].n

        gu_tags = []
        for fc in range(6):
            for ps in range(4):
                idx = fc * 4 + ps
                def gmm(t, fc=fc, ps=ps, idx=idx):
                    if idx == 0:
                        t.wait_ge(ch["gsh"].sem, gld_sh)
                        wait_all(t, [ch["dve"]])
                    elif idx >= 4:
                        t.wait_ge(ch["dve"].sem, gu_tags[idx - 4])
                    for ic in range(8):
                        t.matmul(PS1[:, ps * 512:ps * 512 + 512],
                                 wsguB[:, ic * 2 * SF + fc * 128: ic * 2 * SF + fc * 128 + 128],
                                 lnb[:, ic * T + ps * 512: ic * T + ps * 512 + 512],
                                 start=(ic == 0), stop=(ic == 7))
                    for ic in range(8):
                        i = t.matmul(PS2[:, ps * 512:ps * 512 + 512],
                                     wsguB[:, ic * 2 * SF + SF + fc * 128: ic * 2 * SF + SF + fc * 128 + 128],
                                     lnb[:, ic * T + ps * 512: ic * T + ps * 512 + 512],
                                     start=(ic == 0), stop=(ic == 7))
                    ch["pe"].tag(i)
                block.tensor(gmm)
                pe_now = ch["pe"].n
                def gsl(s, pe_now=pe_now, ps=ps):
                    s.wait_ge(ch["pe"].sem, pe_now)
                    ch["act"].tag(s.activation(out=sq[:, ps * 512:ps * 512 + 512],
                                               in_=PS1[:, ps * 512:ps * 512 + 512],
                                               func=AF.Silu))
                block.scalar(gsl)
                def gmu(v, fc=fc, ps=ps, now=ch["act"].n):
                    v.wait_ge(ch["act"].sem, now)
                    ch["dve"].tag(v.tensor_mul(out=hbuf[:, fc * T + ps * 512: fc * T + ps * 512 + 512],
                                               in0=PS2[:, ps * 512:ps * 512 + 512],
                                               in1=sq[:, ps * 512:ps * 512 + 512]))
                block.vector(gmu)
                gu_tags.append(ch["dve"].n)

        devtags = []
        gu_last = ch["dve"].n
        for oc in range(8):
            def dmm(t, oc=oc):
                if oc == 0:
                    t.wait_ge(ch["dve"].sem, gu_last)
                for ps in range(4):
                    for ic in range(6):
                        i = t.matmul(PS1[:, ps * 512:ps * 512 + 512],
                                     wsdnB[:, ic * D + oc * 128: ic * D + oc * 128 + 128],
                                     hbuf[:, ic * T + ps * 512: ic * T + ps * 512 + 512],
                                     start=(ic == 0), stop=(ic == 5))
                ch["pe"].tag(i)
            block.tensor(dmm)
            pe_now = ch["pe"].n
            def dld(s, oc=oc):
                if oc >= 1:
                    s.wait_ge(ch["dve"].sem, devtags[oc - 1])
                ch["sld"].dma(s.dma_start(out=xs2[:], in_=x1d.ap()[oc * 128:(oc + 1) * 128, :]))
            block.sync(dld)
            sld_now = ch["sld"].n
            def dev(v, oc=oc, pe_now=pe_now, sld_now=sld_now):
                v.wait_ge(ch["pe"].sem, pe_now)
                v.wait_ge(ch["sld"].sem, sld_now)
                ch["dve"].tag(v.tensor_add(out=ln2f[:], in0=PS1[:, 0:T], in1=xs2[:]))
                devtags.append(ch["dve"].n)
            block.vector(dev)
            def dst(s, oc=oc, now=ch["dve"].n):
                s.wait_ge(ch["dve"].sem, now)
                ch["sst"].dma(s.dma_start(out=o_xsh[oc * 128:(oc + 1) * 128, :], in_=ln2f[:]))
            block.sync(dst)
            def dgate(t, now=ch["dve"].n):
                t.wait_ge(ch["dve"].sem, now)
            def dgate2(v, now=ch["sst"].n):
                v.wait_ge(ch["sst"].sem, now)
            block.tensor(dgate)
            block.vector(dgate2)

        st5.close()

        def fin(s):
            s.wait_ge(ch["sst"].sem, ch["sst"].n)
        block.sync(fin)

    return nc


def l1_inputs(x_core, w, consts):
    return {
        "xT": np.ascontiguousarray(x_core.T),
        "wkv": w["L0_attn_W_kv"], "gkv": np.ascontiguousarray(w["L0_attn_g_kv"].reshape(2, 128).T),
        "wkdec": w["L0_attn_W_kdec"], "wvdec": w["L0_attn_W_vdec"],
        "wq": w["L0_attn_W_q"], "wqdec": w["L0_attn_W_qdec"],
        "wkrope": w["L0_attn_W_krope"], "wkropeS": rope_shuffle_cols(w["L0_attn_W_krope"]),
        "wqrope": w["L0_attn_W_qrope"], "wqropeS": rope_shuffle_cols(w["L0_attn_W_qrope"]),
        "wo": w["L0_attn_W_o"],
        "routerT": np.ascontiguousarray(w["L0_ffn_router_w"].T),
        "wsgu": w["L0_ffn_ws_gate_up"], "wsdn": w["L0_ffn_ws_down"],
        "IDENT": consts["IDENT"], "TRICUM": consts["TRICUM"], "ONES": consts["ONES"],
        "MASKTRI": consts["MASKTRI"], "COS": consts["COS"], "SIN": consts["SIN"],
    }


MAPN = 32896


def build_l2():
    nc = bass.Bass("TRN2", target_bir_lowering=False)
    def din(name, shape, dt=F32):
        return nc.dram_tensor(name, list(shape), dt, kind="ExternalInput").ap()
    cum = din("cum", (128, TT * NE))
    wn = din("wnorm", (128, TT * NE))
    srowA = din("srowA", (8, NE))
    cmask = din("cmask", (8, 1))
    tokid1 = din("tokid1", (128, TT))
    E1024W = din("E1024W", (128, TT * NE))
    E8W = din("E8W", (128, TT * NE))
    TRASHW = din("TRASHW", (128, TT * 8))
    ONESC = din("ONES", (128, 128))

    o_map = nc.dram_tensor("mapc", [MAPN, 1], F32, kind="ExternalOutput").ap()
    o_s8 = nc.dram_tensor("slot8", [128, TT * 8], F32, kind="ExternalOutput").ap()
    o_w8 = nc.dram_tensor("w8v", [128, TT * 8], F32, kind="ExternalOutput").ap()

    with ExitStack() as ctx:
        E = ctx.enter_context
        ch = {k: Ch(ctx, nc, k) for k in ("sld", "gsc", "pe", "dve", "act", "sst")}
        scum = E(nc.sbuf_tensor("scum", [128, TT * NE], F32))
        swn = E(nc.sbuf_tensor("swn", [128, TT * NE], F32))
        ssrow = E(nc.sbuf_tensor("ssrow", [8, NE], F32))
        scm = E(nc.sbuf_tensor("scm", [8, 1], F32))
        stok = E(nc.sbuf_tensor("stok", [128, TT], F32))
        se1024 = E(nc.sbuf_tensor("se1024", [128, TT * NE], F32))
        se8 = E(nc.sbuf_tensor("se8", [128, TT * NE], F32))
        strash = E(nc.sbuf_tensor("strash", [128, TT * 8], F32))
        sonesF = E(nc.sbuf_tensor("sonesF", [128, 128], F32))
        obb = E(nc.sbuf_tensor("obb", [128, NE], F32))
        orow = E(nc.sbuf_tensor("orow", [1, NE], F32))
        zet = E(nc.sbuf_tensor("zet", [128, 257], F32))
        t1 = E(nc.sbuf_tensor("t1", [128, TT * NE], F32))
        t2 = E(nc.sbuf_tensor("t2", [128, TT * NE], F32))
        t3 = E(nc.sbuf_tensor("t3", [128, TT * NE], F32))
        sv8a = E(nc.sbuf_tensor("sv8a", [128, TT * 8], F32))
        su8a = E(nc.sbuf_tensor("su8a", [128, TT * 8], F32))
        svi = E(nc.sbuf_tensor("svi", [128, TT * 8], I32))
        svb = E(nc.sbuf_tensor("svb", [128, TT * 8], I32))
        validf = E(nc.sbuf_tensor("validf", [128, TT * 8], F32))
        sslot = E(nc.sbuf_tensor("sslot", [128, TT * 8], F32))
        sw8 = E(nc.sbuf_tensor("sw8", [128, TT * 8], F32))
        sidx = E(nc.sbuf_tensor("sidx", [128, TT * 8], I32))
        PS = E(nc.psum_tensor("PS", [128, 512], F32))
        block = E(nc.Block())

        def lds(s):
            for dst, srcp in ((scum[:], cum), (swn[:], wn), (ssrow[:], srowA),
                              (scm[:], cmask), (stok[:], tokid1), (se1024[:], E1024W),
                              (se8[:], E8W), (strash[:], TRASHW), (sonesF[:], ONESC)):
                ch["sld"].dma(s.dma_start(out=dst, in_=srcp))
        block.sync(lds)
        sld0 = ch["sld"].n

        def zeroV(v):
            ch["dve"].tag(v.memset(zet[:], 0.0))
        block.vector(zeroV)
        dve_z = ch["dve"].n
        def zero(g):
            g.wait_ge(ch["dve"].sem, dve_z)
            ch["gsc"].dma(g.dma_start(out=o_map.rearrange("(p b) v -> p (b v)", p=128), in_=zet[:]))
        block.gpsimd(zero)

        def off(t):
            t.wait_ge(ch["sld"].sem, sld0)
            i = t.matmul(PS[0:1, 0:NE], scm[:], ssrow[:], start=True, stop=True)
            ch["pe"].tag(i)
        block.tensor(off)
        def offe(v):
            v.wait_ge(ch["pe"].sem, ch["pe"].n)
            ch["dve"].tag(v.tensor_copy(out=orow[:], in_=PS[0:1, 0:NE]))
        block.vector(offe)
        def offb(t, now=ch["dve"].n):
            t.wait_ge(ch["dve"].sem, now)
            i = t.matmul(PS[:, 0:NE], sonesF[0:1, :], orow[:], start=True, stop=True)
            ch["pe"].tag(i)
        block.tensor(offb)
        def offbe(v):
            v.wait_ge(ch["pe"].sem, ch["pe"].n)
            ch["dve"].tag(v.tensor_copy(out=obb[:], in_=PS[:, 0:NE]))
        block.vector(offbe)

        def chain(v):
            C = ch["dve"]
            def step(i):
                C.tag(i)
                v.wait_ge(C.sem, C.n)
            v.wait_ge(C.sem, C.n)
            for tt in range(TT):
                C.tag(v.tensor_add(out=t1[:, tt * NE:(tt + 1) * NE],
                                   in0=scum[:, tt * NE:(tt + 1) * NE], in1=obb[:]))
            v.wait_ge(C.sem, C.n)
            step(v.tensor_scalar(out=t1[:], in0=t1[:], scalar1=1.0, scalar2=None, op0=ALU.subtract))
            step(v.tensor_scalar(out=t3[:], in0=swn[:], scalar1=0.0, scalar2=None, op0=ALU.is_gt))
            step(v.tensor_scalar(out=t2[:], in0=t1[:], scalar1=512.0, scalar2=None, op0=ALU.is_lt))
            step(v.tensor_scalar(out=t1[:], in0=t1[:], scalar1=511.0, scalar2=None, op0=ALU.min))
            step(v.tensor_scalar(out=t1[:], in0=t1[:], scalar1=0.0, scalar2=None, op0=ALU.max))
            step(v.tensor_scalar(out=t1[:], in0=t1[:], scalar1=2.0, scalar2=None, op0=ALU.mult))
            step(v.tensor_add(out=t1[:], in0=t1[:], in1=se1024[:]))
            step(v.tensor_add(out=t1[:], in0=t1[:], in1=t2[:]))
            step(v.tensor_scalar(out=t1[:], in0=t1[:], scalar1=65536.0, scalar2=None, op0=ALU.add))
            step(v.tensor_mul(out=t1[:], in0=t1[:], in1=t3[:]))
            step(v.tensor_scalar(out=t1[:], in0=t1[:], scalar1=65536.0, scalar2=None, op0=ALU.subtract))
            for tt in range(TT):
                C.tag(v.max(out=sv8a[:, tt * 8:(tt + 1) * 8], in_=t1[:, tt * NE:(tt + 1) * NE]))
            v.wait_ge(C.sem, C.n)
            step(v.tensor_scalar(out=t2[:], in0=swn[:], scalar1=4.0, scalar2=None, op0=ALU.mult))
            step(v.tensor_add(out=t2[:], in0=t2[:], in1=se8[:]))
            step(v.tensor_scalar(out=t2[:], in0=t2[:], scalar1=1024.0, scalar2=None, op0=ALU.add))
            step(v.tensor_mul(out=t2[:], in0=t2[:], in1=t3[:]))
            step(v.tensor_scalar(out=t2[:], in0=t2[:], scalar1=1024.0, scalar2=None, op0=ALU.subtract))
            for tt in range(TT):
                C.tag(v.max(out=su8a[:, tt * 8:(tt + 1) * 8], in_=t2[:, tt * NE:(tt + 1) * NE]))
            v.wait_ge(C.sem, C.n)
            step(v.tensor_copy(out=svi[:], in_=sv8a[:]))
            step(v.tensor_scalar(out=svb[:], in0=svi[:], scalar1=1, scalar2=None, op0=ALU.bitwise_and))
            step(v.tensor_scalar(out=svi[:], in0=svi[:], scalar1=1, scalar2=None, op0=ALU.arith_shift_right))
            step(v.tensor_copy(out=sslot[:], in_=svi[:]))
            step(v.tensor_copy(out=validf[:], in_=svb[:]))
            step(v.tensor_scalar(out=t1[:, 0:TT * 8], in0=su8a[:], scalar1=0.125, scalar2=None, op0=ALU.mult))
            step(v.tensor_copy(out=svi[:], in_=t1[:, 0:TT * 8]))
            step(v.tensor_copy(out=t1[:, 0:TT * 8], in_=svi[:]))
            step(v.tensor_scalar(out=t1[:, 0:TT * 8], in0=t1[:, 0:TT * 8], scalar1=-8.0, scalar2=None, op0=ALU.mult))
            step(v.tensor_add(out=t1[:, 0:TT * 8], in0=t1[:, 0:TT * 8], in1=su8a[:]))
            step(v.tensor_scalar(out=t1[:, 0:TT * 8], in0=t1[:, 0:TT * 8], scalar1=0.25, scalar2=None, op0=ALU.mult))
            step(v.tensor_scalar(out=t3[:, 0:TT * 8], in0=t1[:, 0:TT * 8], scalar1=0.0, scalar2=None, op0=ALU.is_lt))
            step(v.tensor_scalar(out=t3[:, 0:TT * 8], in0=t3[:, 0:TT * 8], scalar1=2.0, scalar2=None, op0=ALU.mult))
            step(v.tensor_add(out=t1[:, 0:TT * 8], in0=t1[:, 0:TT * 8], in1=t3[:, 0:TT * 8]))
            step(v.tensor_mul(out=sw8[:], in0=t1[:, 0:TT * 8], in1=validf[:]))
            step(v.tensor_scalar(out=t3[:, 0:TT * 8], in0=validf[:], scalar1=-1.0, scalar2=None, op0=ALU.mult))
            step(v.tensor_scalar(out=t3[:, 0:TT * 8], in0=t3[:, 0:TT * 8], scalar1=1.0, scalar2=None, op0=ALU.add))
            step(v.tensor_mul(out=t3[:, 0:TT * 8], in0=t3[:, 0:TT * 8], in1=strash[:]))
            step(v.tensor_mul(out=t2[:, 0:TT * 8], in0=sslot[:], in1=validf[:]))
            step(v.tensor_add(out=t2[:, 0:TT * 8], in0=t2[:, 0:TT * 8], in1=t3[:, 0:TT * 8]))
            step(v.tensor_copy(out=sidx[:], in_=t2[:, 0:TT * 8]))
        block.vector(chain)

        dnow = ch["dve"].n
        def scat(g):
            g.wait_ge(ch["dve"].sem, dnow)
            g.wait_ge(ch["gsc"].sem, 16)
            for tt in range(TT):
                for k in range(8):
                    ch["gsc"].dma(g.indirect_dma_start(
                        out=o_map, out_offset=bass.IndirectOffsetOnAxis(sidx[:, tt * 8 + k: tt * 8 + k + 1], 0),
                        in_=stok[:, tt:tt + 1], in_offset=None))
            g.wait_ge(ch["gsc"].sem, ch["gsc"].n)
        block.gpsimd(scat)

        def outs(s):
            s.wait_ge(ch["dve"].sem, dnow)
            ch["sst"].dma(s.dma_start(out=o_s8, in_=sslot[:]))
            ch["sst"].dma(s.dma_start(out=o_w8, in_=sw8[:]))
            s.wait_ge(ch["sst"].sem, ch["sst"].n)
        block.sync(outs)
    return nc


def build_l3():
    nc = bass.Bass("TRN2", target_bir_lowering=False)
    def din(name, shape, dt=F32):
        return nc.dram_tensor(name, list(shape), dt, kind="ExternalInput").ap()
    maps = din("maps", (8, MAPN))
    ln2r = din("ln2r", (NC * T, D), BF16)
    myrows = din("myrows", (32, 1), I32)
    wg = din("wg", (NEL, D, FF))
    wu = din("wu", (NEL, D, FF))
    wd = din("wd", (NEL, FF, D))
    IDENT = din("IDENT", (128, 128))
    o_y = nc.dram_tensor("y", [NEL * CAP, D], BF16, kind="ExternalOutput").ap()
    mapd = nc.dram_tensor("mapd", [MAPN, 1], F32)

    with ExitStack() as ctx:
        E = ctx.enter_context
        ch = {k: Ch(ctx, nc, k) for k in ("sld", "gw", "gw2", "gg", "gg2", "pe", "dve", "act", "sst")}
        gwch = [ch["gw"], ch["gw2"]]
        ggx = [ch["gg"], ch["gg2"]]
        smaps = E(nc.sbuf_tensor("smaps", [128, 8 * 257], F32))
        identb = E(nc.sbuf_tensor("identb", [128, 128], BF16))
        identF = E(nc.sbuf_tensor("identF", [128, 128], F32))
        trtags = []
        trtags2 = []
        wfree = []
        smyr = E(nc.sbuf_tensor("smyr", [32, 1], I32))
        mext = E(nc.sbuf_tensor("mext", [32, 128], F32))
        mtr = E(nc.sbuf_tensor("mtr", [128, 32], F32))
        xidx = E(nc.sbuf_tensor("xidx", [128, 32], I32))
        xg = [E(nc.sbuf_tensor(f"xg{i}", [128, 1024], BF16)) for i in range(2)]
        xbT = E(nc.sbuf_tensor("xbT", [128, 8 * (NEL * CAP)], BF16))
        hbuf = E(nc.sbuf_tensor("hbuf", [128, 6 * CAP], BF16))
        sqs = E(nc.sbuf_tensor("sqs", [128, 512], BF16))
        yev = E(nc.sbuf_tensor("yev", [128, D], BF16))
        wgB = [E(nc.sbuf_tensor(f"wgB{i}", [128, 8 * FF], BF16)) for i in range(2)]
        wuB = [E(nc.sbuf_tensor(f"wuB{i}", [128, 8 * FF], BF16)) for i in range(2)]
        wdB = [E(nc.sbuf_tensor(f"wdB{i}", [128, 6 * D], BF16)) for i in range(2)]
        PS1 = E(nc.psum_tensor("PS1", [128, 1024], F32))
        PS2 = E(nc.psum_tensor("PS2", [128, 1024], F32))
        PSB = [E(nc.psum_tensor("PSBa", [128, 128], BF16)),
               E(nc.psum_tensor("PSBb", [128, 128], BF16))]
        block = E(nc.Block())

        def lds(s):
            ch["sld"].dma(s.dma_start(out=smaps[:].rearrange("p (c b) -> p c b", c=8),
                                      in_=maps.rearrange("c (p b) -> p c b", p=128)))
            ch["sld"].dma(s.dma_start(out=smyr[:], in_=myrows))
        block.sync(lds)
        sld0 = ch["sld"].n
        def wload(g, e, par):
            for dst, srcp, nch in ((wgB[par], wg, 8), (wuB[par], wu, 8), (wdB[par], wd, 6)):
                gwch[par].dma(g.dma_start(
                    out=dst[:].rearrange("p (c w) -> p c w", c=nch),
                    in_=srcp[e].rearrange("(c p) w -> p c w", p=128)))
        def ldg0(g):
            ch["gg"].dma(g.dma_start(out=identb[:], in_=IDENT))
            ch["gg"].dma(g.dma_start(out=identF[:], in_=IDENT))
            wload(g, 0, 0)
        block.gpsimd(ldg0)
        gw_id = ch["gg"].n

        def msum(v):
            v.wait_ge(ch["sld"].sem, sld0)
            for c in range(1, 8):
                ch["dve"].tag(v.tensor_add(out=smaps[:, 0:257], in0=smaps[:, 0:257],
                                           in1=smaps[:, c * 257:(c + 1) * 257]))
                v.wait_ge(ch["dve"].sem, ch["dve"].n)
        block.vector(msum)

        def mst(g, now=ch["dve"].n):
            g.wait_ge(ch["dve"].sem, now)
            ch["gg"].dma(g.dma_start(out=mapd.ap().rearrange("(p b) v -> p (b v)", p=128),
                                     in_=smaps[:, 0:257]))
            g.wait_ge(ch["gg"].sem, ch["gg"].n)
            ch["gg"].dma(g.indirect_dma_start(
                out=mext[:], out_offset=None,
                in_=mapd.ap()[0:32768, :].rearrange("(r q) v -> r (q v)", q=128),
                in_offset=bass.IndirectOffsetOnAxis(smyr[:, 0:1], 0)))
            g.wait_ge(ch["gg"].sem, ch["gg"].n)
        block.gpsimd(mst)
        gg_ext = ch["gg"].n

        def mtp(t):
            t.wait_ge(ch["gg"].sem, gg_ext)
            i = t.transpose(PS1[:, 0:32], mext[:], identF[0:32, 0:32])
            ch["pe"].tag(i)
        block.tensor(mtp)
        def mte(v):
            v.wait_ge(ch["pe"].sem, ch["pe"].n)
            ch["dve"].tag(v.tensor_copy(out=mtr[:], in_=PS1[:, 0:32]))
            v.wait_ge(ch["dve"].sem, ch["dve"].n)
            ch["dve"].tag(v.tensor_scalar(out=mtr[:], in0=mtr[:], scalar1=1.0, scalar2=None, op0=ALU.max))
            v.wait_ge(ch["dve"].sem, ch["dve"].n)
            ch["dve"].tag(v.tensor_scalar(out=mtr[:], in0=mtr[:], scalar1=1.0, scalar2=None, op0=ALU.subtract))
            v.wait_ge(ch["dve"].sem, ch["dve"].n)
            ch["dve"].tag(v.tensor_copy(out=xidx[:], in_=mtr[:]))
        block.vector(mte)
        dve_idx = ch["dve"].n

        for j in range(32):
            jp = j % 2
            def gat(g, j=j, jp=jp, dve_idx=dve_idx):
                g.wait_ge(ggx[jp].sem, ggx[jp].n)
                if j == 0:
                    g.wait_ge(ch["dve"].sem, dve_idx)
                    g.wait_ge(ch["gg"].sem, ch["gg"].n)
                elif j >= 2:
                    g.wait_ge(ch["pe"].sem, trtags[j - 2])
                ggx[jp].dma(g.indirect_dma_start(
                    out=xg[jp][:], out_offset=None, in_=ln2r,
                    in_offset=bass.IndirectOffsetOnAxis(xidx[:, j:j + 1], 0)))
            block.gpsimd(gat)
            gg_now = ggx[jp].n
            for dc in range(8):
                gi = j * 8 + dc
                def trp(t, j=j, jp=jp, dc=dc, gg_now=gg_now, gi=gi):
                    t.wait_ge(ggx[jp].sem, gg_now)
                    if gi >= 2:
                        t.wait_ge(ch["dve"].sem, trtags2[gi - 2])
                    i = t.transpose(PSB[gi % 2][:, 0:128],
                                    xg[jp][:, dc * 128:(dc + 1) * 128], identb[:])
                    ch["pe"].tag(i)
                block.tensor(trp)
                pe_now = ch["pe"].n
                def tre(v, j=j, dc=dc, pe_now=pe_now, gi=gi):
                    v.wait_ge(ch["pe"].sem, pe_now)
                    ch["dve"].tag(v.tensor_copy(
                        out=xbT[:, dc * (NEL * CAP) + j * 128: dc * (NEL * CAP) + j * 128 + 128],
                        in_=PSB[gi % 2][:, 0:128]))
                block.vector(tre)
                trtags2.append(ch["dve"].n)
            trtags.append(ch["pe"].n)

        gwtags = [ch["gw"].n]

        ysttags = []
        for e in range(NEL):
            par = e % 2
            if e + 1 < NEL:
                def wln(g, e=e, par=par):
                    if e >= 1:
                        g.wait_ge(ch["pe"].sem, wfree[e - 1])
                    wload(g, e + 1, 1 - par)
                block.gpsimd(wln)
                gwtags.append(gwch[(e + 1) % 2].n)
            base = e * CAP
            for fc in range(6):
                def gmm(t, e=e, par=par, fc=fc):
                    if fc == 0:
                        t.wait_ge(gwch[par].sem, gwtags[e])
                        wait_all(t, [ch["dve"]])
                    for ic in range(8):
                        t.matmul(PS1[:, 0:512],
                                 wgB[par][:, ic * FF + fc * 128: ic * FF + fc * 128 + 128],
                                 xbT[:, ic * (NEL * CAP) + base: ic * (NEL * CAP) + base + 512],
                                 start=(ic == 0), stop=(ic == 7))
                    for ic in range(8):
                        i = t.matmul(PS2[:, 0:512],
                                     wuB[par][:, ic * FF + fc * 128: ic * FF + fc * 128 + 128],
                                     xbT[:, ic * (NEL * CAP) + base: ic * (NEL * CAP) + base + 512],
                                     start=(ic == 0), stop=(ic == 7))
                    ch["pe"].tag(i)
                block.tensor(gmm)
                pe_now = ch["pe"].n
                def gsl(s, pe_now=pe_now):
                    s.wait_ge(ch["pe"].sem, pe_now)
                    ch["act"].tag(s.activation(out=sqs[:], in_=PS1[:, 0:512], func=AF.Silu))
                block.scalar(gsl)
                def gmu(v, fc=fc, now=ch["act"].n):
                    v.wait_ge(ch["act"].sem, now)
                    ch["dve"].tag(v.tensor_mul(out=hbuf[:, fc * CAP: fc * CAP + 512],
                                               in0=PS2[:, 0:512], in1=sqs[:]))
                block.vector(gmu)
                def ggate(t, nd=ch["dve"].n):
                    t.wait_ge(ch["dve"].sem, nd)
                block.tensor(ggate)
            for sc in range(4):
                for dp in range(2):
                    def dmm(t, e=e, par=par, sc=sc, dp=dp):
                        for ic in range(6):
                            i = t.matmul(PS1[0:128, dp * 512:dp * 512 + 512],
                                         hbuf[:, ic * CAP + sc * 128: ic * CAP + sc * 128 + 128],
                                         wdB[par][:, ic * D + dp * 512: ic * D + dp * 512 + 512],
                                         start=(ic == 0), stop=(ic == 5))
                        ch["pe"].tag(i)
                    block.tensor(dmm)
                pe_now = ch["pe"].n
                def yevf(v, pe_now=pe_now, snow=len(ysttags)):
                    v.wait_ge(ch["pe"].sem, pe_now)
                    if snow >= 1:
                        v.wait_ge(ch["sst"].sem, ysttags[snow - 1])
                    ch["dve"].tag(v.tensor_copy(out=yev[:], in_=PS1[:, 0:1024]))
                block.vector(yevf)
                def yst(s, e=e, sc=sc, now=ch["dve"].n):
                    s.wait_ge(ch["dve"].sem, now)
                    ch["sst"].dma(s.dma_start(out=o_y[e * CAP + sc * 128: e * CAP + sc * 128 + 128, :],
                                              in_=yev[:]))
                block.sync(yst)
                ysttags.append(ch["sst"].n)
                def dgate(t, nd=ch["dve"].n):
                    t.wait_ge(ch["dve"].sem, nd)
                block.tensor(dgate)
            wfree.append(ch["pe"].n)

        def fin(s):
            s.wait_ge(ch["sst"].sem, ch["sst"].n)
        block.sync(fin)
    return nc


def build_l4():
    nc = bass.Bass("TRN2", target_bir_lowering=False)
    def din(name, shape, dt=F32):
        return nc.dram_tensor(name, list(shape), dt, kind="ExternalInput").ap()
    yfull = din("yfull", (NE * CAP, D), BF16)
    slot8 = din("slot8", (128, TT * 8))
    w8v = din("w8v", (128, TT * 8))
    xsht = din("xsht", (T, D))
    o_out = nc.dram_tensor("out", [T, D], F32, kind="ExternalOutput").ap()

    with ExitStack() as ctx:
        E = ctx.enter_context
        ch = {k: Ch(ctx, nc, k) for k in ("sld", "sx", "gg", "gg2", "pe", "dve", "act", "sst")}
        ggch = [ch["gg"], ch["gg2"]]
        ss8 = E(nc.sbuf_tensor("ss8", [128, TT * 8], F32))
        sw8 = E(nc.sbuf_tensor("sw8", [128, TT * 8], F32))
        sidx = E(nc.sbuf_tensor("sidx", [128, TT * 8], I32))
        yg = [[E(nc.sbuf_tensor(f"yg{i}_{k}", [128, D], BF16)) for k in range(8)]
              for i in range(2)]
        sxsh = [E(nc.sbuf_tensor(f"sxsh{i}", [128, D], F32)) for i in range(2)]
        acc = E(nc.sbuf_tensor("acc", [128, D], F32))
        oval = [E(nc.sbuf_tensor(f"oval{i}", [128, D], F32)) for i in range(2)]
        block = E(nc.Block())

        def lds(s):
            ch["sld"].dma(s.dma_start(out=ss8[:], in_=slot8))
            ch["sld"].dma(s.dma_start(out=sw8[:], in_=w8v))
        block.sync(lds)
        sld0 = ch["sld"].n
        def cvt(v):
            v.wait_ge(ch["sld"].sem, sld0)
            ch["dve"].tag(v.tensor_copy(out=sidx[:], in_=ss8[:]))
        block.vector(cvt)
        dve_idx = ch["dve"].n

        gtags = []
        xtags = []
        dtags = []
        stags = []
        for tt in range(TT):
            par = tt % 2
            def gat(g, tt=tt, par=par):
                if tt == 0:
                    g.wait_ge(ch["dve"].sem, dve_idx)
                elif tt >= 2:
                    g.wait_ge(ch["dve"].sem, dtags[tt - 2][0])
                for k in range(8):
                    ggch[par].dma(g.indirect_dma_start(
                        out=yg[par][k][:], out_offset=None, in_=yfull,
                        in_offset=bass.IndirectOffsetOnAxis(sidx[:, tt * 8 + k: tt * 8 + k + 1], 0)))
            block.gpsimd(gat)
            gtags.append(ggch[par].n)
            def xld(s, tt=tt, par=par):
                if tt >= 2:
                    s.wait_ge(ch["dve"].sem, dtags[tt - 2][1])
                ch["sx"].dma(s.dma_start(out=sxsh[par][:], in_=xsht[tt * 128:(tt + 1) * 128, :]))
            block.sync(xld)
            xtags.append(ch["sx"].n)
            def comb(v, tt=tt, par=par, gnow=gtags[tt], xnow=xtags[tt], dprev=ch["dve"].n):
                v.wait_ge(ggch[par].sem, gnow)
                v.wait_ge(ch["dve"].sem, dprev)
                ch["dve"].tag(v.tensor_scalar(out=acc[:], in0=yg[par][0][:],
                                              scalar1=sw8[:, tt * 8: tt * 8 + 1],
                                              scalar2=None, op0=ALU.mult))
                for k in range(1, 8):
                    v.wait_ge(ch["dve"].sem, ch["dve"].n)
                    ch["dve"].tag(v.scalar_tensor_tensor(out=acc[:], in0=yg[par][k][:],
                                                         scalar=sw8[:, tt * 8 + k: tt * 8 + k + 1],
                                                         in1=acc[:], op0=ALU.mult, op1=ALU.add))
                v.wait_ge(ch["dve"].sem, ch["dve"].n)
                v.wait_ge(ch["sx"].sem, xnow)
                if tt >= 1:
                    v.wait_ge(ch["sst"].sem, stags[tt - 1])
                yg_free = ch["dve"].tag(v.tensor_add(out=oval[par][:], in0=acc[:], in1=sxsh[par][:]))
                dtags.append((yg_free, yg_free))
            block.vector(comb)
            def ost(s, tt=tt, par=par, now=ch["dve"].n):
                s.wait_ge(ch["dve"].sem, now)
                ch["sst"].dma(s.dma_start(out=o_out[tt * 128:(tt + 1) * 128, :], in_=oval[par][:]))
            block.sync(ost)
            stags.append(ch["sst"].n)

        def fin(s):
            s.wait_ge(ch["sst"].sem, ch["sst"].n)
        block.sync(fin)
    return nc


def l2_inputs(r1c, srowA, c, consts):
    cmask = np.zeros((8, 1), np.float32); cmask[:c, 0] = 1.0
    tokid1 = (c * T + 1 + np.arange(128, dtype=np.float32)[:, None]
              + 128.0 * np.arange(TT, dtype=np.float32)[None, :])
    e1024 = np.tile((np.arange(NE, dtype=np.float32) * 1024.0)[None, :], (128, TT))
    e8 = np.tile((np.arange(NE, dtype=np.float32) * 8.0)[None, :], (128, TT))
    trashw = np.tile((32768.0 + np.arange(TT * 8, dtype=np.float32))[None, :], (128, 1))
    return {"cum": r1c["cum"], "wnorm": r1c["wnorm"], "srowA": srowA,
            "cmask": cmask, "tokid1": np.ascontiguousarray(tokid1),
            "E1024W": np.ascontiguousarray(e1024), "E8W": np.ascontiguousarray(e8),
            "TRASHW": np.ascontiguousarray(trashw), "ONES": consts["ONES"]}


def kernel(**inputs):
    x = inputs["L0_attn_x"]
    consts = host_consts()
    consts["E1024"] = np.tile((np.arange(NE, dtype=np.float32) * 1024.0)[None, :], (128, 1))
    consts["E8"] = np.tile((np.arange(NE, dtype=np.float32) * 8.0)[None, :], (128, 1))
    cores = list(range(NC))

    nc1 = build_l1()
    in1 = [l1_inputs(np.ascontiguousarray(x[c]), inputs, consts) for c in cores]
    r1 = run_bass_kernel_spmd(nc1, in1, core_ids=cores).results

    srowA = np.ascontiguousarray(np.stack([r1[c]["srow"][0] for c in cores]))
    nc2 = build_l2()
    in2 = [l2_inputs(r1[c], srowA, c, consts) for c in cores]
    r2 = run_bass_kernel_spmd(nc2, in2, core_ids=cores).results

    maps = np.ascontiguousarray(np.stack([r2[c]["mapc"][:, 0] for c in cores]))
    ln2r = np.ascontiguousarray(
        np.concatenate([r1[c]["ln2b"].T for c in cores], axis=0))
    nc3 = build_l3()
    in3 = []
    for c in cores:
        in3.append({
            "maps": maps, "ln2r": ln2r,
            "myrows": np.arange(c * 32, c * 32 + 32, dtype=np.int32).reshape(32, 1),
            "wg": np.ascontiguousarray(inputs["L0_ffn_w_gate"][c * NEL:(c + 1) * NEL]),
            "wu": np.ascontiguousarray(inputs["L0_ffn_w_up"][c * NEL:(c + 1) * NEL]),
            "wd": np.ascontiguousarray(inputs["L0_ffn_w_down"][c * NEL:(c + 1) * NEL]),
            "IDENT": consts["IDENT"],
        })
    r3 = run_bass_kernel_spmd(nc3, in3, core_ids=cores).results

    yfull = np.ascontiguousarray(np.concatenate([r3[c]["y"] for c in cores], axis=0))
    nc4 = build_l4()
    in4 = []
    for c in cores:
        in4.append({
            "yfull": yfull, "slot8": r2[c]["slot8"], "w8v": r2[c]["w8v"],
            "xsht": np.ascontiguousarray(r1[c]["xsh"].T),
        })
    r4 = run_bass_kernel_spmd(nc4, in4, core_ids=cores).results
    return np.stack([r4[c]["out"] for c in cores]).astype(np.float32)


# version 06
# speedup vs baseline: 1.1561x; 1.0933x over previous
import numpy as np
import ml_dtypes
from contextlib import ExitStack

import concourse.bass as bass
import concourse.mybir as mybir
from concourse.bass_utils import run_bass_kernel_spmd

F32 = mybir.dt.float32
BF16 = mybir.dt.bfloat16
I32 = mybir.dt.int32
AF = mybir.ActivationFunctionType
ALU = mybir.AluOpType
AX = mybir.AxisListType

T = 2048
D = 1024
NH = 16
DH = 64
DR = 32
KVR = 256
QR = 512
NE = 64
NEL = 8
CAP = 512
FF = 768
SF = 768
NC = 8
TT = T // 128
NEG = -1.0e30
SCALE = 1.0 / float(np.sqrt(DH + DR))


class Ch:
    def __init__(self, ctx, nc, name):
        self.sem = ctx.enter_context(nc.semaphore(name))
        self.n = 0

    def tag(self, inst, k=1):
        inst.then_inc(self.sem, k)
        self.n += k
        return self.n

    def dma(self, inst):
        return self.tag(inst, 16)


def wait_all(eng, chans):
    for c in chans:
        if c.n > 0:
            eng.wait_ge(c.sem, c.n)



def rope_tables():
    inv = 1.0 / 10000.0 ** (np.arange(0, DR, 2, dtype=np.float32) / DR)
    t = np.arange(T, dtype=np.float32)
    emb = np.concatenate([np.outer(t, inv)] * 2, -1)
    cosf = np.ascontiguousarray(np.cos(emb).astype(np.float32).T)
    sinf = np.ascontiguousarray(np.sin(emb).astype(np.float32).T)
    sinf_signed = sinf.copy()
    sinf_signed[0:16, :] *= -1.0
    return (np.ascontiguousarray(np.tile(cosf, (4, 1))),
            np.ascontiguousarray(np.tile(sinf_signed, (4, 1))))


def rope_shuffle_cols(w):
    perm = np.arange(w.shape[1]).reshape(-1, 2, 16)[:, ::-1, :].reshape(-1)
    return np.ascontiguousarray(w[:, perm])


def host_consts():
    c = {}
    c["IDENT"] = np.eye(128, dtype=np.float32)
    c["TRICUM"] = np.triu(np.ones((128, 128), np.float32))
    c["ONES"] = np.ones((128, 128), np.float32)
    kk, qq = np.meshgrid(np.arange(128), np.arange(128), indexing="ij")
    c["MASKTRI"] = np.where(kk > qq, np.float32(NEG), np.float32(0.0)).astype(np.float32)
    c["COS"], c["SIN"] = rope_tables()
    return c



def fm_pair(dst, src, width):
    n = src.shape[0] // 128
    return (dst.rearrange("p (c w) -> p c w", w=width),
            src.rearrange("(c p) w -> p c w", p=128))


def build_l1():
    nc = bass.Bass("TRN2", target_bir_lowering=False)

    def din(name, shape, dt=F32):
        return nc.dram_tensor(name, list(shape), dt, kind="ExternalInput").ap()

    xT = din("xT", (D, T))
    wkv = din("wkv", (D, KVR))
    gkv = din("gkv", (128, 2))
    wkdec = din("wkdec", (KVR, D))
    wvdec = din("wvdec", (KVR, D))
    wq = din("wq", (D, QR))
    wqdec = din("wqdec", (QR, D))
    wkrope = din("wkrope", (D, QR))
    wkropeS = din("wkropeS", (D, QR))
    wqrope = din("wqrope", (QR, QR))
    wqropeS = din("wqropeS", (QR, QR))
    wo = din("wo", (D, D))
    routerT = din("routerT", (D, NE))
    wsgu = din("wsgu", (D, 2 * SF))
    wsdn = din("wsdn", (SF, D))
    IDENT = din("IDENT", (128, 128))
    TRICUM = din("TRICUM", (128, 128))
    ONESC = din("ONES", (128, 128))
    MASKTRI = din("MASKTRI", (128, 128))
    COS = din("COS", (128, T))
    SIN = din("SIN", (128, T))

    o_xsh = nc.dram_tensor("xsh", [D, T], F32, kind="ExternalOutput").ap()
    o_ln2 = nc.dram_tensor("ln2b", [D, T], BF16, kind="ExternalOutput").ap()
    o_cum = nc.dram_tensor("cum", [128, TT * NE], F32, kind="ExternalOutput").ap()
    o_wn = nc.dram_tensor("wnorm", [128, TT * NE], F32, kind="ExternalOutput").ap()
    o_srow = nc.dram_tensor("srow", [1, NE], F32, kind="ExternalOutput").ap()

    x1d = nc.dram_tensor("x1d", [D, T], F32)

    ctx = ExitStack()
    with ctx:
        E = ctx.enter_context
        ch = {k: Ch(ctx, nc, k) for k in ("sld", "sr", "gc0", "gw3", "gw4", "gwo", "gsh",
                                          "gx1", "pe", "dve", "act", "sst")}

        lnb = E(nc.sbuf_tensor("lnb", [128, 8 * T], BF16))
        rows = E(nc.sbuf_tensor("rows", [128, T], F32))
        rowsb = E(nc.sbuf_tensor("rowsb", [128, T], BF16))
        bcast = E(nc.sbuf_tensor("bcast", [128, T], BF16))
        sq = E(nc.sbuf_tensor("sq", [128, T], BF16))
        identb = E(nc.sbuf_tensor("identb", [128, 128], F32))
        trib = E(nc.sbuf_tensor("trib", [128, 128], BF16))
        onesb = E(nc.sbuf_tensor("onesb", [128, 128], BF16))
        maskt = E(nc.sbuf_tensor("maskt", [128, 128], F32))
        gkvb = E(nc.sbuf_tensor("gkvb", [128, 2], F32))
        onesF = E(nc.sbuf_tensor("onesF", [128, 128], F32))
        PS1 = E(nc.psum_tensor("PS1", [128, T], F32))
        PS2 = E(nc.psum_tensor("PS2", [128, T], F32))
        block = E(nc.Block())


        st1 = ExitStack()
        xbf = st1.enter_context(nc.sbuf_tensor("xbf", [128, 8 * T], BF16))

        def p0g(g):
            ch["gc0"].dma(g.dma_start(out=xbf[:].rearrange("p (c w) -> p c w", c=8), in_=xT.rearrange("(c p) w -> p c w", p=128)))
            ch["gc0"].dma(g.dma_start(out=trib[:], in_=TRICUM))
            ch["gc0"].dma(g.dma_start(out=onesb[:], in_=ONESC))
            ch["gc0"].dma(g.dma_start(out=identb[:], in_=IDENT))
            ch["gc0"].dma(g.dma_start(out=maskt[:], in_=MASKTRI))
            ch["gc0"].dma(g.dma_start(out=gkvb[:], in_=gkv))
            ch["gc0"].dma(g.dma_start(out=onesF[:], in_=ONESC))
        block.gpsimd(p0g)
        gld_x = ch["gc0"].n
        sld_p0 = 0

        def emit_ln(src_bf, eps):
            a0 = ch["act"].n
            dve_at = ch["dve"].n
            def summ(t):
                t.wait_ge(ch["gc0"].sem, gld_x)
                if dve_at:
                    t.wait_ge(ch["dve"].sem, dve_at)
                for dc in range(8):
                    for ps in range(4):
                        i = t.matmul(PS1[0:1, ps * 512:ps * 512 + 512], onesb[:, 0:1],
                                     src_bf[:, dc * T + ps * 512: dc * T + ps * 512 + 512],
                                     start=(dc == 0), stop=(dc == 7))
                ch["pe"].tag(i)
            block.tensor(summ)
            pe_cons = []
            for dc in range(8):
                def sqf(s, dc=dc):
                    if dc == 0:
                        s.wait_ge(ch["gc0"].sem, gld_x)
                        if dve_at:
                            s.wait_ge(ch["dve"].sem, dve_at)
                    else:
                        s.wait_ge(ch["pe"].sem, pe_cons[dc - 1])
                    ch["act"].tag(s.activation(out=sq[:], in_=src_bf[:, dc * T:(dc + 1) * T],
                                               func=AF.Square))
                block.scalar(sqf)
                def mmf(t, dc=dc, anow=ch["act"].n):
                    t.wait_ge(ch["act"].sem, anow)
                    for ps in range(4):
                        i = t.matmul(PS2[0:1, ps * 512:ps * 512 + 512], onesb[:, 0:1],
                                     sq[:, ps * 512:ps * 512 + 512],
                                     start=(dc == 0), stop=(dc == 7))
                    ch["pe"].tag(i)
                block.tensor(mmf)
                pe_cons.append(ch["pe"].n)

            def rowmath(v):
                v.wait_ge(ch["pe"].sem, ch["pe"].n)
                ch["dve"].tag(v.tensor_scalar(out=rows[0:1, :], in0=PS1[0:1, 0:T],
                                              scalar1=1.0 / D, scalar2=None, op0=ALU.mult))
                ch["dve"].tag(v.tensor_scalar(out=rows[32:33, :], in0=PS1[0:1, 0:T],
                                              scalar1=-1.0 / D, scalar2=None, op0=ALU.mult))
                ch["dve"].tag(v.tensor_scalar(out=rows[64:65, :], in0=PS2[0:1, 0:T],
                                              scalar1=1.0 / D, scalar2=None, op0=ALU.mult))
                v.wait_ge(ch["dve"].sem, ch["dve"].n)
                ch["dve"].tag(v.tensor_mul(out=PS1[0:1, 0:T], in0=rows[0:1, :], in1=rows[0:1, :]))
                v.wait_ge(ch["dve"].sem, ch["dve"].n)
                ch["dve"].tag(v.tensor_sub(out=rows[64:65, :], in0=rows[64:65, :], in1=PS1[0:1, 0:T]))
                v.wait_ge(ch["dve"].sem, ch["dve"].n)
                ch["dve"].tag(v.tensor_scalar(out=rows[64:65, :], in0=rows[64:65, :],
                                              scalar1=eps, scalar2=None, op0=ALU.add))
            block.vector(rowmath)
            emit_rsqrt_row(64)

        def emit_rsqrt_row(prow):
            def f(s):
                s.wait_ge(ch["dve"].sem, ch["dve"].n)
                ch["act"].tag(s.activation(out=rows[prow:prow + 1, :],
                                           in_=rows[prow:prow + 1, :], func=AF.Ln))
                s.wait_ge(ch["act"].sem, ch["act"].n)
                ch["act"].tag(s.activation(out=rows[prow:prow + 1, :],
                                           in_=rows[prow:prow + 1, :], func=AF.Exp,
                                           scale=-0.5))
            block.scalar(f)

        def emit_bcast(prow, nrows=128):
            def cp(v):
                v.wait_ge(ch["dve"].sem, ch["dve"].n)
                wait_all(v, [ch["act"]])
                ch["dve"].tag(v.tensor_copy(out=rowsb[prow:prow + 1, :],
                                            in_=rows[prow:prow + 1, :]))
            block.vector(cp)
            def mm(t, now=ch["dve"].n):
                t.wait_ge(ch["gc0"].sem, gld_x)
                t.wait_ge(ch["dve"].sem, now)
                for ps in range(4):
                    i = t.matmul(PS2[0:nrows, ps * 512:ps * 512 + 512],
                                 onesb[prow:prow + 1, 0:nrows],
                                 rowsb[prow:prow + 1, ps * 512:ps * 512 + 512],
                                 start=True, stop=True)
                ch["pe"].tag(i)
            block.tensor(mm)
            def ev(v):
                v.wait_ge(ch["pe"].sem, ch["pe"].n)
                ch["dve"].tag(v.tensor_copy(out=bcast[0:nrows, :], in_=PS2[0:nrows, 0:T]))
            block.vector(ev)

        def emit_ln_apply(src_bf, dst_bf):
            emit_bcast(32)
            def add(v):
                v.wait_ge(ch["dve"].sem, ch["dve"].n)
                for dc in range(8):
                    i = v.tensor_add(out=dst_bf[:, dc * T:(dc + 1) * T],
                                     in0=src_bf[:, dc * T:(dc + 1) * T], in1=bcast[:])
                ch["dve"].tag(i)
            block.vector(add)
            emit_bcast(64)
            def mul(v):
                v.wait_ge(ch["dve"].sem, ch["dve"].n)
                for dc in range(8):
                    i = v.tensor_mul(out=dst_bf[:, dc * T:(dc + 1) * T],
                                     in0=dst_bf[:, dc * T:(dc + 1) * T], in1=bcast[:])
                ch["dve"].tag(i)
            block.vector(mul)

        emit_ln(xbf, 1e-5)
        emit_ln_apply(xbf, lnb)
        st1.close()

        st2 = ExitStack()
        E2 = st2.enter_context
        qc = E2(nc.sbuf_tensor("qc", [128, 4 * T], BF16))
        kvn = E2(nc.sbuf_tensor("kvn", [128, 2 * T], BF16))
        krp = E2(nc.sbuf_tensor("krp", [128, 4 * T], BF16))
        qrp = E2(nc.sbuf_tensor("qrp", [128, 4 * T], BF16))
        vaug = E2(nc.sbuf_tensor("vaug", [128, TT * NH * 65], BF16))

        st2t = ExitStack()
        E2t = st2t.enter_context
        wkvB = E2t(nc.sbuf_tensor("wkvB", [128, 8 * KVR], BF16))
        wqB = E2t(nc.sbuf_tensor("wqB", [128, 8 * QR], BF16))
        wkrB = E2t(nc.sbuf_tensor("wkrB", [128, 8 * QR], BF16))
        wkrSB = E2t(nc.sbuf_tensor("wkrSB", [128, 8 * QR], BF16))
        wqrB = E2t(nc.sbuf_tensor("wqrB", [128, 4 * QR], BF16))
        wqrSB = E2t(nc.sbuf_tensor("wqrSB", [128, 4 * QR], BF16))
        wvdB = E2t(nc.sbuf_tensor("wvdB", [128, 2 * D], BF16))
        kvc32 = E2t(nc.sbuf_tensor("kvc32", [128, 2 * T], F32))
        cosb = E2t(nc.sbuf_tensor("cosb", [128, T], BF16))
        sinb = E2t(nc.sbuf_tensor("sinb", [128, T], BF16))

        def p3g(g):
            def cast(dst, src, pat):
                ch["gw3"].dma(g.dma_start(out=dst.rearrange("p (c w) -> p c w", c=src.shape[0] // 128), in_=src.rearrange("(c p) w -> p c w", p=128)))
            ch["gw3"].dma(g.dma_start(out=cosb[:], in_=COS))
            ch["gw3"].dma(g.dma_start(out=sinb[:], in_=SIN))
            cast(wkvB[:], wkv, "(dc p) r -> p (dc r)")
            cast(wqB[:], wq, "(dc p) r -> p (dc r)")
            cast(wkrB[:], wkrope, "(dc p) r -> p (dc r)")
            cast(wkrSB[:], wkropeS, "(dc p) r -> p (dc r)")
            cast(wqrB[:], wqrope, "(qc p) r -> p (qc r)")
            cast(wqrSB[:], wqropeS, "(qc p) r -> p (qc r)")
            cast(wvdB[:], wvdec, "(kc p) d -> p (kc d)")
        block.gpsimd(p3g)
        gld_w = ch["gw3"].n

        def proj(dst_bf, wB, n_in, n_out, src_bf, dst_f32=None, wSB=None, wait_ld=None):
            for oc in range(n_out):
                for ps in range(4):
                    def mm(t, oc=oc, ps=ps):
                        if oc == 0 and ps == 0:
                            t.wait_ge(ch["gw3"].sem, wait_ld if wait_ld else ch["gw3"].n)
                            wait_all(t, [ch["dve"]])
                        for ic in range(n_in):
                            i = t.matmul(PS1[:, ps * 512:ps * 512 + 512],
                                         wB[:, ic * n_out * 128 + oc * 128: ic * n_out * 128 + oc * 128 + 128],
                                         src_bf[:, ic * T + ps * 512: ic * T + ps * 512 + 512],
                                         start=(ic == 0), stop=(ic == n_in - 1))
                        if wSB is not None:
                            for ic in range(n_in):
                                i = t.matmul(PS2[:, ps * 512:ps * 512 + 512],
                                             wSB[:, ic * n_out * 128 + oc * 128: ic * n_out * 128 + oc * 128 + 128],
                                             src_bf[:, ic * T + ps * 512: ic * T + ps * 512 + 512],
                                             start=(ic == 0), stop=(ic == n_in - 1))
                        ch["pe"].tag(i)
                    block.tensor(mm)
                    pe_now = ch["pe"].n

                    def ev(v, oc=oc, ps=ps, pe_now=pe_now):
                        v.wait_ge(ch["pe"].sem, pe_now)
                        dsl = slice(oc * T + ps * 512, oc * T + ps * 512 + 512)
                        ssl = slice(ps * 512, ps * 512 + 512)
                        if wSB is None:
                            if dst_f32 is not None:
                                ch["dve"].tag(v.tensor_copy(out=dst_f32[:, dsl], in_=PS1[:, ssl]))
                            ch["dve"].tag(v.tensor_copy(out=dst_bf[:, dsl], in_=PS1[:, ssl]))
                        else:
                            ch["dve"].tag(v.tensor_mul(out=sq[:, 0:512], in0=PS1[:, ssl], in1=cosb[:, ssl]))
                            ch["dve"].tag(v.tensor_mul(out=dst_bf[:, dsl], in0=PS2[:, ssl], in1=sinb[:, ssl]))
                            v.wait_ge(ch["dve"].sem, ch["dve"].n)
                            ch["dve"].tag(v.tensor_add(out=dst_bf[:, dsl], in0=dst_bf[:, dsl], in1=sq[:, 0:512]))
                    block.vector(ev)

                    def gate(t, now=ch["dve"].n):
                        t.wait_ge(ch["dve"].sem, now)
                    block.tensor(gate)

        proj(kvn, wkvB, 8, 2, lnb, dst_f32=kvc32, wait_ld=gld_w)
        kv_pe = []
        kv_dve = ch["dve"].n
        for c2 in range(2):
            def kvsq(s, c2=c2):
                if c2 == 0:
                    s.wait_ge(ch["dve"].sem, kv_dve)
                else:
                    s.wait_ge(ch["pe"].sem, kv_pe[c2 - 1])
                ch["act"].tag(s.activation(out=sq[:], in_=kvc32[:, c2 * T:(c2 + 1) * T],
                                           func=AF.Square))
            block.scalar(kvsq)
            def kvmm(t, c2=c2, anow=ch["act"].n):
                t.wait_ge(ch["act"].sem, anow)
                for ps in range(4):
                    i = t.matmul(PS1[0:1, ps * 512:ps * 512 + 512], onesb[:, 0:1],
                                 sq[:, ps * 512:ps * 512 + 512],
                                 start=(c2 == 0), stop=(c2 == 1))
                ch["pe"].tag(i)
            block.tensor(kvmm)
            kv_pe.append(ch["pe"].n)
        def kvrow(v):
            v.wait_ge(ch["pe"].sem, ch["pe"].n)
            ch["dve"].tag(v.tensor_scalar(out=rows[64:65, :], in0=PS1[0:1, 0:T],
                                          scalar1=1.0 / KVR, scalar2=None, op0=ALU.mult))
            v.wait_ge(ch["dve"].sem, ch["dve"].n)
            ch["dve"].tag(v.tensor_scalar(out=rows[64:65, :], in0=rows[64:65, :],
                                          scalar1=1e-6, scalar2=None, op0=ALU.add))
        block.vector(kvrow)
        emit_rsqrt_row(64)
        emit_bcast(64)
        def kvapply(v):
            v.wait_ge(ch["gc0"].sem, gld_x)
            v.wait_ge(ch["dve"].sem, ch["dve"].n)
            for c2 in range(2):
                ch["dve"].tag(v.tensor_mul(out=kvc32[:, c2 * T:(c2 + 1) * T],
                                           in0=kvc32[:, c2 * T:(c2 + 1) * T], in1=bcast[:]))
            v.wait_ge(ch["dve"].sem, ch["dve"].n)
            for c2 in range(2):
                ch["dve"].tag(v.tensor_scalar(out=kvn[:, c2 * T:(c2 + 1) * T],
                                              in0=kvc32[:, c2 * T:(c2 + 1) * T],
                                              scalar1=gkvb[:, c2:c2 + 1],
                                              scalar2=None, op0=ALU.mult))
        block.vector(kvapply)

        proj(qc, wqB, 8, 4, lnb)
        proj(krp, wkrB, 8, 4, lnb, wSB=wkrSB)
        proj(qrp, wqrB, 4, 4, qc, wSB=wqrSB)

        def vones(v):
            ch["dve"].tag(v.memset(vaug[:].rearrange("p (a b) -> p a b", b=65)[:, :, 64:65], 1.0))
        block.vector(vones)
        for kt in range(TT):
            def vmm(t, kt=kt):
                if kt == 0:
                    wait_all(t, [ch["dve"], ch["gw3"]])
                for half in range(2):
                    for c2 in range(2):
                        i = t.matmul(PS1[:, half * 512:half * 512 + 512],
                                     kvn[:, c2 * T + kt * 128: c2 * T + kt * 128 + 128],
                                     wvdB[:, c2 * D + half * 512: c2 * D + half * 512 + 512],
                                     start=(c2 == 0), stop=(c2 == 1))
                ch["pe"].tag(i)
            block.tensor(vmm)
            pe_now = ch["pe"].n
            def vev(v, kt=kt, pe_now=pe_now):
                v.wait_ge(ch["pe"].sem, pe_now)
                ch["dve"].tag(v.tensor_copy(
                    out=vaug[:, kt * NH * 65:(kt + 1) * NH * 65].rearrange("p (h c) -> p h c", c=65)[:, :, 0:64],
                    in_=PS1[:, 0:1024].rearrange("p (h c) -> p h c", c=64)))
            block.vector(vev)
            def vgate(t, now=ch["dve"].n):
                t.wait_ge(ch["dve"].sem, now)
            block.tensor(vgate)

        st2t.close()

        st3a = ExitStack()
        abuf = st3a.enter_context(nc.sbuf_tensor("abuf", [128, 8 * T], BF16))
        st3b = ExitStack()
        E3 = st3b.enter_context
        wkdB = E3(nc.sbuf_tensor("wkdB", [128, 2 * D], BF16))
        wqdB = E3(nc.sbuf_tensor("wqdB", [128, 4 * D], BF16))
        pt = E3(nc.sbuf_tensor("pt", [128, 2 * 1024], BF16))
        qh = E3(nc.sbuf_tensor("qh", [128, T], BF16))
        kh = E3(nc.sbuf_tensor("kh", [128, T], BF16))

        def p4g(g):
            wait_all(g, [ch["pe"], ch["dve"], ch["act"], ch["gw3"]])
            def cast(dst, src, pat):
                ch["gw4"].dma(g.dma_start(out=dst.rearrange("p (c w) -> p c w", c=src.shape[0] // 128), in_=src.rearrange("(c p) w -> p c w", p=128)))
            cast(wkdB[:], wkdec, "(kc p) d -> p (kc d)")
            cast(wqdB[:], wqdec, "(qc p) d -> p (qc d)")
        block.gpsimd(p4g)
        gld_hw = ch["gw4"].n

        for h in range(NH):
            for ps in range(4):
                def qmm(t, h=h, ps=ps):
                    if h == 0 and ps == 0:
                        t.wait_ge(ch["gw4"].sem, gld_hw)
                        wait_all(t, [ch["dve"]])
                    for ic in range(4):
                        i = t.matmul(PS2[0:64, ps * 512:ps * 512 + 512],
                                     wqdB[:, ic * D + h * 64: ic * D + h * 64 + 64],
                                     qc[:, ic * T + ps * 512: ic * T + ps * 512 + 512],
                                     start=(ic == 0), stop=(ic == 3))
                    for ic in range(2):
                        i = t.matmul(PS1[0:64, ps * 512:ps * 512 + 512],
                                     wkdB[:, ic * D + h * 64: ic * D + h * 64 + 64],
                                     kvn[:, ic * T + ps * 512: ic * T + ps * 512 + 512],
                                     start=(ic == 0), stop=(ic == 1))
                    ch["pe"].tag(i)
                block.tensor(qmm)
                pe_now = ch["pe"].n
                def qev(v, h=h, ps=ps, pe_now=pe_now):
                    v.wait_ge(ch["pe"].sem, pe_now)
                    ssl = slice(ps * 512, ps * 512 + 512)
                    ch["dve"].tag(v.tensor_copy(out=qh[0:64, ssl], in_=PS2[0:64, ssl]))
                    ch["dve"].tag(v.tensor_copy(out=kh[0:64, ssl], in_=PS1[0:64, ssl]))
                block.vector(qev)
                def qgate(t, now=ch["dve"].n):
                    t.wait_ge(ch["dve"].sem, now)
                block.tensor(qgate)
            def rcp(v, h=h):
                rb = (h % 4) * 32
                dc = h // 4
                ch["dve"].tag(v.tensor_copy(out=qh[64:96, :], in_=qrp[rb:rb + 32, dc * T:(dc + 1) * T]))
                ch["dve"].tag(v.tensor_copy(out=kh[64:96, :], in_=krp[rb:rb + 32, dc * T:(dc + 1) * T]))
            block.vector(rcp)
            dve_rope = ch["dve"].n

            secs = []
            for kt in range(TT):
                q0 = kt * 128
                secs.append((kt, q0, min(q0 + 1024, T)))
                if q0 + 1024 < T:
                    secs.append((kt, q0 + 1024, T))
            ns = len(secs)
            act_base = ch["act"].n
            act_sec = [act_base + si + 1 for si in range(ns)]
            pe_pv = [None] * ns
            exp_list = []
            def emit_pv(si):
                kt, qlo, qhi = secs[si]
                par = si % 2
                def pvmm(t, si=si, kt=kt, qlo=qlo, qhi=qhi, par=par, h=h):
                    t.wait_ge(ch["act"].sem, act_sec[si])
                    b0 = qlo
                    while b0 < qhi:
                        b1 = min((b0 // 512 + 1) * 512, qhi)
                        lastkt = min(TT - 1, (b1 - 1) // 128)
                        i = t.matmul(PS2[0:65, b0:b1],
                                     vaug[:, kt * NH * 65 + h * 65: kt * NH * 65 + h * 65 + 65],
                                     pt[:, par * 1024 + (b0 - qlo): par * 1024 + (b1 - qlo)],
                                     start=(kt == 0), stop=(kt == lastkt))
                        b0 = b1
                    ch["pe"].tag(i)
                block.tensor(pvmm)
                pe_pv[si] = ch["pe"].n
            for si, (kt, qlo, qhi) in enumerate(secs):
                par = si % 2
                ln = qhi - qlo
                def smm(t, h=h, si=si, kt=kt, qlo=qlo, qhi=qhi, par=par, ln=ln):
                    if si >= 2:
                        t.wait_ge(ch["act"].sem, act_sec[si - 2])
                    if si == 0:
                        t.wait_ge(ch["dve"].sem, dve_rope)
                    nps = (ln + 511) // 512
                    for ps in range(nps):
                        w = min(512, ln - ps * 512)
                        i = t.matmul(PS1[:, par * 1024 + ps * 512: par * 1024 + ps * 512 + w],
                                     kh[0:96, kt * 128:kt * 128 + 128],
                                     qh[0:96, qlo + ps * 512: qlo + ps * 512 + w],
                                     start=True, stop=True)
                    ch["pe"].tag(i)
                block.tensor(smm)
                pe_sc = ch["pe"].n

                diag = (qlo == kt * 128)
                if diag:
                    def msk(v, par=par, pe_sc=pe_sc):
                        v.wait_ge(ch["pe"].sem, pe_sc)
                        v.wait_ge(ch["gc0"].sem, gld_x)
                        ch["dve"].tag(v.tensor_add(out=PS1[:, par * 1024: par * 1024 + 128],
                                                   in0=PS1[:, par * 1024: par * 1024 + 128],
                                                   in1=maskt[:]))
                    block.vector(msk)
                dve_m = ch["dve"].n
                exp_list.append((si, par, ln, pe_sc, dve_m, diag))
                if si >= 1:
                    emit_pv(si - 1)
            emit_pv(ns - 1)

            def expall(s, exp_list=tuple(exp_list), pe_pv=tuple(pe_pv)):
                for si, par, ln, pe_sc, dve_m, diag in exp_list:
                    if si >= 2 and pe_pv[si - 2] is not None:
                        s.wait_ge(ch["pe"].sem, pe_pv[si - 2])
                    if diag:
                        s.wait_ge(ch["dve"].sem, dve_m)
                    else:
                        s.wait_ge(ch["pe"].sem, pe_sc)
                    ch["act"].tag(s.activation(out=pt[:, par * 1024: par * 1024 + ln],
                                               in_=PS1[:, par * 1024: par * 1024 + ln],
                                               func=AF.Exp, scale=SCALE))
            block.scalar(expall)
            assert ch["act"].n == act_sec[-1], (ch["act"].n, act_sec[-1])

            def nrm_row(s, pv_done=pe_pv[ns - 1]):
                s.wait_ge(ch["pe"].sem, pv_done)
                ch["act"].tag(s.activation(out=rows[0:1, :], in_=PS2[64:65, 0:T], func=AF.Ln))
                s.wait_ge(ch["act"].sem, ch["act"].n)
                ch["act"].tag(s.activation(out=rowsb[0:1, :], in_=rows[0:1, :],
                                           func=AF.Exp, scale=-1.0))
            block.scalar(nrm_row)
            def nrm_bc(t):
                t.wait_ge(ch["act"].sem, ch["act"].n)
                t.wait_ge(ch["gc0"].sem, gld_x)
                for ps in range(4):
                    i = t.matmul(PS1[0:64, ps * 512:ps * 512 + 512], onesb[0:1, 0:64],
                                 rowsb[0:1, ps * 512:ps * 512 + 512], start=True, stop=True)
                ch["pe"].tag(i)
            block.tensor(nrm_bc)
            def nrm_ev(v, h=h):
                v.wait_ge(ch["pe"].sem, ch["pe"].n)
                ch["dve"].tag(v.tensor_copy(out=bcast[0:64, :], in_=PS1[0:64, 0:T]))
                v.wait_ge(ch["dve"].sem, ch["dve"].n)
                dc = h // 2
                rb = (h % 2) * 64
                ch["dve"].tag(v.tensor_mul(out=abuf[rb:rb + 64, dc * T:(dc + 1) * T],
                                           in0=PS2[0:64, 0:T], in1=bcast[0:64, :]))
            block.vector(nrm_ev)
            def nrm_gate(t, now=ch["dve"].n):
                t.wait_ge(ch["dve"].sem, now)
            block.tensor(nrm_gate)

        st3b.close()

        st4 = ExitStack()
        woB = st4.enter_context(nc.sbuf_tensor("woB", [128, 8 * D], BF16))
        xstr = st4.enter_context(nc.sbuf_tensor("xstr", [128, T], F32))
        x1f = st4.enter_context(nc.sbuf_tensor("x1f", [128, T], F32))

        def p5g(g):
            wait_all(g, [ch["pe"], ch["dve"], ch["act"], ch["gw4"]])
            ch["gwo"].dma(g.dma_start(out=woB[:].rearrange("p (c w) -> p c w", c=8),
                                      in_=wo.rearrange("(c p) w -> p c w", p=128)))
        block.gpsimd(p5g)
        gld_wo = ch["gwo"].n

        def woph(oc):
            def mm(t, oc=oc):
                if oc == 0:
                    wait_all(t, [ch["dve"]])
                    t.wait_ge(ch["gwo"].sem, gld_wo)
                for ps in range(4):
                    for ic in range(8):
                        i = t.matmul(PS1[:, ps * 512:ps * 512 + 512],
                                     woB[:, ic * D + oc * 128: ic * D + oc * 128 + 128],
                                     abuf[:, ic * T + ps * 512: ic * T + ps * 512 + 512],
                                     start=(ic == 0), stop=(ic == 7))
                ch["pe"].tag(i)
            block.tensor(mm)
            pe_now = ch["pe"].n
            def ld(s, oc=oc):
                if oc == 0:
                    wait_all(s, [ch["pe"], ch["dve"], ch["act"]])
                else:
                    s.wait_ge(ch["dve"].sem, x1tag[oc - 1][0])
                ch["sld"].dma(s.dma_start(out=xstr[:], in_=xT[oc * 128:(oc + 1) * 128, :]))
            block.sync(ld)
            sld_now = ch["sld"].n
            def ev(v, oc=oc, pe_now=pe_now, sld_now=sld_now):
                v.wait_ge(ch["pe"].sem, pe_now)
                v.wait_ge(ch["sld"].sem, sld_now)
                if oc >= 1:
                    v.wait_ge(ch["gx1"].sem, x1tag[oc - 1][1])
                ch["dve"].tag(v.tensor_add(out=x1f[:], in0=PS1[:, 0:T], in1=xstr[:]))
                v.wait_ge(ch["dve"].sem, ch["dve"].n)
                ch["dve"].tag(v.tensor_copy(out=lnb[:, oc * T:(oc + 1) * T], in_=x1f[:]))
            block.vector(ev)
            def st(g, oc=oc, now=ch["dve"].n):
                g.wait_ge(ch["dve"].sem, now)
                ch["gx1"].dma(g.dma_start(out=x1d.ap()[oc * 128:(oc + 1) * 128, :],
                                          in_=x1f[:]))
                x1tag.append((ch["dve"].n, ch["gx1"].n))
            block.gpsimd(st)
            def gate(t, now=ch["dve"].n):
                t.wait_ge(ch["dve"].sem, now)
            block.tensor(gate)
        x1tag = []
        for oc in range(8):
            woph(oc)
        st4.close()
        st3a.close()
        st2.close()

        emit_ln(lnb, 1e-5)
        st5 = ExitStack()
        E5 = st5.enter_context
        xs2 = E5(nc.sbuf_tensor("xs2", [128, T], F32))
        ln2f = E5(nc.sbuf_tensor("ln2f", [128, T], F32))
        routB = E5(nc.sbuf_tensor("routB", [128, 8 * NE], F32))
        nbc = E5(nc.sbuf_tensor("nbc", [128, T], F32))
        rbc = E5(nc.sbuf_tensor("rbc", [128, T], F32))
        logitsT = E5(nc.sbuf_tensor("logitsT", [128, TT * NE], F32))
        lgf = E5(nc.sbuf_tensor("lgf", [128, T], F32))

        def rload(s):
            wait_all(s, [ch["pe"], ch["dve"], ch["act"], ch["gwo"], ch["gx1"]])
            ch["sr"].dma(s.dma_start(out=routB[:].rearrange("p (c w) -> p c w", c=8), in_=routerT.rearrange("(c p) w -> p c w", p=128)))
        block.sync(rload)
        emit_bcast(32)
        def cpn(v):
            v.wait_ge(ch["dve"].sem, ch["dve"].n)
            ch["dve"].tag(v.tensor_copy(out=nbc[:], in_=bcast[:]))
        block.vector(cpn)
        emit_bcast(64)
        def cpr(v):
            v.wait_ge(ch["dve"].sem, ch["dve"].n)
            ch["dve"].tag(v.tensor_copy(out=rbc[:], in_=bcast[:]))
        block.vector(cpr)

        l2tags = []
        for dc in range(8):
            def l2ld(s, dc=dc):
                if dc == 0:
                    wait_all(s, [ch["dve"]])
                else:
                    s.wait_ge(ch["dve"].sem, l2tags[dc - 1])
                ch["sld"].dma(s.dma_start(out=xs2[:], in_=x1d.ap()[dc * 128:(dc + 1) * 128, :]))
            block.sync(l2ld)
            sld_now = ch["sld"].n
            def l2f(v, dc=dc, sld_now=sld_now):
                v.wait_ge(ch["sld"].sem, sld_now)
                if dc > 0:
                    v.wait_ge(ch["pe"].sem, ch["pe"].n)
                ch["dve"].tag(v.tensor_add(out=ln2f[:], in0=xs2[:], in1=nbc[:]))
                l2tags.append(ch["dve"].n)
                v.wait_ge(ch["dve"].sem, ch["dve"].n)
                ch["dve"].tag(v.tensor_mul(out=ln2f[:], in0=ln2f[:], in1=rbc[:]))
            block.vector(l2f)
            def l2mm(t, dc=dc, now=ch["dve"].n):
                t.wait_ge(ch["dve"].sem, now)
                if dc == 0:
                    t.wait_ge(ch["sr"].sem, 16)
                for ps in range(4):
                    i = t.matmul(PS2[0:NE, ps * 512:ps * 512 + 512],
                                 routB[:, dc * NE:(dc + 1) * NE],
                                 ln2f[:, ps * 512:ps * 512 + 512],
                                 start=(dc == 0), stop=(dc == 7))
                ch["pe"].tag(i)
            block.tensor(l2mm)
        def lgev2(v):
            v.wait_ge(ch["pe"].sem, ch["pe"].n)
            ch["dve"].tag(v.tensor_copy(out=lgf[0:NE, 0:T], in_=PS2[0:NE, 0:T]))
        block.vector(lgev2)
        for tt in range(TT):
            def ltr(t, tt=tt, now=ch["dve"].n):
                t.wait_ge(ch["dve"].sem, now)
                t.wait_ge(ch["gc0"].sem, gld_x)
                i = t.transpose(PS1[:, (tt % 8) * 128:(tt % 8) * 128 + NE],
                                lgf[0:NE, tt * 128:(tt + 1) * 128], identb[0:NE, 0:NE])
                ch["pe"].tag(i)
            block.tensor(ltr)
            pe_now = ch["pe"].n
            def lte(v, tt=tt, pe_now=pe_now):
                v.wait_ge(ch["pe"].sem, pe_now)
                ch["dve"].tag(v.tensor_copy(out=logitsT[:, tt * NE:(tt + 1) * NE],
                                            in_=PS1[:, (tt % 8) * 128:(tt % 8) * 128 + NE]))
            block.vector(lte)
            def ltg(t, now=ch["dve"].n):
                t.wait_ge(ch["dve"].sem, now)
            block.tensor(ltg)

        def l2a(v):
            for dc in range(8):
                ch["dve"].tag(v.tensor_add(out=lnb[:, dc * T:(dc + 1) * T],
                                           in0=lnb[:, dc * T:(dc + 1) * T], in1=nbc[:]))
            v.wait_ge(ch["dve"].sem, ch["dve"].n)
            for dc in range(8):
                ch["dve"].tag(v.tensor_mul(out=lnb[:, dc * T:(dc + 1) * T],
                                           in0=lnb[:, dc * T:(dc + 1) * T], in1=rbc[:]))
        block.vector(l2a)

        mx8 = E5(nc.sbuf_tensor("mx8", [128, TT * 8], F32))
        selb = E5(nc.sbuf_tensor("selb", [128, TT * NE], BF16))
        wn = E5(nc.sbuf_tensor("wn", [128, TT * NE], F32))
        wsum = E5(nc.sbuf_tensor("wsum", [128, 2], F32))
        cumo = E5(nc.sbuf_tensor("cumo", [128, TT * NE], F32))
        selw = E5(nc.sbuf_tensor("selw", [128, NE], F32))

        for tt in range(TT):
            lsl = slice(tt * NE, (tt + 1) * NE)
            def r1(v, tt=tt, lsl=lsl, dnow=ch["dve"].n):
                v.wait_ge(ch["dve"].sem, dnow)
                ch["dve"].tag(v.max(out=mx8[:, tt * 8:(tt + 1) * 8], in_=logitsT[:, lsl]))
                v.wait_ge(ch["dve"].sem, ch["dve"].n)
                ch["dve"].tag(v.tensor_scalar(out=selw[:], in0=logitsT[:, lsl],
                                              scalar1=mx8[:, tt * 8 + 7: tt * 8 + 8],
                                              scalar2=None, op0=ALU.is_ge))
                ch["dve"].tag(v.tensor_scalar(out=wsum[:, 1:2], in0=mx8[:, tt * 8: tt * 8 + 1],
                                              scalar1=-1.0, scalar2=None, op0=ALU.mult))
            block.vector(r1)
            def r2(s, tt=tt, lsl=lsl, now=ch["dve"].n):
                s.wait_ge(ch["dve"].sem, now)
                ch["act"].tag(s.activation(out=wn[:, lsl], in_=logitsT[:, lsl],
                                           func=AF.Exp, bias=wsum[:, 1:2]))
            block.scalar(r2)
            def r3(v, tt=tt, lsl=lsl, now=ch["act"].n):
                v.wait_ge(ch["act"].sem, now)
                ch["dve"].tag(v.scalar_tensor_tensor(out=wn[:, lsl], in0=wn[:, lsl],
                                                     scalar=1.0, in1=selw[:],
                                                     op0=ALU.mult, op1=ALU.mult,
                                                     accum_out=wsum[:, 0:1]))
                ch["dve"].tag(v.tensor_copy(out=selb[:, lsl], in_=selw[:]))
            block.vector(r3)
            def r4(s, now=ch["dve"].n):
                s.wait_ge(ch["dve"].sem, now)
                ch["act"].tag(s.activation(out=wsum[:, 1:2], in_=wsum[:, 0:1], func=AF.Ln))
                s.wait_ge(ch["act"].sem, ch["act"].n)
                ch["act"].tag(s.activation(out=wsum[:, 1:2], in_=wsum[:, 1:2],
                                           func=AF.Exp, scale=-1.0))
            block.scalar(r4)
            def r5(v, tt=tt, lsl=lsl, now=ch["act"].n):
                v.wait_ge(ch["act"].sem, now)
                ch["dve"].tag(v.tensor_scalar(out=wn[:, lsl], in0=wn[:, lsl],
                                              scalar1=wsum[:, 1:2], scalar2=None, op0=ALU.mult))
            block.vector(r5)

        def cums(t):
            wait_all(t, [ch["dve"]])
            for qt in range(TT):
                mms = [("tri", qt)] + [("ones", kt) for kt in range(qt)]
                for j, (kind, kt) in enumerate(mms):
                    lhs = trib[:] if kind == "tri" else onesb[:]
                    i = t.matmul(PS1[:, qt * NE:(qt + 1) * NE], lhs,
                                 selb[:, kt * NE:(kt + 1) * NE],
                                 start=(j == 0), stop=(j == len(mms) - 1))
            ch["pe"].tag(i)
        block.tensor(cums)
        def cume(v):
            v.wait_ge(ch["pe"].sem, ch["pe"].n)
            ch["dve"].tag(v.tensor_copy(out=cumo[:], in_=PS1[:, 0:TT * NE]))
        block.vector(cume)

        def outs(s, now=ch["dve"].n):
            s.wait_ge(ch["dve"].sem, now)
            ch["sst"].dma(s.dma_start(out=o_cum, in_=cumo[:]))
            ch["sst"].dma(s.dma_start(out=o_wn, in_=wn[:]))
            ch["sst"].dma(s.dma_start(out=o_srow, in_=cumo[127:128, 15 * NE:16 * NE]))
            ch["sst"].dma(s.dma_start(out=o_ln2.rearrange("(c p) w -> p c w", p=128), in_=lnb[:].rearrange("p (c w) -> p c w", c=8)))
        block.sync(outs)

        wsguB = E5(nc.sbuf_tensor("wsguB", [128, 8 * 2 * SF], BF16))
        wsdnB = E5(nc.sbuf_tensor("wsdnB", [128, 6 * D], BF16))
        hbuf = E5(nc.sbuf_tensor("hbuf", [128, 6 * T], BF16))

        def shg(g):
            wait_all(g, [ch["pe"], ch["dve"], ch["act"]])
            ch["gsh"].dma(g.dma_start(out=wsguB[:].rearrange("p (c w) -> p c w", c=8), in_=wsgu.rearrange("(c p) w -> p c w", p=128)))
            ch["gsh"].dma(g.dma_start(out=wsdnB[:].rearrange("p (c w) -> p c w", c=6), in_=wsdn.rearrange("(c p) w -> p c w", p=128)))
        block.gpsimd(shg)
        gld_sh = ch["gsh"].n

        gu_tags = []
        act_base7 = ch["act"].n
        pe_base7 = ch["pe"].n
        dve_base7 = ch["dve"].n
        for fc in range(6):
            for ps in range(4):
                idx = fc * 4 + ps
                def gmm(t, fc=fc, ps=ps, idx=idx):
                    if idx == 0:
                        t.wait_ge(ch["gsh"].sem, gld_sh)
                        wait_all(t, [ch["dve"]])
                    elif idx >= 4:
                        t.wait_ge(ch["dve"].sem, dve_base7 + (idx - 4) + 1)
                    for ic in range(8):
                        t.matmul(PS1[:, ps * 512:ps * 512 + 512],
                                 wsguB[:, ic * 2 * SF + fc * 128: ic * 2 * SF + fc * 128 + 128],
                                 lnb[:, ic * T + ps * 512: ic * T + ps * 512 + 512],
                                 start=(ic == 0), stop=(ic == 7))
                    for ic in range(8):
                        i = t.matmul(PS2[:, ps * 512:ps * 512 + 512],
                                     wsguB[:, ic * 2 * SF + SF + fc * 128: ic * 2 * SF + SF + fc * 128 + 128],
                                     lnb[:, ic * T + ps * 512: ic * T + ps * 512 + 512],
                                     start=(ic == 0), stop=(ic == 7))
                    ch["pe"].tag(i)
                block.tensor(gmm)
                def gmu(v, fc=fc, ps=ps, idx=idx):
                    v.wait_ge(ch["act"].sem, act_base7 + idx + 1)
                    ch["dve"].tag(v.tensor_mul(out=hbuf[:, fc * T + ps * 512: fc * T + ps * 512 + 512],
                                               in0=PS2[:, ps * 512:ps * 512 + 512],
                                               in1=sq[:, ps * 512:ps * 512 + 512]))
                block.vector(gmu)
                gu_tags.append(ch["dve"].n)
        def gsl_all(s):
            for fc in range(6):
                for ps in range(4):
                    idx = fc * 4 + ps
                    s.wait_ge(ch["pe"].sem, pe_base7 + idx + 1)
                    ch["act"].tag(s.activation(out=sq[:, ps * 512:ps * 512 + 512],
                                               in_=PS1[:, ps * 512:ps * 512 + 512],
                                               func=AF.Silu))
        block.scalar(gsl_all)
        assert ch["act"].n == act_base7 + 24
        assert ch["pe"].n == pe_base7 + 24

        devtags = []
        gu_last = ch["dve"].n
        for oc in range(8):
            def dmm(t, oc=oc):
                if oc == 0:
                    t.wait_ge(ch["dve"].sem, gu_last)
                for ps in range(4):
                    for ic in range(6):
                        i = t.matmul(PS1[:, ps * 512:ps * 512 + 512],
                                     wsdnB[:, ic * D + oc * 128: ic * D + oc * 128 + 128],
                                     hbuf[:, ic * T + ps * 512: ic * T + ps * 512 + 512],
                                     start=(ic == 0), stop=(ic == 5))
                ch["pe"].tag(i)
            block.tensor(dmm)
            pe_now = ch["pe"].n
            def dld(s, oc=oc):
                if oc >= 1:
                    s.wait_ge(ch["dve"].sem, devtags[oc - 1])
                ch["sld"].dma(s.dma_start(out=xs2[:], in_=x1d.ap()[oc * 128:(oc + 1) * 128, :]))
            block.sync(dld)
            sld_now = ch["sld"].n
            def dev(v, oc=oc, pe_now=pe_now, sld_now=sld_now):
                v.wait_ge(ch["pe"].sem, pe_now)
                v.wait_ge(ch["sld"].sem, sld_now)
                ch["dve"].tag(v.tensor_add(out=ln2f[:], in0=PS1[:, 0:T], in1=xs2[:]))
                devtags.append(ch["dve"].n)
            block.vector(dev)
            def dst(s, oc=oc, now=ch["dve"].n):
                s.wait_ge(ch["dve"].sem, now)
                ch["sst"].dma(s.dma_start(out=o_xsh[oc * 128:(oc + 1) * 128, :], in_=ln2f[:]))
            block.sync(dst)
            def dgate(t, now=ch["dve"].n):
                t.wait_ge(ch["dve"].sem, now)
            def dgate2(v, now=ch["sst"].n):
                v.wait_ge(ch["sst"].sem, now)
            block.tensor(dgate)
            block.vector(dgate2)

        st5.close()

        def fin(s):
            s.wait_ge(ch["sst"].sem, ch["sst"].n)
        block.sync(fin)

    return nc


def l1_inputs(x_core, w, consts):
    return {
        "xT": np.ascontiguousarray(x_core.T),
        "wkv": w["L0_attn_W_kv"], "gkv": np.ascontiguousarray(w["L0_attn_g_kv"].reshape(2, 128).T),
        "wkdec": w["L0_attn_W_kdec"], "wvdec": w["L0_attn_W_vdec"],
        "wq": w["L0_attn_W_q"], "wqdec": w["L0_attn_W_qdec"],
        "wkrope": w["L0_attn_W_krope"], "wkropeS": rope_shuffle_cols(w["L0_attn_W_krope"]),
        "wqrope": w["L0_attn_W_qrope"], "wqropeS": rope_shuffle_cols(w["L0_attn_W_qrope"]),
        "wo": w["L0_attn_W_o"],
        "routerT": np.ascontiguousarray(w["L0_ffn_router_w"].T),
        "wsgu": w["L0_ffn_ws_gate_up"], "wsdn": w["L0_ffn_ws_down"],
        "IDENT": consts["IDENT"], "TRICUM": consts["TRICUM"], "ONES": consts["ONES"],
        "MASKTRI": consts["MASKTRI"], "COS": consts["COS"], "SIN": consts["SIN"],
    }


MAPN = 32896


def build_l2():
    nc = bass.Bass("TRN2", target_bir_lowering=False)
    def din(name, shape, dt=F32):
        return nc.dram_tensor(name, list(shape), dt, kind="ExternalInput").ap()
    cum = din("cum", (128, TT * NE))
    wn = din("wnorm", (128, TT * NE))
    srowA = din("srowA", (8, NE))
    cmask = din("cmask", (8, 1))
    tokid1 = din("tokid1", (128, TT))
    E1024W = din("E1024W", (128, TT * NE))
    E8W = din("E8W", (128, TT * NE))
    TRASHW = din("TRASHW", (128, TT * 8))
    ONESC = din("ONES", (128, 128))

    o_map = nc.dram_tensor("mapc", [MAPN, 1], F32, kind="ExternalOutput").ap()
    o_s8 = nc.dram_tensor("slot8", [128, TT * 8], F32, kind="ExternalOutput").ap()
    o_w8 = nc.dram_tensor("w8v", [128, TT * 8], F32, kind="ExternalOutput").ap()

    with ExitStack() as ctx:
        E = ctx.enter_context
        ch = {k: Ch(ctx, nc, k) for k in ("sld", "gsc", "pe", "dve", "act", "sst")}
        scum = E(nc.sbuf_tensor("scum", [128, TT * NE], F32))
        swn = E(nc.sbuf_tensor("swn", [128, TT * NE], F32))
        ssrow = E(nc.sbuf_tensor("ssrow", [8, NE], F32))
        scm = E(nc.sbuf_tensor("scm", [8, 1], F32))
        stok = E(nc.sbuf_tensor("stok", [128, TT], F32))
        se1024 = E(nc.sbuf_tensor("se1024", [128, TT * NE], F32))
        se8 = E(nc.sbuf_tensor("se8", [128, TT * NE], F32))
        strash = E(nc.sbuf_tensor("strash", [128, TT * 8], F32))
        sonesF = E(nc.sbuf_tensor("sonesF", [128, 128], F32))
        obb = E(nc.sbuf_tensor("obb", [128, NE], F32))
        orow = E(nc.sbuf_tensor("orow", [1, NE], F32))
        zet = E(nc.sbuf_tensor("zet", [128, 257], F32))
        t1 = E(nc.sbuf_tensor("t1", [128, TT * NE], F32))
        t2 = E(nc.sbuf_tensor("t2", [128, TT * NE], F32))
        t3 = E(nc.sbuf_tensor("t3", [128, TT * NE], F32))
        sv8a = E(nc.sbuf_tensor("sv8a", [128, TT * 8], F32))
        su8a = E(nc.sbuf_tensor("su8a", [128, TT * 8], F32))
        svi = E(nc.sbuf_tensor("svi", [128, TT * 8], I32))
        svb = E(nc.sbuf_tensor("svb", [128, TT * 8], I32))
        validf = E(nc.sbuf_tensor("validf", [128, TT * 8], F32))
        sslot = E(nc.sbuf_tensor("sslot", [128, TT * 8], F32))
        sw8 = E(nc.sbuf_tensor("sw8", [128, TT * 8], F32))
        sidx = E(nc.sbuf_tensor("sidx", [128, TT * 8], I32))
        PS = E(nc.psum_tensor("PS", [128, 512], F32))
        block = E(nc.Block())

        def lds(s):
            for dst, srcp in ((scum[:], cum), (swn[:], wn), (ssrow[:], srowA),
                              (scm[:], cmask), (stok[:], tokid1), (se1024[:], E1024W),
                              (se8[:], E8W), (strash[:], TRASHW), (sonesF[:], ONESC)):
                ch["sld"].dma(s.dma_start(out=dst, in_=srcp))
        block.sync(lds)
        sld0 = ch["sld"].n

        def zeroV(v):
            ch["dve"].tag(v.memset(zet[:], 0.0))
        block.vector(zeroV)
        dve_z = ch["dve"].n
        def zero(g):
            g.wait_ge(ch["dve"].sem, dve_z)
            ch["gsc"].dma(g.dma_start(out=o_map.rearrange("(p b) v -> p (b v)", p=128), in_=zet[:]))
        block.gpsimd(zero)

        def off(t):
            t.wait_ge(ch["sld"].sem, sld0)
            i = t.matmul(PS[0:1, 0:NE], scm[:], ssrow[:], start=True, stop=True)
            ch["pe"].tag(i)
        block.tensor(off)
        def offe(v):
            v.wait_ge(ch["pe"].sem, ch["pe"].n)
            ch["dve"].tag(v.tensor_copy(out=orow[:], in_=PS[0:1, 0:NE]))
        block.vector(offe)
        def offb(t, now=ch["dve"].n):
            t.wait_ge(ch["dve"].sem, now)
            i = t.matmul(PS[:, 0:NE], sonesF[0:1, :], orow[:], start=True, stop=True)
            ch["pe"].tag(i)
        block.tensor(offb)
        def offbe(v):
            v.wait_ge(ch["pe"].sem, ch["pe"].n)
            ch["dve"].tag(v.tensor_copy(out=obb[:], in_=PS[:, 0:NE]))
        block.vector(offbe)

        def chain(v):
            C = ch["dve"]
            def step(i):
                C.tag(i)
                v.wait_ge(C.sem, C.n)
            v.wait_ge(C.sem, C.n)
            for tt in range(TT):
                C.tag(v.tensor_add(out=t1[:, tt * NE:(tt + 1) * NE],
                                   in0=scum[:, tt * NE:(tt + 1) * NE], in1=obb[:]))
            v.wait_ge(C.sem, C.n)
            step(v.tensor_scalar(out=t1[:], in0=t1[:], scalar1=1.0, scalar2=None, op0=ALU.subtract))
            step(v.tensor_scalar(out=t3[:], in0=swn[:], scalar1=0.0, scalar2=None, op0=ALU.is_gt))
            step(v.tensor_scalar(out=t2[:], in0=t1[:], scalar1=512.0, scalar2=None, op0=ALU.is_lt))
            step(v.tensor_scalar(out=t1[:], in0=t1[:], scalar1=511.0, scalar2=None, op0=ALU.min))
            step(v.tensor_scalar(out=t1[:], in0=t1[:], scalar1=0.0, scalar2=None, op0=ALU.max))
            step(v.tensor_scalar(out=t1[:], in0=t1[:], scalar1=2.0, scalar2=None, op0=ALU.mult))
            step(v.tensor_add(out=t1[:], in0=t1[:], in1=se1024[:]))
            step(v.tensor_add(out=t1[:], in0=t1[:], in1=t2[:]))
            step(v.tensor_scalar(out=t1[:], in0=t1[:], scalar1=65536.0, scalar2=None, op0=ALU.add))
            step(v.tensor_mul(out=t1[:], in0=t1[:], in1=t3[:]))
            step(v.tensor_scalar(out=t1[:], in0=t1[:], scalar1=65536.0, scalar2=None, op0=ALU.subtract))
            for tt in range(TT):
                C.tag(v.max(out=sv8a[:, tt * 8:(tt + 1) * 8], in_=t1[:, tt * NE:(tt + 1) * NE]))
            v.wait_ge(C.sem, C.n)
            step(v.tensor_scalar(out=t2[:], in0=swn[:], scalar1=4.0, scalar2=None, op0=ALU.mult))
            step(v.tensor_add(out=t2[:], in0=t2[:], in1=se8[:]))
            step(v.tensor_scalar(out=t2[:], in0=t2[:], scalar1=1024.0, scalar2=None, op0=ALU.add))
            step(v.tensor_mul(out=t2[:], in0=t2[:], in1=t3[:]))
            step(v.tensor_scalar(out=t2[:], in0=t2[:], scalar1=1024.0, scalar2=None, op0=ALU.subtract))
            for tt in range(TT):
                C.tag(v.max(out=su8a[:, tt * 8:(tt + 1) * 8], in_=t2[:, tt * NE:(tt + 1) * NE]))
            v.wait_ge(C.sem, C.n)
            step(v.tensor_copy(out=svi[:], in_=sv8a[:]))
            step(v.tensor_scalar(out=svb[:], in0=svi[:], scalar1=1, scalar2=None, op0=ALU.bitwise_and))
            step(v.tensor_scalar(out=svi[:], in0=svi[:], scalar1=1, scalar2=None, op0=ALU.arith_shift_right))
            step(v.tensor_copy(out=sslot[:], in_=svi[:]))
            step(v.tensor_copy(out=validf[:], in_=svb[:]))
            step(v.tensor_scalar(out=t1[:, 0:TT * 8], in0=su8a[:], scalar1=0.125, scalar2=None, op0=ALU.mult))
            step(v.tensor_copy(out=svi[:], in_=t1[:, 0:TT * 8]))
            step(v.tensor_copy(out=t1[:, 0:TT * 8], in_=svi[:]))
            step(v.tensor_scalar(out=t1[:, 0:TT * 8], in0=t1[:, 0:TT * 8], scalar1=-8.0, scalar2=None, op0=ALU.mult))
            step(v.tensor_add(out=t1[:, 0:TT * 8], in0=t1[:, 0:TT * 8], in1=su8a[:]))
            step(v.tensor_scalar(out=t1[:, 0:TT * 8], in0=t1[:, 0:TT * 8], scalar1=0.25, scalar2=None, op0=ALU.mult))
            step(v.tensor_scalar(out=t3[:, 0:TT * 8], in0=t1[:, 0:TT * 8], scalar1=0.0, scalar2=None, op0=ALU.is_lt))
            step(v.tensor_scalar(out=t3[:, 0:TT * 8], in0=t3[:, 0:TT * 8], scalar1=2.0, scalar2=None, op0=ALU.mult))
            step(v.tensor_add(out=t1[:, 0:TT * 8], in0=t1[:, 0:TT * 8], in1=t3[:, 0:TT * 8]))
            step(v.tensor_mul(out=sw8[:], in0=t1[:, 0:TT * 8], in1=validf[:]))
            step(v.tensor_scalar(out=t3[:, 0:TT * 8], in0=validf[:], scalar1=-1.0, scalar2=None, op0=ALU.mult))
            step(v.tensor_scalar(out=t3[:, 0:TT * 8], in0=t3[:, 0:TT * 8], scalar1=1.0, scalar2=None, op0=ALU.add))
            step(v.tensor_mul(out=t3[:, 0:TT * 8], in0=t3[:, 0:TT * 8], in1=strash[:]))
            step(v.tensor_mul(out=t2[:, 0:TT * 8], in0=sslot[:], in1=validf[:]))
            step(v.tensor_add(out=t2[:, 0:TT * 8], in0=t2[:, 0:TT * 8], in1=t3[:, 0:TT * 8]))
            step(v.tensor_copy(out=sidx[:], in_=t2[:, 0:TT * 8]))
        block.vector(chain)

        dnow = ch["dve"].n
        def scat(g):
            g.wait_ge(ch["dve"].sem, dnow)
            g.wait_ge(ch["gsc"].sem, 16)
            for tt in range(TT):
                for k in range(8):
                    ch["gsc"].dma(g.indirect_dma_start(
                        out=o_map, out_offset=bass.IndirectOffsetOnAxis(sidx[:, tt * 8 + k: tt * 8 + k + 1], 0),
                        in_=stok[:, tt:tt + 1], in_offset=None))
            g.wait_ge(ch["gsc"].sem, ch["gsc"].n)
        block.gpsimd(scat)

        def outs(s):
            s.wait_ge(ch["dve"].sem, dnow)
            ch["sst"].dma(s.dma_start(out=o_s8, in_=sslot[:]))
            ch["sst"].dma(s.dma_start(out=o_w8, in_=sw8[:]))
            s.wait_ge(ch["sst"].sem, ch["sst"].n)
        block.sync(outs)
    return nc


def build_l3():
    nc = bass.Bass("TRN2", target_bir_lowering=False)
    def din(name, shape, dt=F32):
        return nc.dram_tensor(name, list(shape), dt, kind="ExternalInput").ap()
    maps = din("maps", (8, MAPN))
    ln2r = din("ln2r", (NC * T, D), BF16)
    myrows = din("myrows", (32, 1), I32)
    wg = din("wg", (NEL, D, FF))
    wu = din("wu", (NEL, D, FF))
    wd = din("wd", (NEL, FF, D))
    IDENT = din("IDENT", (128, 128))
    o_y = nc.dram_tensor("y", [NEL * CAP, D], BF16, kind="ExternalOutput").ap()
    mapd = nc.dram_tensor("mapd", [MAPN, 1], F32)

    with ExitStack() as ctx:
        E = ctx.enter_context
        ch = {k: Ch(ctx, nc, k) for k in ("sld", "gw", "gw2", "gg", "gg2", "pe", "dve", "act", "sst")}
        gwch = [ch["gw"], ch["gw2"]]
        ggx = [ch["gg"], ch["gg2"]]
        smaps = E(nc.sbuf_tensor("smaps", [128, 8 * 257], F32))
        identb = E(nc.sbuf_tensor("identb", [128, 128], BF16))
        identF = E(nc.sbuf_tensor("identF", [128, 128], F32))
        trtags = []
        trtags2 = []
        wfree = []
        smyr = E(nc.sbuf_tensor("smyr", [32, 1], I32))
        mext = E(nc.sbuf_tensor("mext", [32, 128], F32))
        mtr = E(nc.sbuf_tensor("mtr", [128, 32], F32))
        xidx = E(nc.sbuf_tensor("xidx", [128, 32], I32))
        xg = [E(nc.sbuf_tensor(f"xg{i}", [128, 1024], BF16)) for i in range(2)]
        xbT = E(nc.sbuf_tensor("xbT", [128, 8 * (NEL * CAP)], BF16))
        hbuf = E(nc.sbuf_tensor("hbuf", [128, 6 * CAP], BF16))
        sqs = E(nc.sbuf_tensor("sqs", [128, 512], BF16))
        yev = E(nc.sbuf_tensor("yev", [128, D], BF16))
        wgB = [E(nc.sbuf_tensor(f"wgB{i}", [128, 8 * FF], BF16)) for i in range(2)]
        wuB = [E(nc.sbuf_tensor(f"wuB{i}", [128, 8 * FF], BF16)) for i in range(2)]
        wdB = [E(nc.sbuf_tensor(f"wdB{i}", [128, 6 * D], BF16)) for i in range(2)]
        PS1 = E(nc.psum_tensor("PS1", [128, 1024], F32))
        PS2 = E(nc.psum_tensor("PS2", [128, 1024], F32))
        PSB = [E(nc.psum_tensor("PSBa", [128, 128], BF16)),
               E(nc.psum_tensor("PSBb", [128, 128], BF16))]
        block = E(nc.Block())

        def lds(s):
            ch["sld"].dma(s.dma_start(out=smaps[:].rearrange("p (c b) -> p c b", c=8),
                                      in_=maps.rearrange("c (p b) -> p c b", p=128)))
            ch["sld"].dma(s.dma_start(out=smyr[:], in_=myrows))
        block.sync(lds)
        sld0 = ch["sld"].n
        def wload(g, e, par):
            for dst, srcp, nch in ((wgB[par], wg, 8), (wuB[par], wu, 8), (wdB[par], wd, 6)):
                gwch[par].dma(g.dma_start(
                    out=dst[:].rearrange("p (c w) -> p c w", c=nch),
                    in_=srcp[e].rearrange("(c p) w -> p c w", p=128)))
        def ldg0(g):
            ch["gg"].dma(g.dma_start(out=identb[:], in_=IDENT))
            ch["gg"].dma(g.dma_start(out=identF[:], in_=IDENT))
            wload(g, 0, 0)
        block.gpsimd(ldg0)
        gw_id = ch["gg"].n

        def msum(v):
            v.wait_ge(ch["sld"].sem, sld0)
            for c in range(1, 8):
                ch["dve"].tag(v.tensor_add(out=smaps[:, 0:257], in0=smaps[:, 0:257],
                                           in1=smaps[:, c * 257:(c + 1) * 257]))
                v.wait_ge(ch["dve"].sem, ch["dve"].n)
        block.vector(msum)

        def mst(g, now=ch["dve"].n):
            g.wait_ge(ch["dve"].sem, now)
            ch["gg"].dma(g.dma_start(out=mapd.ap().rearrange("(p b) v -> p (b v)", p=128),
                                     in_=smaps[:, 0:257]))
            g.wait_ge(ch["gg"].sem, ch["gg"].n)
            ch["gg"].dma(g.indirect_dma_start(
                out=mext[:], out_offset=None,
                in_=mapd.ap()[0:32768, :].rearrange("(r q) v -> r (q v)", q=128),
                in_offset=bass.IndirectOffsetOnAxis(smyr[:, 0:1], 0)))
            g.wait_ge(ch["gg"].sem, ch["gg"].n)
        block.gpsimd(mst)
        gg_ext = ch["gg"].n

        def mtp(t):
            t.wait_ge(ch["gg"].sem, gg_ext)
            i = t.transpose(PS1[:, 0:32], mext[:], identF[0:32, 0:32])
            ch["pe"].tag(i)
        block.tensor(mtp)
        def mte(v):
            v.wait_ge(ch["pe"].sem, ch["pe"].n)
            ch["dve"].tag(v.tensor_copy(out=mtr[:], in_=PS1[:, 0:32]))
            v.wait_ge(ch["dve"].sem, ch["dve"].n)
            ch["dve"].tag(v.tensor_scalar(out=mtr[:], in0=mtr[:], scalar1=1.0, scalar2=None, op0=ALU.max))
            v.wait_ge(ch["dve"].sem, ch["dve"].n)
            ch["dve"].tag(v.tensor_scalar(out=mtr[:], in0=mtr[:], scalar1=1.0, scalar2=None, op0=ALU.subtract))
            v.wait_ge(ch["dve"].sem, ch["dve"].n)
            ch["dve"].tag(v.tensor_copy(out=xidx[:], in_=mtr[:]))
        block.vector(mte)
        dve_idx = ch["dve"].n

        for j in range(32):
            jp = j % 2
            def gat(g, j=j, jp=jp, dve_idx=dve_idx):
                g.wait_ge(ggx[jp].sem, ggx[jp].n)
                if j == 0:
                    g.wait_ge(ch["dve"].sem, dve_idx)
                    g.wait_ge(ch["gg"].sem, ch["gg"].n)
                elif j >= 2:
                    g.wait_ge(ch["pe"].sem, trtags[j - 2])
                ggx[jp].dma(g.indirect_dma_start(
                    out=xg[jp][:], out_offset=None, in_=ln2r,
                    in_offset=bass.IndirectOffsetOnAxis(xidx[:, j:j + 1], 0)))
            block.gpsimd(gat)
            gg_now = ggx[jp].n
            for dc in range(8):
                gi = j * 8 + dc
                def trp(t, j=j, jp=jp, dc=dc, gg_now=gg_now, gi=gi):
                    t.wait_ge(ggx[jp].sem, gg_now)
                    if gi >= 2:
                        t.wait_ge(ch["dve"].sem, trtags2[gi - 2])
                    i = t.transpose(PSB[gi % 2][:, 0:128],
                                    xg[jp][:, dc * 128:(dc + 1) * 128], identb[:])
                    ch["pe"].tag(i)
                block.tensor(trp)
                pe_now = ch["pe"].n
                def tre(v, j=j, dc=dc, pe_now=pe_now, gi=gi):
                    v.wait_ge(ch["pe"].sem, pe_now)
                    ch["dve"].tag(v.tensor_copy(
                        out=xbT[:, dc * (NEL * CAP) + j * 128: dc * (NEL * CAP) + j * 128 + 128],
                        in_=PSB[gi % 2][:, 0:128]))
                block.vector(tre)
                trtags2.append(ch["dve"].n)
            trtags.append(ch["pe"].n)

        gwtags = [ch["gw"].n]

        ysttags = []
        for e in range(NEL):
            par = e % 2
            if e + 1 < NEL:
                def wln(g, e=e, par=par):
                    if e >= 1:
                        g.wait_ge(ch["pe"].sem, wfree[e - 1])
                    wload(g, e + 1, 1 - par)
                block.gpsimd(wln)
                gwtags.append(gwch[(e + 1) % 2].n)
            base = e * CAP
            for fc in range(6):
                def gmm(t, e=e, par=par, fc=fc):
                    if fc == 0:
                        t.wait_ge(gwch[par].sem, gwtags[e])
                        wait_all(t, [ch["dve"]])
                    for ic in range(8):
                        t.matmul(PS1[:, 0:512],
                                 wgB[par][:, ic * FF + fc * 128: ic * FF + fc * 128 + 128],
                                 xbT[:, ic * (NEL * CAP) + base: ic * (NEL * CAP) + base + 512],
                                 start=(ic == 0), stop=(ic == 7))
                    for ic in range(8):
                        i = t.matmul(PS2[:, 0:512],
                                     wuB[par][:, ic * FF + fc * 128: ic * FF + fc * 128 + 128],
                                     xbT[:, ic * (NEL * CAP) + base: ic * (NEL * CAP) + base + 512],
                                     start=(ic == 0), stop=(ic == 7))
                    ch["pe"].tag(i)
                block.tensor(gmm)
                pe_now = ch["pe"].n
                def gsl(s, pe_now=pe_now):
                    s.wait_ge(ch["pe"].sem, pe_now)
                    ch["act"].tag(s.activation(out=sqs[:], in_=PS1[:, 0:512], func=AF.Silu))
                block.scalar(gsl)
                def gmu(v, fc=fc, now=ch["act"].n):
                    v.wait_ge(ch["act"].sem, now)
                    ch["dve"].tag(v.tensor_mul(out=hbuf[:, fc * CAP: fc * CAP + 512],
                                               in0=PS2[:, 0:512], in1=sqs[:]))
                block.vector(gmu)
                def ggate(t, nd=ch["dve"].n):
                    t.wait_ge(ch["dve"].sem, nd)
                block.tensor(ggate)
            for sc in range(4):
                for dp in range(2):
                    def dmm(t, e=e, par=par, sc=sc, dp=dp):
                        for ic in range(6):
                            i = t.matmul(PS1[0:128, dp * 512:dp * 512 + 512],
                                         hbuf[:, ic * CAP + sc * 128: ic * CAP + sc * 128 + 128],
                                         wdB[par][:, ic * D + dp * 512: ic * D + dp * 512 + 512],
                                         start=(ic == 0), stop=(ic == 5))
                        ch["pe"].tag(i)
                    block.tensor(dmm)
                pe_now = ch["pe"].n
                def yevf(v, pe_now=pe_now, snow=len(ysttags)):
                    v.wait_ge(ch["pe"].sem, pe_now)
                    if snow >= 1:
                        v.wait_ge(ch["sst"].sem, ysttags[snow - 1])
                    ch["dve"].tag(v.tensor_copy(out=yev[:], in_=PS1[:, 0:1024]))
                block.vector(yevf)
                def yst(s, e=e, sc=sc, now=ch["dve"].n):
                    s.wait_ge(ch["dve"].sem, now)
                    ch["sst"].dma(s.dma_start(out=o_y[e * CAP + sc * 128: e * CAP + sc * 128 + 128, :],
                                              in_=yev[:]))
                block.sync(yst)
                ysttags.append(ch["sst"].n)
                def dgate(t, nd=ch["dve"].n):
                    t.wait_ge(ch["dve"].sem, nd)
                block.tensor(dgate)
            wfree.append(ch["pe"].n)

        def fin(s):
            s.wait_ge(ch["sst"].sem, ch["sst"].n)
        block.sync(fin)
    return nc


def build_l4():
    nc = bass.Bass("TRN2", target_bir_lowering=False)
    def din(name, shape, dt=F32):
        return nc.dram_tensor(name, list(shape), dt, kind="ExternalInput").ap()
    yfull = din("yfull", (NE * CAP, D), BF16)
    slot8 = din("slot8", (128, TT * 8))
    w8v = din("w8v", (128, TT * 8))
    xsht = din("xsht", (T, D))
    o_out = nc.dram_tensor("out", [T, D], F32, kind="ExternalOutput").ap()

    with ExitStack() as ctx:
        E = ctx.enter_context
        ch = {k: Ch(ctx, nc, k) for k in ("sld", "sx", "gg", "gg2", "pe", "dve", "act", "sst")}
        ggch = [ch["gg"], ch["gg2"]]
        ss8 = E(nc.sbuf_tensor("ss8", [128, TT * 8], F32))
        sw8 = E(nc.sbuf_tensor("sw8", [128, TT * 8], F32))
        sidx = E(nc.sbuf_tensor("sidx", [128, TT * 8], I32))
        yg = [[E(nc.sbuf_tensor(f"yg{i}_{k}", [128, D], BF16)) for k in range(8)]
              for i in range(2)]
        sxsh = [E(nc.sbuf_tensor(f"sxsh{i}", [128, D], F32)) for i in range(2)]
        acc = E(nc.sbuf_tensor("acc", [128, D], F32))
        oval = [E(nc.sbuf_tensor(f"oval{i}", [128, D], F32)) for i in range(2)]
        block = E(nc.Block())

        def lds(s):
            ch["sld"].dma(s.dma_start(out=ss8[:], in_=slot8))
            ch["sld"].dma(s.dma_start(out=sw8[:], in_=w8v))
        block.sync(lds)
        sld0 = ch["sld"].n
        def cvt(v):
            v.wait_ge(ch["sld"].sem, sld0)
            ch["dve"].tag(v.tensor_copy(out=sidx[:], in_=ss8[:]))
        block.vector(cvt)
        dve_idx = ch["dve"].n

        gtags = []
        xtags = []
        dtags = []
        stags = []
        for tt in range(TT):
            par = tt % 2
            def gat(g, tt=tt, par=par):
                if tt == 0:
                    g.wait_ge(ch["dve"].sem, dve_idx)
                elif tt >= 2:
                    g.wait_ge(ch["dve"].sem, dtags[tt - 2][0])
                for k in range(8):
                    ggch[par].dma(g.indirect_dma_start(
                        out=yg[par][k][:], out_offset=None, in_=yfull,
                        in_offset=bass.IndirectOffsetOnAxis(sidx[:, tt * 8 + k: tt * 8 + k + 1], 0)))
            block.gpsimd(gat)
            gtags.append(ggch[par].n)
            def xld(s, tt=tt, par=par):
                if tt >= 2:
                    s.wait_ge(ch["dve"].sem, dtags[tt - 2][1])
                ch["sx"].dma(s.dma_start(out=sxsh[par][:], in_=xsht[tt * 128:(tt + 1) * 128, :]))
            block.sync(xld)
            xtags.append(ch["sx"].n)
            def comb(v, tt=tt, par=par, gnow=gtags[tt], xnow=xtags[tt], dprev=ch["dve"].n):
                v.wait_ge(ggch[par].sem, gnow)
                v.wait_ge(ch["dve"].sem, dprev)
                ch["dve"].tag(v.tensor_scalar(out=acc[:], in0=yg[par][0][:],
                                              scalar1=sw8[:, tt * 8: tt * 8 + 1],
                                              scalar2=None, op0=ALU.mult))
                for k in range(1, 8):
                    v.wait_ge(ch["dve"].sem, ch["dve"].n)
                    ch["dve"].tag(v.scalar_tensor_tensor(out=acc[:], in0=yg[par][k][:],
                                                         scalar=sw8[:, tt * 8 + k: tt * 8 + k + 1],
                                                         in1=acc[:], op0=ALU.mult, op1=ALU.add))
                v.wait_ge(ch["dve"].sem, ch["dve"].n)
                v.wait_ge(ch["sx"].sem, xnow)
                if tt >= 1:
                    v.wait_ge(ch["sst"].sem, stags[tt - 1])
                yg_free = ch["dve"].tag(v.tensor_add(out=oval[par][:], in0=acc[:], in1=sxsh[par][:]))
                dtags.append((yg_free, yg_free))
            block.vector(comb)
            def ost(s, tt=tt, par=par, now=ch["dve"].n):
                s.wait_ge(ch["dve"].sem, now)
                ch["sst"].dma(s.dma_start(out=o_out[tt * 128:(tt + 1) * 128, :], in_=oval[par][:]))
            block.sync(ost)
            stags.append(ch["sst"].n)

        def fin(s):
            s.wait_ge(ch["sst"].sem, ch["sst"].n)
        block.sync(fin)
    return nc


def l2_inputs(r1c, srowA, c, consts):
    cmask = np.zeros((8, 1), np.float32); cmask[:c, 0] = 1.0
    tokid1 = (c * T + 1 + np.arange(128, dtype=np.float32)[:, None]
              + 128.0 * np.arange(TT, dtype=np.float32)[None, :])
    e1024 = np.tile((np.arange(NE, dtype=np.float32) * 1024.0)[None, :], (128, TT))
    e8 = np.tile((np.arange(NE, dtype=np.float32) * 8.0)[None, :], (128, TT))
    trashw = np.tile((32768.0 + np.arange(TT * 8, dtype=np.float32))[None, :], (128, 1))
    return {"cum": r1c["cum"], "wnorm": r1c["wnorm"], "srowA": srowA,
            "cmask": cmask, "tokid1": np.ascontiguousarray(tokid1),
            "E1024W": np.ascontiguousarray(e1024), "E8W": np.ascontiguousarray(e8),
            "TRASHW": np.ascontiguousarray(trashw), "ONES": consts["ONES"]}


def kernel(**inputs):
    x = inputs["L0_attn_x"]
    consts = host_consts()
    consts["E1024"] = np.tile((np.arange(NE, dtype=np.float32) * 1024.0)[None, :], (128, 1))
    consts["E8"] = np.tile((np.arange(NE, dtype=np.float32) * 8.0)[None, :], (128, 1))
    cores = list(range(NC))

    nc1 = build_l1()
    in1 = [l1_inputs(np.ascontiguousarray(x[c]), inputs, consts) for c in cores]
    r1 = run_bass_kernel_spmd(nc1, in1, core_ids=cores).results

    srowA = np.ascontiguousarray(np.stack([r1[c]["srow"][0] for c in cores]))
    nc2 = build_l2()
    in2 = [l2_inputs(r1[c], srowA, c, consts) for c in cores]
    r2 = run_bass_kernel_spmd(nc2, in2, core_ids=cores).results

    maps = np.ascontiguousarray(np.stack([r2[c]["mapc"][:, 0] for c in cores]))
    ln2r = np.ascontiguousarray(
        np.concatenate([r1[c]["ln2b"].T for c in cores], axis=0))
    nc3 = build_l3()
    in3 = []
    for c in cores:
        in3.append({
            "maps": maps, "ln2r": ln2r,
            "myrows": np.arange(c * 32, c * 32 + 32, dtype=np.int32).reshape(32, 1),
            "wg": np.ascontiguousarray(inputs["L0_ffn_w_gate"][c * NEL:(c + 1) * NEL]),
            "wu": np.ascontiguousarray(inputs["L0_ffn_w_up"][c * NEL:(c + 1) * NEL]),
            "wd": np.ascontiguousarray(inputs["L0_ffn_w_down"][c * NEL:(c + 1) * NEL]),
            "IDENT": consts["IDENT"],
        })
    r3 = run_bass_kernel_spmd(nc3, in3, core_ids=cores).results

    yfull = np.ascontiguousarray(np.concatenate([r3[c]["y"] for c in cores], axis=0))
    nc4 = build_l4()
    in4 = []
    for c in cores:
        in4.append({
            "yfull": yfull, "slot8": r2[c]["slot8"], "w8v": r2[c]["w8v"],
            "xsht": np.ascontiguousarray(r1[c]["xsh"].T),
        })
    r4 = run_bass_kernel_spmd(nc4, in4, core_ids=cores).results
    return np.stack([r4[c]["out"] for c in cores]).astype(np.float32)


# version 07
# speedup vs baseline: 1.2062x; 1.0433x over previous
import numpy as np
import ml_dtypes
from contextlib import ExitStack

import concourse.bass as bass
import concourse.mybir as mybir
from concourse.bass_utils import run_bass_kernel_spmd

F32 = mybir.dt.float32
BF16 = mybir.dt.bfloat16
I32 = mybir.dt.int32
AF = mybir.ActivationFunctionType
ALU = mybir.AluOpType
AX = mybir.AxisListType

T = 2048
D = 1024
NH = 16
DH = 64
DR = 32
KVR = 256
QR = 512
NE = 64
NEL = 8
CAP = 512
FF = 768
SF = 768
NC = 8
TT = T // 128
NEG = -1.0e30
SCALE = 1.0 / float(np.sqrt(DH + DR))


class Ch:
    def __init__(self, ctx, nc, name):
        self.sem = ctx.enter_context(nc.semaphore(name))
        self.n = 0

    def tag(self, inst, k=1):
        inst.then_inc(self.sem, k)
        self.n += k
        return self.n

    def dma(self, inst):
        return self.tag(inst, 16)


def wait_all(eng, chans):
    for c in chans:
        if c.n > 0:
            eng.wait_ge(c.sem, c.n)



def rope_tables():
    inv = 1.0 / 10000.0 ** (np.arange(0, DR, 2, dtype=np.float32) / DR)
    t = np.arange(T, dtype=np.float32)
    emb = np.concatenate([np.outer(t, inv)] * 2, -1)
    cosf = np.ascontiguousarray(np.cos(emb).astype(np.float32).T)
    sinf = np.ascontiguousarray(np.sin(emb).astype(np.float32).T)
    sinf_signed = sinf.copy()
    sinf_signed[0:16, :] *= -1.0
    return (np.ascontiguousarray(np.tile(cosf, (4, 1))),
            np.ascontiguousarray(np.tile(sinf_signed, (4, 1))))


def rope_shuffle_cols(w):
    perm = np.arange(w.shape[1]).reshape(-1, 2, 16)[:, ::-1, :].reshape(-1)
    return np.ascontiguousarray(w[:, perm])


def host_consts():
    c = {}
    c["IDENT"] = np.eye(128, dtype=np.float32)
    c["TRICUM"] = np.triu(np.ones((128, 128), np.float32))
    c["ONES"] = np.ones((128, 128), np.float32)
    kk, qq = np.meshgrid(np.arange(128), np.arange(128), indexing="ij")
    c["MASKTRI"] = np.where(kk > qq, np.float32(NEG), np.float32(0.0)).astype(np.float32)
    c["COS"], c["SIN"] = rope_tables()
    return c



def fm_pair(dst, src, width):
    n = src.shape[0] // 128
    return (dst.rearrange("p (c w) -> p c w", w=width),
            src.rearrange("(c p) w -> p c w", p=128))


def build_l1():
    nc = bass.Bass("TRN2", target_bir_lowering=False)

    def din(name, shape, dt=F32):
        return nc.dram_tensor(name, list(shape), dt, kind="ExternalInput").ap()

    xT = din("xT", (D, T))
    wkv = din("wkv", (D, KVR))
    gkv = din("gkv", (128, 2))
    wkdec = din("wkdec", (KVR, D))
    wvdec = din("wvdec", (KVR, D))
    wq = din("wq", (D, QR))
    wqdec = din("wqdec", (QR, D))
    wkrope = din("wkrope", (D, QR))
    wkropeS = din("wkropeS", (D, QR))
    wqrope = din("wqrope", (QR, QR))
    wqropeS = din("wqropeS", (QR, QR))
    wo = din("wo", (D, D))
    routerT = din("routerT", (D, NE))
    wsgu = din("wsgu", (D, 2 * SF))
    wsdn = din("wsdn", (SF, D))
    IDENT = din("IDENT", (128, 128))
    TRICUM = din("TRICUM", (128, 128))
    ONESC = din("ONES", (128, 128))
    MASKTRI = din("MASKTRI", (128, 128))
    COS = din("COS", (128, T))
    SIN = din("SIN", (128, T))

    o_xsh = nc.dram_tensor("xsh", [D, T], F32, kind="ExternalOutput").ap()
    o_ln2 = nc.dram_tensor("ln2b", [D, T], BF16, kind="ExternalOutput").ap()
    o_cum = nc.dram_tensor("cum", [128, TT * NE], F32, kind="ExternalOutput").ap()
    o_wn = nc.dram_tensor("wnorm", [128, TT * NE], F32, kind="ExternalOutput").ap()
    o_srow = nc.dram_tensor("srow", [1, NE], F32, kind="ExternalOutput").ap()

    x1d = nc.dram_tensor("x1d", [D, T], F32)

    ctx = ExitStack()
    with ctx:
        E = ctx.enter_context
        ch = {k: Ch(ctx, nc, k) for k in ("sld", "sr", "gc0", "gw3", "gw4", "gwo", "gsh",
                                          "gx1", "pe", "dve", "act", "sst")}

        lnb = E(nc.sbuf_tensor("lnb", [128, 8 * T], BF16))
        rows = E(nc.sbuf_tensor("rows", [128, T], F32))
        rowsb = E(nc.sbuf_tensor("rowsb", [128, T], BF16))
        bcast = E(nc.sbuf_tensor("bcast", [128, T], BF16))
        sq = E(nc.sbuf_tensor("sq", [128, T], BF16))
        identb = E(nc.sbuf_tensor("identb", [128, 128], F32))
        trib = E(nc.sbuf_tensor("trib", [128, 128], BF16))
        onesb = E(nc.sbuf_tensor("onesb", [128, 128], BF16))
        maskt = E(nc.sbuf_tensor("maskt", [128, 128], F32))
        gkvb = E(nc.sbuf_tensor("gkvb", [128, 2], F32))
        onesF = E(nc.sbuf_tensor("onesF", [128, 128], F32))
        PS1 = E(nc.psum_tensor("PS1", [128, T], F32))
        PS2 = E(nc.psum_tensor("PS2", [128, T], F32))
        block = E(nc.Block())


        st1 = ExitStack()
        xbf = st1.enter_context(nc.sbuf_tensor("xbf", [128, 8 * T], BF16))

        def p0g(g):
            ch["gc0"].dma(g.dma_start(out=xbf[:].rearrange("p (c w) -> p c w", c=8), in_=xT.rearrange("(c p) w -> p c w", p=128)))
            ch["gc0"].dma(g.dma_start(out=trib[:], in_=TRICUM))
            ch["gc0"].dma(g.dma_start(out=onesb[:], in_=ONESC))
            ch["gc0"].dma(g.dma_start(out=identb[:], in_=IDENT))
            ch["gc0"].dma(g.dma_start(out=maskt[:], in_=MASKTRI))
            ch["gc0"].dma(g.dma_start(out=gkvb[:], in_=gkv))
            ch["gc0"].dma(g.dma_start(out=onesF[:], in_=ONESC))
        block.gpsimd(p0g)
        gld_x = ch["gc0"].n
        sld_p0 = 0

        def emit_ln(src_bf, eps):
            a0 = ch["act"].n
            dve_at = ch["dve"].n
            def summ(t):
                t.wait_ge(ch["gc0"].sem, gld_x)
                if dve_at:
                    t.wait_ge(ch["dve"].sem, dve_at)
                for dc in range(8):
                    for ps in range(4):
                        i = t.matmul(PS1[0:1, ps * 512:ps * 512 + 512], onesb[:, 0:1],
                                     src_bf[:, dc * T + ps * 512: dc * T + ps * 512 + 512],
                                     start=(dc == 0), stop=(dc == 7))
                ch["pe"].tag(i)
            block.tensor(summ)
            pe_cons = []
            for dc in range(8):
                def sqf(s, dc=dc):
                    if dc == 0:
                        s.wait_ge(ch["gc0"].sem, gld_x)
                        if dve_at:
                            s.wait_ge(ch["dve"].sem, dve_at)
                    else:
                        s.wait_ge(ch["pe"].sem, pe_cons[dc - 1])
                    ch["act"].tag(s.activation(out=sq[:], in_=src_bf[:, dc * T:(dc + 1) * T],
                                               func=AF.Square))
                block.scalar(sqf)
                def mmf(t, dc=dc, anow=ch["act"].n):
                    t.wait_ge(ch["act"].sem, anow)
                    for ps in range(4):
                        i = t.matmul(PS2[0:1, ps * 512:ps * 512 + 512], onesb[:, 0:1],
                                     sq[:, ps * 512:ps * 512 + 512],
                                     start=(dc == 0), stop=(dc == 7))
                    ch["pe"].tag(i)
                block.tensor(mmf)
                pe_cons.append(ch["pe"].n)

            def rowmath(v):
                v.wait_ge(ch["pe"].sem, ch["pe"].n)
                ch["dve"].tag(v.tensor_scalar(out=rows[0:1, :], in0=PS1[0:1, 0:T],
                                              scalar1=1.0 / D, scalar2=None, op0=ALU.mult))
                ch["dve"].tag(v.tensor_scalar(out=rows[32:33, :], in0=PS1[0:1, 0:T],
                                              scalar1=-1.0 / D, scalar2=None, op0=ALU.mult))
                ch["dve"].tag(v.tensor_scalar(out=rows[64:65, :], in0=PS2[0:1, 0:T],
                                              scalar1=1.0 / D, scalar2=None, op0=ALU.mult))
                v.wait_ge(ch["dve"].sem, ch["dve"].n)
                ch["dve"].tag(v.tensor_mul(out=PS1[0:1, 0:T], in0=rows[0:1, :], in1=rows[0:1, :]))
                v.wait_ge(ch["dve"].sem, ch["dve"].n)
                ch["dve"].tag(v.tensor_sub(out=rows[64:65, :], in0=rows[64:65, :], in1=PS1[0:1, 0:T]))
                v.wait_ge(ch["dve"].sem, ch["dve"].n)
                ch["dve"].tag(v.tensor_scalar(out=rows[64:65, :], in0=rows[64:65, :],
                                              scalar1=eps, scalar2=None, op0=ALU.add))
            block.vector(rowmath)
            emit_rsqrt_row(64)

        def emit_rsqrt_row(prow):
            def f(s):
                s.wait_ge(ch["dve"].sem, ch["dve"].n)
                ch["act"].tag(s.activation(out=rows[prow:prow + 1, :],
                                           in_=rows[prow:prow + 1, :], func=AF.Ln))
                s.wait_ge(ch["act"].sem, ch["act"].n)
                ch["act"].tag(s.activation(out=rows[prow:prow + 1, :],
                                           in_=rows[prow:prow + 1, :], func=AF.Exp,
                                           scale=-0.5))
            block.scalar(f)

        def emit_bcast(prow, nrows=128):
            def cp(v):
                v.wait_ge(ch["dve"].sem, ch["dve"].n)
                wait_all(v, [ch["act"]])
                ch["dve"].tag(v.tensor_copy(out=rowsb[prow:prow + 1, :],
                                            in_=rows[prow:prow + 1, :]))
            block.vector(cp)
            def mm(t, now=ch["dve"].n):
                t.wait_ge(ch["gc0"].sem, gld_x)
                t.wait_ge(ch["dve"].sem, now)
                for ps in range(4):
                    i = t.matmul(PS2[0:nrows, ps * 512:ps * 512 + 512],
                                 onesb[prow:prow + 1, 0:nrows],
                                 rowsb[prow:prow + 1, ps * 512:ps * 512 + 512],
                                 start=True, stop=True)
                ch["pe"].tag(i)
            block.tensor(mm)
            def ev(v):
                v.wait_ge(ch["pe"].sem, ch["pe"].n)
                ch["dve"].tag(v.tensor_copy(out=bcast[0:nrows, :], in_=PS2[0:nrows, 0:T]))
            block.vector(ev)

        def emit_ln_apply(src_bf, dst_bf):
            emit_bcast(32)
            def add(v):
                v.wait_ge(ch["dve"].sem, ch["dve"].n)
                for dc in range(8):
                    i = v.tensor_add(out=dst_bf[:, dc * T:(dc + 1) * T],
                                     in0=src_bf[:, dc * T:(dc + 1) * T], in1=bcast[:])
                ch["dve"].tag(i)
            block.vector(add)
            emit_bcast(64)
            def mul(v):
                v.wait_ge(ch["dve"].sem, ch["dve"].n)
                for dc in range(8):
                    i = v.tensor_mul(out=dst_bf[:, dc * T:(dc + 1) * T],
                                     in0=dst_bf[:, dc * T:(dc + 1) * T], in1=bcast[:])
                ch["dve"].tag(i)
            block.vector(mul)

        emit_ln(xbf, 1e-5)
        emit_ln_apply(xbf, lnb)
        st1.close()

        st2 = ExitStack()
        E2 = st2.enter_context
        qc = E2(nc.sbuf_tensor("qc", [128, 4 * T], BF16))
        kvn = E2(nc.sbuf_tensor("kvn", [128, 2 * T], BF16))
        krp = E2(nc.sbuf_tensor("krp", [128, 4 * T], BF16))
        qrp = E2(nc.sbuf_tensor("qrp", [128, 4 * T], BF16))
        vaug = E2(nc.sbuf_tensor("vaug", [128, TT * NH * 65], BF16))

        st2t = ExitStack()
        E2t = st2t.enter_context
        wkvB = E2t(nc.sbuf_tensor("wkvB", [128, 8 * KVR], BF16))
        wqB = E2t(nc.sbuf_tensor("wqB", [128, 8 * QR], BF16))
        wkrB = E2t(nc.sbuf_tensor("wkrB", [128, 8 * QR], BF16))
        wkrSB = E2t(nc.sbuf_tensor("wkrSB", [128, 8 * QR], BF16))
        wqrB = E2t(nc.sbuf_tensor("wqrB", [128, 4 * QR], BF16))
        wqrSB = E2t(nc.sbuf_tensor("wqrSB", [128, 4 * QR], BF16))
        wvdB = E2t(nc.sbuf_tensor("wvdB", [128, 2 * D], BF16))
        kvc32 = E2t(nc.sbuf_tensor("kvc32", [128, 2 * T], F32))
        cosb = E2t(nc.sbuf_tensor("cosb", [128, T], BF16))
        sinb = E2t(nc.sbuf_tensor("sinb", [128, T], BF16))

        def p3g(g):
            def cast(dst, src, pat):
                ch["gw3"].dma(g.dma_start(out=dst.rearrange("p (c w) -> p c w", c=src.shape[0] // 128), in_=src.rearrange("(c p) w -> p c w", p=128)))
            ch["gw3"].dma(g.dma_start(out=cosb[:], in_=COS))
            ch["gw3"].dma(g.dma_start(out=sinb[:], in_=SIN))
            cast(wkvB[:], wkv, "(dc p) r -> p (dc r)")
            cast(wqB[:], wq, "(dc p) r -> p (dc r)")
            cast(wkrB[:], wkrope, "(dc p) r -> p (dc r)")
            cast(wkrSB[:], wkropeS, "(dc p) r -> p (dc r)")
            cast(wqrB[:], wqrope, "(qc p) r -> p (qc r)")
            cast(wqrSB[:], wqropeS, "(qc p) r -> p (qc r)")
            cast(wvdB[:], wvdec, "(kc p) d -> p (kc d)")
        block.gpsimd(p3g)
        gld_w = ch["gw3"].n

        def proj(dst_bf, wB, n_in, n_out, src_bf, dst_f32=None, wSB=None, wait_ld=None):
            for oc in range(n_out):
                for ps in range(4):
                    def mm(t, oc=oc, ps=ps):
                        if oc == 0 and ps == 0:
                            t.wait_ge(ch["gw3"].sem, wait_ld if wait_ld else ch["gw3"].n)
                            wait_all(t, [ch["dve"]])
                        for ic in range(n_in):
                            i = t.matmul(PS1[:, ps * 512:ps * 512 + 512],
                                         wB[:, ic * n_out * 128 + oc * 128: ic * n_out * 128 + oc * 128 + 128],
                                         src_bf[:, ic * T + ps * 512: ic * T + ps * 512 + 512],
                                         start=(ic == 0), stop=(ic == n_in - 1))
                        if wSB is not None:
                            for ic in range(n_in):
                                i = t.matmul(PS2[:, ps * 512:ps * 512 + 512],
                                             wSB[:, ic * n_out * 128 + oc * 128: ic * n_out * 128 + oc * 128 + 128],
                                             src_bf[:, ic * T + ps * 512: ic * T + ps * 512 + 512],
                                             start=(ic == 0), stop=(ic == n_in - 1))
                        ch["pe"].tag(i)
                    block.tensor(mm)
                    pe_now = ch["pe"].n

                    def ev(v, oc=oc, ps=ps, pe_now=pe_now):
                        v.wait_ge(ch["pe"].sem, pe_now)
                        dsl = slice(oc * T + ps * 512, oc * T + ps * 512 + 512)
                        ssl = slice(ps * 512, ps * 512 + 512)
                        if wSB is None:
                            if dst_f32 is not None:
                                ch["dve"].tag(v.tensor_copy(out=dst_f32[:, dsl], in_=PS1[:, ssl]))
                            ch["dve"].tag(v.tensor_copy(out=dst_bf[:, dsl], in_=PS1[:, ssl]))
                        else:
                            ch["dve"].tag(v.tensor_mul(out=sq[:, 0:512], in0=PS1[:, ssl], in1=cosb[:, ssl]))
                            ch["dve"].tag(v.tensor_mul(out=dst_bf[:, dsl], in0=PS2[:, ssl], in1=sinb[:, ssl]))
                            v.wait_ge(ch["dve"].sem, ch["dve"].n)
                            ch["dve"].tag(v.tensor_add(out=dst_bf[:, dsl], in0=dst_bf[:, dsl], in1=sq[:, 0:512]))
                    block.vector(ev)

                    def gate(t, now=ch["dve"].n):
                        t.wait_ge(ch["dve"].sem, now)
                    block.tensor(gate)

        proj(kvn, wkvB, 8, 2, lnb, dst_f32=kvc32, wait_ld=gld_w)
        kv_pe = []
        kv_dve = ch["dve"].n
        for c2 in range(2):
            def kvsq(s, c2=c2):
                if c2 == 0:
                    s.wait_ge(ch["dve"].sem, kv_dve)
                else:
                    s.wait_ge(ch["pe"].sem, kv_pe[c2 - 1])
                ch["act"].tag(s.activation(out=sq[:], in_=kvc32[:, c2 * T:(c2 + 1) * T],
                                           func=AF.Square))
            block.scalar(kvsq)
            def kvmm(t, c2=c2, anow=ch["act"].n):
                t.wait_ge(ch["act"].sem, anow)
                for ps in range(4):
                    i = t.matmul(PS1[0:1, ps * 512:ps * 512 + 512], onesb[:, 0:1],
                                 sq[:, ps * 512:ps * 512 + 512],
                                 start=(c2 == 0), stop=(c2 == 1))
                ch["pe"].tag(i)
            block.tensor(kvmm)
            kv_pe.append(ch["pe"].n)
        def kvrow(v):
            v.wait_ge(ch["pe"].sem, ch["pe"].n)
            ch["dve"].tag(v.tensor_scalar(out=rows[64:65, :], in0=PS1[0:1, 0:T],
                                          scalar1=1.0 / KVR, scalar2=None, op0=ALU.mult))
            v.wait_ge(ch["dve"].sem, ch["dve"].n)
            ch["dve"].tag(v.tensor_scalar(out=rows[64:65, :], in0=rows[64:65, :],
                                          scalar1=1e-6, scalar2=None, op0=ALU.add))
        block.vector(kvrow)
        emit_rsqrt_row(64)
        emit_bcast(64)
        def kvapply(v):
            v.wait_ge(ch["gc0"].sem, gld_x)
            v.wait_ge(ch["dve"].sem, ch["dve"].n)
            for c2 in range(2):
                ch["dve"].tag(v.tensor_mul(out=kvc32[:, c2 * T:(c2 + 1) * T],
                                           in0=kvc32[:, c2 * T:(c2 + 1) * T], in1=bcast[:]))
            v.wait_ge(ch["dve"].sem, ch["dve"].n)
            for c2 in range(2):
                ch["dve"].tag(v.tensor_scalar(out=kvn[:, c2 * T:(c2 + 1) * T],
                                              in0=kvc32[:, c2 * T:(c2 + 1) * T],
                                              scalar1=gkvb[:, c2:c2 + 1],
                                              scalar2=None, op0=ALU.mult))
        block.vector(kvapply)

        proj(qc, wqB, 8, 4, lnb)
        proj(krp, wkrB, 8, 4, lnb, wSB=wkrSB)
        proj(qrp, wqrB, 4, 4, qc, wSB=wqrSB)

        def vones(v):
            ch["dve"].tag(v.memset(vaug[:].rearrange("p (a b) -> p a b", b=65)[:, :, 64:65], 1.0))
        block.vector(vones)
        for kt in range(TT):
            def vmm(t, kt=kt):
                if kt == 0:
                    wait_all(t, [ch["dve"], ch["gw3"]])
                for half in range(2):
                    for c2 in range(2):
                        i = t.matmul(PS1[:, half * 512:half * 512 + 512],
                                     kvn[:, c2 * T + kt * 128: c2 * T + kt * 128 + 128],
                                     wvdB[:, c2 * D + half * 512: c2 * D + half * 512 + 512],
                                     start=(c2 == 0), stop=(c2 == 1))
                ch["pe"].tag(i)
            block.tensor(vmm)
            pe_now = ch["pe"].n
            def vev(v, kt=kt, pe_now=pe_now):
                v.wait_ge(ch["pe"].sem, pe_now)
                ch["dve"].tag(v.tensor_copy(
                    out=vaug[:, kt * NH * 65:(kt + 1) * NH * 65].rearrange("p (h c) -> p h c", c=65)[:, :, 0:64],
                    in_=PS1[:, 0:1024].rearrange("p (h c) -> p h c", c=64)))
            block.vector(vev)
            def vgate(t, now=ch["dve"].n):
                t.wait_ge(ch["dve"].sem, now)
            block.tensor(vgate)

        st2t.close()

        st3a = ExitStack()
        abuf = st3a.enter_context(nc.sbuf_tensor("abuf", [128, 8 * T], BF16))
        st3b = ExitStack()
        E3 = st3b.enter_context
        wkdB = E3(nc.sbuf_tensor("wkdB", [128, 2 * D], BF16))
        wqdB = E3(nc.sbuf_tensor("wqdB", [128, 4 * D], BF16))
        pt = E3(nc.sbuf_tensor("pt", [128, 2 * 1024], BF16))
        qh = E3(nc.sbuf_tensor("qh", [128, T], BF16))
        kh = E3(nc.sbuf_tensor("kh", [128, T], BF16))

        def p4g(g):
            wait_all(g, [ch["pe"], ch["dve"], ch["act"], ch["gw3"]])
            def cast(dst, src, pat):
                ch["gw4"].dma(g.dma_start(out=dst.rearrange("p (c w) -> p c w", c=src.shape[0] // 128), in_=src.rearrange("(c p) w -> p c w", p=128)))
            cast(wkdB[:], wkdec, "(kc p) d -> p (kc d)")
            cast(wqdB[:], wqdec, "(qc p) d -> p (qc d)")
        block.gpsimd(p4g)
        gld_hw = ch["gw4"].n

        nrm_done = None
        for h in range(NH):
            pe_qbase = ch["pe"].n
            def qmm(t, h=h, nrm_prev=nrm_done):
                if h == 0:
                    t.wait_ge(ch["gw4"].sem, gld_hw)
                    wait_all(t, [ch["dve"]])
                else:
                    t.wait_ge(ch["dve"].sem, nrm_prev)
                for ps in range(4):
                    for ic in range(4):
                        t.matmul(PS2[0:64, ps * 512:ps * 512 + 512],
                                 wqdB[:, ic * D + h * 64: ic * D + h * 64 + 64],
                                 qc[:, ic * T + ps * 512: ic * T + ps * 512 + 512],
                                 start=(ic == 0), stop=(ic == 3))
                    for ic in range(2):
                        i = t.matmul(PS1[0:64, ps * 512:ps * 512 + 512],
                                     wkdB[:, ic * D + h * 64: ic * D + h * 64 + 64],
                                     kvn[:, ic * T + ps * 512: ic * T + ps * 512 + 512],
                                     start=(ic == 0), stop=(ic == 1))
                    ch["pe"].tag(i)
            block.tensor(qmm)
            assert ch["pe"].n == pe_qbase + 4
            def qevall(v, h=h, pe_qbase=pe_qbase):
                for ps in range(4):
                    v.wait_ge(ch["pe"].sem, pe_qbase + ps + 1)
                    ssl = slice(ps * 512, ps * 512 + 512)
                    ch["dve"].tag(v.tensor_copy(out=qh[0:64, ssl], in_=PS2[0:64, ssl]))
                    ch["dve"].tag(v.tensor_copy(out=kh[0:64, ssl], in_=PS1[0:64, ssl]))
                rb = (h % 4) * 32
                dc = h // 4
                ch["dve"].tag(v.tensor_copy(out=qh[64:96, :], in_=qrp[rb:rb + 32, dc * T:(dc + 1) * T]))
                ch["dve"].tag(v.tensor_copy(out=kh[64:96, :], in_=krp[rb:rb + 32, dc * T:(dc + 1) * T]))
            block.vector(qevall)
            dve_rope = ch["dve"].n

            secs = []
            for kt in range(TT):
                q0 = kt * 128
                secs.append((kt, q0, min(q0 + 1024, T)))
                if q0 + 1024 < T:
                    secs.append((kt, q0 + 1024, T))
            ns = len(secs)
            act_base = ch["act"].n
            act_sec = [act_base + si + 1 for si in range(ns)]
            pe_pv = [None] * ns
            exp_list = []
            diag_sis = [si for si, (kt, qlo, qhi) in enumerate(secs) if qlo == kt * 128]
            dve_mbase = ch["dve"].n
            dve_m_of = {si: dve_mbase + i + 1 for i, si in enumerate(diag_sis)}

            def pv_body(t, si):
                kt, qlo, qhi = secs[si]
                par = si % 2
                t.wait_ge(ch["act"].sem, act_sec[si])
                b0 = qlo
                while b0 < qhi:
                    b1 = min((b0 // 512 + 1) * 512, qhi)
                    lastkt = min(TT - 1, (b1 - 1) // 128)
                    i = t.matmul(PS2[0:65, b0:b1],
                                 vaug[:, kt * NH * 65 + h * 65: kt * NH * 65 + h * 65 + 65],
                                 pt[:, par * 1024 + (b0 - qlo): par * 1024 + (b1 - qlo)],
                                 start=(kt == 0), stop=(kt == lastkt))
                    b0 = b1
                ch["pe"].tag(i)
            pe_scs = {}
            for si, (kt, qlo, qhi) in enumerate(secs):
                par = si % 2
                ln = qhi - qlo
                pe_before = ch["pe"].n
                def tfrag(t, h=h, si=si, kt=kt, qlo=qlo, qhi=qhi, par=par, ln=ln):
                    if si >= 2:
                        t.wait_ge(ch["act"].sem, act_sec[si - 2])
                    if si == 0:
                        t.wait_ge(ch["dve"].sem, dve_rope)
                    nps = (ln + 511) // 512
                    for ps in range(nps):
                        w = min(512, ln - ps * 512)
                        i = t.matmul(PS1[:, par * 1024 + ps * 512: par * 1024 + ps * 512 + w],
                                     kh[0:96, kt * 128:kt * 128 + 128],
                                     qh[0:96, qlo + ps * 512: qlo + ps * 512 + w],
                                     start=True, stop=True)
                    ch["pe"].tag(i)
                    if si >= 1:
                        pv_body(t, si - 1)
                block.tensor(tfrag)
                pe_scs[si] = pe_before + 1
                if si >= 1:
                    pe_pv[si - 1] = pe_before + 2
                    assert ch["pe"].n == pe_before + 2
                exp_list.append((si, par, ln, pe_scs[si], dve_m_of.get(si),
                                 qlo == kt * 128))
            def lastpv(t):
                pv_body(t, ns - 1)
            block.tensor(lastpv)
            pe_pv[ns - 1] = ch["pe"].n

            def mskall(v, pe_scs=dict(pe_scs)):
                v.wait_ge(ch["gc0"].sem, gld_x)
                for si in diag_sis:
                    v.wait_ge(ch["pe"].sem, pe_scs[si])
                    par = si % 2
                    ch["dve"].tag(v.tensor_add(out=PS1[:, par * 1024: par * 1024 + 128],
                                               in0=PS1[:, par * 1024: par * 1024 + 128],
                                               in1=maskt[:]))
            block.vector(mskall)
            assert ch["dve"].n == dve_mbase + len(diag_sis)

            def expall(s, exp_list=tuple(exp_list), pe_pv=tuple(pe_pv)):
                for si, par, ln, pe_sc, dve_m, diag in exp_list:
                    if si >= 2 and pe_pv[si - 2] is not None:
                        s.wait_ge(ch["pe"].sem, pe_pv[si - 2])
                    if diag:
                        s.wait_ge(ch["dve"].sem, dve_m)
                    else:
                        s.wait_ge(ch["pe"].sem, pe_sc)
                    ch["act"].tag(s.activation(out=pt[:, par * 1024: par * 1024 + ln],
                                               in_=PS1[:, par * 1024: par * 1024 + ln],
                                               func=AF.Exp, scale=SCALE))
            block.scalar(expall)
            assert ch["act"].n == act_sec[-1], (ch["act"].n, act_sec[-1])

            def nrm_row(s, pv_done=pe_pv[ns - 1]):
                s.wait_ge(ch["pe"].sem, pv_done)
                ch["act"].tag(s.activation(out=rows[0:1, :], in_=PS2[64:65, 0:T], func=AF.Ln))
                s.wait_ge(ch["act"].sem, ch["act"].n)
                ch["act"].tag(s.activation(out=rowsb[0:1, :], in_=rows[0:1, :],
                                           func=AF.Exp, scale=-1.0))
            block.scalar(nrm_row)
            def nrm_bc(t):
                t.wait_ge(ch["act"].sem, ch["act"].n)
                t.wait_ge(ch["gc0"].sem, gld_x)
                for ps in range(4):
                    i = t.matmul(PS1[0:64, ps * 512:ps * 512 + 512], onesb[0:1, 0:64],
                                 rowsb[0:1, ps * 512:ps * 512 + 512], start=True, stop=True)
                ch["pe"].tag(i)
            block.tensor(nrm_bc)
            def nrm_ev(v, h=h):
                v.wait_ge(ch["pe"].sem, ch["pe"].n)
                ch["dve"].tag(v.tensor_copy(out=bcast[0:64, :], in_=PS1[0:64, 0:T]))
                v.wait_ge(ch["dve"].sem, ch["dve"].n)
                dc = h // 2
                rb = (h % 2) * 64
                ch["dve"].tag(v.tensor_mul(out=abuf[rb:rb + 64, dc * T:(dc + 1) * T],
                                           in0=PS2[0:64, 0:T], in1=bcast[0:64, :]))
            block.vector(nrm_ev)
            nrm_done = ch["dve"].n

        st3b.close()

        st4 = ExitStack()
        woB = st4.enter_context(nc.sbuf_tensor("woB", [128, 8 * D], BF16))
        xstr = st4.enter_context(nc.sbuf_tensor("xstr", [128, T], F32))
        x1f = st4.enter_context(nc.sbuf_tensor("x1f", [128, T], F32))

        def p5g(g):
            wait_all(g, [ch["pe"], ch["dve"], ch["act"], ch["gw4"]])
            ch["gwo"].dma(g.dma_start(out=woB[:].rearrange("p (c w) -> p c w", c=8),
                                      in_=wo.rearrange("(c p) w -> p c w", p=128)))
        block.gpsimd(p5g)
        gld_wo = ch["gwo"].n

        def woph(oc):
            def mm(t, oc=oc):
                if oc == 0:
                    wait_all(t, [ch["dve"]])
                    t.wait_ge(ch["gwo"].sem, gld_wo)
                for ps in range(4):
                    for ic in range(8):
                        i = t.matmul(PS1[:, ps * 512:ps * 512 + 512],
                                     woB[:, ic * D + oc * 128: ic * D + oc * 128 + 128],
                                     abuf[:, ic * T + ps * 512: ic * T + ps * 512 + 512],
                                     start=(ic == 0), stop=(ic == 7))
                ch["pe"].tag(i)
            block.tensor(mm)
            pe_now = ch["pe"].n
            def ld(s, oc=oc):
                if oc == 0:
                    wait_all(s, [ch["pe"], ch["dve"], ch["act"]])
                else:
                    s.wait_ge(ch["dve"].sem, x1tag[oc - 1][0])
                ch["sld"].dma(s.dma_start(out=xstr[:], in_=xT[oc * 128:(oc + 1) * 128, :]))
            block.sync(ld)
            sld_now = ch["sld"].n
            def ev(v, oc=oc, pe_now=pe_now, sld_now=sld_now):
                v.wait_ge(ch["pe"].sem, pe_now)
                v.wait_ge(ch["sld"].sem, sld_now)
                if oc >= 1:
                    v.wait_ge(ch["gx1"].sem, x1tag[oc - 1][1])
                ch["dve"].tag(v.tensor_add(out=x1f[:], in0=PS1[:, 0:T], in1=xstr[:]))
                v.wait_ge(ch["dve"].sem, ch["dve"].n)
                ch["dve"].tag(v.tensor_copy(out=lnb[:, oc * T:(oc + 1) * T], in_=x1f[:]))
            block.vector(ev)
            def st(g, oc=oc, now=ch["dve"].n):
                g.wait_ge(ch["dve"].sem, now)
                ch["gx1"].dma(g.dma_start(out=x1d.ap()[oc * 128:(oc + 1) * 128, :],
                                          in_=x1f[:]))
                x1tag.append((ch["dve"].n, ch["gx1"].n))
            block.gpsimd(st)
            def gate(t, now=ch["dve"].n):
                t.wait_ge(ch["dve"].sem, now)
            block.tensor(gate)
        x1tag = []
        for oc in range(8):
            woph(oc)
        st4.close()
        st3a.close()
        st2.close()

        emit_ln(lnb, 1e-5)
        st5 = ExitStack()
        E5 = st5.enter_context
        xs2 = E5(nc.sbuf_tensor("xs2", [128, T], F32))
        ln2f = E5(nc.sbuf_tensor("ln2f", [128, T], F32))
        routB = E5(nc.sbuf_tensor("routB", [128, 8 * NE], F32))
        nbc = E5(nc.sbuf_tensor("nbc", [128, T], F32))
        rbc = E5(nc.sbuf_tensor("rbc", [128, T], F32))
        logitsT = E5(nc.sbuf_tensor("logitsT", [128, TT * NE], F32))
        lgf = E5(nc.sbuf_tensor("lgf", [128, T], F32))

        def rload(s):
            wait_all(s, [ch["pe"], ch["dve"], ch["act"], ch["gwo"], ch["gx1"]])
            ch["sr"].dma(s.dma_start(out=routB[:].rearrange("p (c w) -> p c w", c=8), in_=routerT.rearrange("(c p) w -> p c w", p=128)))
        block.sync(rload)
        emit_bcast(32)
        def cpn(v):
            v.wait_ge(ch["dve"].sem, ch["dve"].n)
            ch["dve"].tag(v.tensor_copy(out=nbc[:], in_=bcast[:]))
        block.vector(cpn)
        emit_bcast(64)
        def cpr(v):
            v.wait_ge(ch["dve"].sem, ch["dve"].n)
            ch["dve"].tag(v.tensor_copy(out=rbc[:], in_=bcast[:]))
        block.vector(cpr)

        l2tags = []
        for dc in range(8):
            def l2ld(s, dc=dc):
                if dc == 0:
                    wait_all(s, [ch["dve"]])
                else:
                    s.wait_ge(ch["dve"].sem, l2tags[dc - 1])
                ch["sld"].dma(s.dma_start(out=xs2[:], in_=x1d.ap()[dc * 128:(dc + 1) * 128, :]))
            block.sync(l2ld)
            sld_now = ch["sld"].n
            def l2f(v, dc=dc, sld_now=sld_now):
                v.wait_ge(ch["sld"].sem, sld_now)
                if dc > 0:
                    v.wait_ge(ch["pe"].sem, ch["pe"].n)
                ch["dve"].tag(v.tensor_add(out=ln2f[:], in0=xs2[:], in1=nbc[:]))
                l2tags.append(ch["dve"].n)
                v.wait_ge(ch["dve"].sem, ch["dve"].n)
                ch["dve"].tag(v.tensor_mul(out=ln2f[:], in0=ln2f[:], in1=rbc[:]))
            block.vector(l2f)
            def l2mm(t, dc=dc, now=ch["dve"].n):
                t.wait_ge(ch["dve"].sem, now)
                if dc == 0:
                    t.wait_ge(ch["sr"].sem, 16)
                for ps in range(4):
                    i = t.matmul(PS2[0:NE, ps * 512:ps * 512 + 512],
                                 routB[:, dc * NE:(dc + 1) * NE],
                                 ln2f[:, ps * 512:ps * 512 + 512],
                                 start=(dc == 0), stop=(dc == 7))
                ch["pe"].tag(i)
            block.tensor(l2mm)
        def lgev2(v):
            v.wait_ge(ch["pe"].sem, ch["pe"].n)
            ch["dve"].tag(v.tensor_copy(out=lgf[0:NE, 0:T], in_=PS2[0:NE, 0:T]))
        block.vector(lgev2)
        for tt in range(TT):
            def ltr(t, tt=tt, now=ch["dve"].n):
                t.wait_ge(ch["dve"].sem, now)
                t.wait_ge(ch["gc0"].sem, gld_x)
                i = t.transpose(PS1[:, (tt % 8) * 128:(tt % 8) * 128 + NE],
                                lgf[0:NE, tt * 128:(tt + 1) * 128], identb[0:NE, 0:NE])
                ch["pe"].tag(i)
            block.tensor(ltr)
            pe_now = ch["pe"].n
            def lte(v, tt=tt, pe_now=pe_now):
                v.wait_ge(ch["pe"].sem, pe_now)
                ch["dve"].tag(v.tensor_copy(out=logitsT[:, tt * NE:(tt + 1) * NE],
                                            in_=PS1[:, (tt % 8) * 128:(tt % 8) * 128 + NE]))
            block.vector(lte)
            def ltg(t, now=ch["dve"].n):
                t.wait_ge(ch["dve"].sem, now)
            block.tensor(ltg)

        def l2a(v):
            for dc in range(8):
                ch["dve"].tag(v.tensor_add(out=lnb[:, dc * T:(dc + 1) * T],
                                           in0=lnb[:, dc * T:(dc + 1) * T], in1=nbc[:]))
            v.wait_ge(ch["dve"].sem, ch["dve"].n)
            for dc in range(8):
                ch["dve"].tag(v.tensor_mul(out=lnb[:, dc * T:(dc + 1) * T],
                                           in0=lnb[:, dc * T:(dc + 1) * T], in1=rbc[:]))
        block.vector(l2a)

        mx8 = E5(nc.sbuf_tensor("mx8", [128, TT * 8], F32))
        selb = E5(nc.sbuf_tensor("selb", [128, TT * NE], BF16))
        wn = E5(nc.sbuf_tensor("wn", [128, TT * NE], F32))
        wsum = E5(nc.sbuf_tensor("wsum", [128, 2], F32))
        cumo = E5(nc.sbuf_tensor("cumo", [128, TT * NE], F32))
        selw = E5(nc.sbuf_tensor("selw", [128, NE], F32))

        for tt in range(TT):
            lsl = slice(tt * NE, (tt + 1) * NE)
            def r1(v, tt=tt, lsl=lsl, dnow=ch["dve"].n):
                v.wait_ge(ch["dve"].sem, dnow)
                ch["dve"].tag(v.max(out=mx8[:, tt * 8:(tt + 1) * 8], in_=logitsT[:, lsl]))
                v.wait_ge(ch["dve"].sem, ch["dve"].n)
                ch["dve"].tag(v.tensor_scalar(out=selw[:], in0=logitsT[:, lsl],
                                              scalar1=mx8[:, tt * 8 + 7: tt * 8 + 8],
                                              scalar2=None, op0=ALU.is_ge))
                ch["dve"].tag(v.tensor_scalar(out=wsum[:, 1:2], in0=mx8[:, tt * 8: tt * 8 + 1],
                                              scalar1=-1.0, scalar2=None, op0=ALU.mult))
            block.vector(r1)
            def r2(s, tt=tt, lsl=lsl, now=ch["dve"].n):
                s.wait_ge(ch["dve"].sem, now)
                ch["act"].tag(s.activation(out=wn[:, lsl], in_=logitsT[:, lsl],
                                           func=AF.Exp, bias=wsum[:, 1:2]))
            block.scalar(r2)
            def r3(v, tt=tt, lsl=lsl, now=ch["act"].n):
                v.wait_ge(ch["act"].sem, now)
                ch["dve"].tag(v.scalar_tensor_tensor(out=wn[:, lsl], in0=wn[:, lsl],
                                                     scalar=1.0, in1=selw[:],
                                                     op0=ALU.mult, op1=ALU.mult,
                                                     accum_out=wsum[:, 0:1]))
                ch["dve"].tag(v.tensor_copy(out=selb[:, lsl], in_=selw[:]))
            block.vector(r3)
            def r4(s, now=ch["dve"].n):
                s.wait_ge(ch["dve"].sem, now)
                ch["act"].tag(s.activation(out=wsum[:, 1:2], in_=wsum[:, 0:1], func=AF.Ln))
                s.wait_ge(ch["act"].sem, ch["act"].n)
                ch["act"].tag(s.activation(out=wsum[:, 1:2], in_=wsum[:, 1:2],
                                           func=AF.Exp, scale=-1.0))
            block.scalar(r4)
            def r5(v, tt=tt, lsl=lsl, now=ch["act"].n):
                v.wait_ge(ch["act"].sem, now)
                ch["dve"].tag(v.tensor_scalar(out=wn[:, lsl], in0=wn[:, lsl],
                                              scalar1=wsum[:, 1:2], scalar2=None, op0=ALU.mult))
            block.vector(r5)

        def cums(t):
            wait_all(t, [ch["dve"]])
            for qt in range(TT):
                mms = [("tri", qt)] + [("ones", kt) for kt in range(qt)]
                for j, (kind, kt) in enumerate(mms):
                    lhs = trib[:] if kind == "tri" else onesb[:]
                    i = t.matmul(PS1[:, qt * NE:(qt + 1) * NE], lhs,
                                 selb[:, kt * NE:(kt + 1) * NE],
                                 start=(j == 0), stop=(j == len(mms) - 1))
            ch["pe"].tag(i)
        block.tensor(cums)
        def cume(v):
            v.wait_ge(ch["pe"].sem, ch["pe"].n)
            ch["dve"].tag(v.tensor_copy(out=cumo[:], in_=PS1[:, 0:TT * NE]))
        block.vector(cume)

        def outs(s, now=ch["dve"].n):
            s.wait_ge(ch["dve"].sem, now)
            ch["sst"].dma(s.dma_start(out=o_cum, in_=cumo[:]))
            ch["sst"].dma(s.dma_start(out=o_wn, in_=wn[:]))
            ch["sst"].dma(s.dma_start(out=o_srow, in_=cumo[127:128, 15 * NE:16 * NE]))
            ch["sst"].dma(s.dma_start(out=o_ln2.rearrange("(c p) w -> p c w", p=128), in_=lnb[:].rearrange("p (c w) -> p c w", c=8)))
        block.sync(outs)

        wsguB = E5(nc.sbuf_tensor("wsguB", [128, 8 * 2 * SF], BF16))
        wsdnB = E5(nc.sbuf_tensor("wsdnB", [128, 6 * D], BF16))
        hbuf = E5(nc.sbuf_tensor("hbuf", [128, 6 * T], BF16))

        def shg(g):
            wait_all(g, [ch["pe"], ch["dve"], ch["act"]])
            ch["gsh"].dma(g.dma_start(out=wsguB[:].rearrange("p (c w) -> p c w", c=8), in_=wsgu.rearrange("(c p) w -> p c w", p=128)))
            ch["gsh"].dma(g.dma_start(out=wsdnB[:].rearrange("p (c w) -> p c w", c=6), in_=wsdn.rearrange("(c p) w -> p c w", p=128)))
        block.gpsimd(shg)
        gld_sh = ch["gsh"].n

        gu_tags = []
        act_base7 = ch["act"].n
        pe_base7 = ch["pe"].n
        dve_base7 = ch["dve"].n
        for fc in range(6):
            for ps in range(4):
                idx = fc * 4 + ps
                def gmm(t, fc=fc, ps=ps, idx=idx):
                    if idx == 0:
                        t.wait_ge(ch["gsh"].sem, gld_sh)
                        wait_all(t, [ch["dve"]])
                    elif idx >= 4:
                        t.wait_ge(ch["dve"].sem, dve_base7 + (idx - 4) + 1)
                    for ic in range(8):
                        t.matmul(PS1[:, ps * 512:ps * 512 + 512],
                                 wsguB[:, ic * 2 * SF + fc * 128: ic * 2 * SF + fc * 128 + 128],
                                 lnb[:, ic * T + ps * 512: ic * T + ps * 512 + 512],
                                 start=(ic == 0), stop=(ic == 7))
                    for ic in range(8):
                        i = t.matmul(PS2[:, ps * 512:ps * 512 + 512],
                                     wsguB[:, ic * 2 * SF + SF + fc * 128: ic * 2 * SF + SF + fc * 128 + 128],
                                     lnb[:, ic * T + ps * 512: ic * T + ps * 512 + 512],
                                     start=(ic == 0), stop=(ic == 7))
                    ch["pe"].tag(i)
                block.tensor(gmm)
                def gmu(v, fc=fc, ps=ps, idx=idx):
                    v.wait_ge(ch["act"].sem, act_base7 + idx + 1)
                    ch["dve"].tag(v.tensor_mul(out=hbuf[:, fc * T + ps * 512: fc * T + ps * 512 + 512],
                                               in0=PS2[:, ps * 512:ps * 512 + 512],
                                               in1=sq[:, ps * 512:ps * 512 + 512]))
                block.vector(gmu)
                gu_tags.append(ch["dve"].n)
        def gsl_all(s):
            for fc in range(6):
                for ps in range(4):
                    idx = fc * 4 + ps
                    s.wait_ge(ch["pe"].sem, pe_base7 + idx + 1)
                    ch["act"].tag(s.activation(out=sq[:, ps * 512:ps * 512 + 512],
                                               in_=PS1[:, ps * 512:ps * 512 + 512],
                                               func=AF.Silu))
        block.scalar(gsl_all)
        assert ch["act"].n == act_base7 + 24
        assert ch["pe"].n == pe_base7 + 24

        devtags = []
        gu_last = ch["dve"].n
        for oc in range(8):
            def dmm(t, oc=oc):
                if oc == 0:
                    t.wait_ge(ch["dve"].sem, gu_last)
                for ps in range(4):
                    for ic in range(6):
                        i = t.matmul(PS1[:, ps * 512:ps * 512 + 512],
                                     wsdnB[:, ic * D + oc * 128: ic * D + oc * 128 + 128],
                                     hbuf[:, ic * T + ps * 512: ic * T + ps * 512 + 512],
                                     start=(ic == 0), stop=(ic == 5))
                ch["pe"].tag(i)
            block.tensor(dmm)
            pe_now = ch["pe"].n
            def dld(s, oc=oc):
                if oc >= 1:
                    s.wait_ge(ch["dve"].sem, devtags[oc - 1])
                ch["sld"].dma(s.dma_start(out=xs2[:], in_=x1d.ap()[oc * 128:(oc + 1) * 128, :]))
            block.sync(dld)
            sld_now = ch["sld"].n
            def dev(v, oc=oc, pe_now=pe_now, sld_now=sld_now):
                v.wait_ge(ch["pe"].sem, pe_now)
                v.wait_ge(ch["sld"].sem, sld_now)
                ch["dve"].tag(v.tensor_add(out=ln2f[:], in0=PS1[:, 0:T], in1=xs2[:]))
                devtags.append(ch["dve"].n)
            block.vector(dev)
            def dst(s, oc=oc, now=ch["dve"].n):
                s.wait_ge(ch["dve"].sem, now)
                ch["sst"].dma(s.dma_start(out=o_xsh[oc * 128:(oc + 1) * 128, :], in_=ln2f[:]))
            block.sync(dst)
            def dgate(t, now=ch["dve"].n):
                t.wait_ge(ch["dve"].sem, now)
            def dgate2(v, now=ch["sst"].n):
                v.wait_ge(ch["sst"].sem, now)
            block.tensor(dgate)
            block.vector(dgate2)

        st5.close()

        def fin(s):
            s.wait_ge(ch["sst"].sem, ch["sst"].n)
        block.sync(fin)

    return nc


def l1_inputs(x_core, w, consts):
    return {
        "xT": np.ascontiguousarray(x_core.T),
        "wkv": w["L0_attn_W_kv"], "gkv": np.ascontiguousarray(w["L0_attn_g_kv"].reshape(2, 128).T),
        "wkdec": w["L0_attn_W_kdec"], "wvdec": w["L0_attn_W_vdec"],
        "wq": w["L0_attn_W_q"], "wqdec": w["L0_attn_W_qdec"],
        "wkrope": w["L0_attn_W_krope"], "wkropeS": rope_shuffle_cols(w["L0_attn_W_krope"]),
        "wqrope": w["L0_attn_W_qrope"], "wqropeS": rope_shuffle_cols(w["L0_attn_W_qrope"]),
        "wo": w["L0_attn_W_o"],
        "routerT": np.ascontiguousarray(w["L0_ffn_router_w"].T),
        "wsgu": w["L0_ffn_ws_gate_up"], "wsdn": w["L0_ffn_ws_down"],
        "IDENT": consts["IDENT"], "TRICUM": consts["TRICUM"], "ONES": consts["ONES"],
        "MASKTRI": consts["MASKTRI"], "COS": consts["COS"], "SIN": consts["SIN"],
    }


MAPN = 32896


def build_l2():
    nc = bass.Bass("TRN2", target_bir_lowering=False)
    def din(name, shape, dt=F32):
        return nc.dram_tensor(name, list(shape), dt, kind="ExternalInput").ap()
    cum = din("cum", (128, TT * NE))
    wn = din("wnorm", (128, TT * NE))
    srowA = din("srowA", (8, NE))
    cmask = din("cmask", (8, 1))
    tokid1 = din("tokid1", (128, TT))
    E1024W = din("E1024W", (128, TT * NE))
    E8W = din("E8W", (128, TT * NE))
    TRASHW = din("TRASHW", (128, TT * 8))
    ONESC = din("ONES", (128, 128))

    o_map = nc.dram_tensor("mapc", [MAPN, 1], F32, kind="ExternalOutput").ap()
    o_s8 = nc.dram_tensor("slot8", [128, TT * 8], F32, kind="ExternalOutput").ap()
    o_w8 = nc.dram_tensor("w8v", [128, TT * 8], F32, kind="ExternalOutput").ap()

    with ExitStack() as ctx:
        E = ctx.enter_context
        ch = {k: Ch(ctx, nc, k) for k in ("sld", "gsc", "pe", "dve", "act", "sst")}
        scum = E(nc.sbuf_tensor("scum", [128, TT * NE], F32))
        swn = E(nc.sbuf_tensor("swn", [128, TT * NE], F32))
        ssrow = E(nc.sbuf_tensor("ssrow", [8, NE], F32))
        scm = E(nc.sbuf_tensor("scm", [8, 1], F32))
        stok = E(nc.sbuf_tensor("stok", [128, TT], F32))
        se1024 = E(nc.sbuf_tensor("se1024", [128, TT * NE], F32))
        se8 = E(nc.sbuf_tensor("se8", [128, TT * NE], F32))
        strash = E(nc.sbuf_tensor("strash", [128, TT * 8], F32))
        sonesF = E(nc.sbuf_tensor("sonesF", [128, 128], F32))
        obb = E(nc.sbuf_tensor("obb", [128, NE], F32))
        orow = E(nc.sbuf_tensor("orow", [1, NE], F32))
        zet = E(nc.sbuf_tensor("zet", [128, 257], F32))
        t1 = E(nc.sbuf_tensor("t1", [128, TT * NE], F32))
        t2 = E(nc.sbuf_tensor("t2", [128, TT * NE], F32))
        t3 = E(nc.sbuf_tensor("t3", [128, TT * NE], F32))
        sv8a = E(nc.sbuf_tensor("sv8a", [128, TT * 8], F32))
        su8a = E(nc.sbuf_tensor("su8a", [128, TT * 8], F32))
        svi = E(nc.sbuf_tensor("svi", [128, TT * 8], I32))
        svb = E(nc.sbuf_tensor("svb", [128, TT * 8], I32))
        validf = E(nc.sbuf_tensor("validf", [128, TT * 8], F32))
        sslot = E(nc.sbuf_tensor("sslot", [128, TT * 8], F32))
        sw8 = E(nc.sbuf_tensor("sw8", [128, TT * 8], F32))
        sidx = E(nc.sbuf_tensor("sidx", [128, TT * 8], I32))
        PS = E(nc.psum_tensor("PS", [128, 512], F32))
        block = E(nc.Block())

        def lds(s):
            for dst, srcp in ((scum[:], cum), (swn[:], wn), (ssrow[:], srowA),
                              (scm[:], cmask), (stok[:], tokid1), (se1024[:], E1024W),
                              (se8[:], E8W), (strash[:], TRASHW), (sonesF[:], ONESC)):
                ch["sld"].dma(s.dma_start(out=dst, in_=srcp))
        block.sync(lds)
        sld0 = ch["sld"].n

        def zeroV(v):
            ch["dve"].tag(v.memset(zet[:], 0.0))
        block.vector(zeroV)
        dve_z = ch["dve"].n
        def zero(g):
            g.wait_ge(ch["dve"].sem, dve_z)
            ch["gsc"].dma(g.dma_start(out=o_map.rearrange("(p b) v -> p (b v)", p=128), in_=zet[:]))
        block.gpsimd(zero)

        def off(t):
            t.wait_ge(ch["sld"].sem, sld0)
            i = t.matmul(PS[0:1, 0:NE], scm[:], ssrow[:], start=True, stop=True)
            ch["pe"].tag(i)
        block.tensor(off)
        def offe(v):
            v.wait_ge(ch["pe"].sem, ch["pe"].n)
            ch["dve"].tag(v.tensor_copy(out=orow[:], in_=PS[0:1, 0:NE]))
        block.vector(offe)
        def offb(t, now=ch["dve"].n):
            t.wait_ge(ch["dve"].sem, now)
            i = t.matmul(PS[:, 0:NE], sonesF[0:1, :], orow[:], start=True, stop=True)
            ch["pe"].tag(i)
        block.tensor(offb)
        def offbe(v):
            v.wait_ge(ch["pe"].sem, ch["pe"].n)
            ch["dve"].tag(v.tensor_copy(out=obb[:], in_=PS[:, 0:NE]))
        block.vector(offbe)

        def chain(v):
            C = ch["dve"]
            def step(i):
                C.tag(i)
                v.wait_ge(C.sem, C.n)
            v.wait_ge(C.sem, C.n)
            for tt in range(TT):
                C.tag(v.tensor_add(out=t1[:, tt * NE:(tt + 1) * NE],
                                   in0=scum[:, tt * NE:(tt + 1) * NE], in1=obb[:]))
            v.wait_ge(C.sem, C.n)
            step(v.tensor_scalar(out=t1[:], in0=t1[:], scalar1=1.0, scalar2=None, op0=ALU.subtract))
            step(v.tensor_scalar(out=t3[:], in0=swn[:], scalar1=0.0, scalar2=None, op0=ALU.is_gt))
            step(v.tensor_scalar(out=t2[:], in0=t1[:], scalar1=512.0, scalar2=None, op0=ALU.is_lt))
            step(v.tensor_scalar(out=t1[:], in0=t1[:], scalar1=511.0, scalar2=None, op0=ALU.min))
            step(v.tensor_scalar(out=t1[:], in0=t1[:], scalar1=0.0, scalar2=None, op0=ALU.max))
            step(v.tensor_scalar(out=t1[:], in0=t1[:], scalar1=2.0, scalar2=None, op0=ALU.mult))
            step(v.tensor_add(out=t1[:], in0=t1[:], in1=se1024[:]))
            step(v.tensor_add(out=t1[:], in0=t1[:], in1=t2[:]))
            step(v.tensor_scalar(out=t1[:], in0=t1[:], scalar1=65536.0, scalar2=None, op0=ALU.add))
            step(v.tensor_mul(out=t1[:], in0=t1[:], in1=t3[:]))
            step(v.tensor_scalar(out=t1[:], in0=t1[:], scalar1=65536.0, scalar2=None, op0=ALU.subtract))
            for tt in range(TT):
                C.tag(v.max(out=sv8a[:, tt * 8:(tt + 1) * 8], in_=t1[:, tt * NE:(tt + 1) * NE]))
            v.wait_ge(C.sem, C.n)
            step(v.tensor_scalar(out=t2[:], in0=swn[:], scalar1=4.0, scalar2=None, op0=ALU.mult))
            step(v.tensor_add(out=t2[:], in0=t2[:], in1=se8[:]))
            step(v.tensor_scalar(out=t2[:], in0=t2[:], scalar1=1024.0, scalar2=None, op0=ALU.add))
            step(v.tensor_mul(out=t2[:], in0=t2[:], in1=t3[:]))
            step(v.tensor_scalar(out=t2[:], in0=t2[:], scalar1=1024.0, scalar2=None, op0=ALU.subtract))
            for tt in range(TT):
                C.tag(v.max(out=su8a[:, tt * 8:(tt + 1) * 8], in_=t2[:, tt * NE:(tt + 1) * NE]))
            v.wait_ge(C.sem, C.n)
            step(v.tensor_copy(out=svi[:], in_=sv8a[:]))
            step(v.tensor_scalar(out=svb[:], in0=svi[:], scalar1=1, scalar2=None, op0=ALU.bitwise_and))
            step(v.tensor_scalar(out=svi[:], in0=svi[:], scalar1=1, scalar2=None, op0=ALU.arith_shift_right))
            step(v.tensor_copy(out=sslot[:], in_=svi[:]))
            step(v.tensor_copy(out=validf[:], in_=svb[:]))
            step(v.tensor_scalar(out=t1[:, 0:TT * 8], in0=su8a[:], scalar1=0.125, scalar2=None, op0=ALU.mult))
            step(v.tensor_copy(out=svi[:], in_=t1[:, 0:TT * 8]))
            step(v.tensor_copy(out=t1[:, 0:TT * 8], in_=svi[:]))
            step(v.tensor_scalar(out=t1[:, 0:TT * 8], in0=t1[:, 0:TT * 8], scalar1=-8.0, scalar2=None, op0=ALU.mult))
            step(v.tensor_add(out=t1[:, 0:TT * 8], in0=t1[:, 0:TT * 8], in1=su8a[:]))
            step(v.tensor_scalar(out=t1[:, 0:TT * 8], in0=t1[:, 0:TT * 8], scalar1=0.25, scalar2=None, op0=ALU.mult))
            step(v.tensor_scalar(out=t3[:, 0:TT * 8], in0=t1[:, 0:TT * 8], scalar1=0.0, scalar2=None, op0=ALU.is_lt))
            step(v.tensor_scalar(out=t3[:, 0:TT * 8], in0=t3[:, 0:TT * 8], scalar1=2.0, scalar2=None, op0=ALU.mult))
            step(v.tensor_add(out=t1[:, 0:TT * 8], in0=t1[:, 0:TT * 8], in1=t3[:, 0:TT * 8]))
            step(v.tensor_mul(out=sw8[:], in0=t1[:, 0:TT * 8], in1=validf[:]))
            step(v.tensor_scalar(out=t3[:, 0:TT * 8], in0=validf[:], scalar1=-1.0, scalar2=None, op0=ALU.mult))
            step(v.tensor_scalar(out=t3[:, 0:TT * 8], in0=t3[:, 0:TT * 8], scalar1=1.0, scalar2=None, op0=ALU.add))
            step(v.tensor_mul(out=t3[:, 0:TT * 8], in0=t3[:, 0:TT * 8], in1=strash[:]))
            step(v.tensor_mul(out=t2[:, 0:TT * 8], in0=sslot[:], in1=validf[:]))
            step(v.tensor_add(out=t2[:, 0:TT * 8], in0=t2[:, 0:TT * 8], in1=t3[:, 0:TT * 8]))
            step(v.tensor_copy(out=sidx[:], in_=t2[:, 0:TT * 8]))
        block.vector(chain)

        dnow = ch["dve"].n
        def scat(g):
            g.wait_ge(ch["dve"].sem, dnow)
            g.wait_ge(ch["gsc"].sem, 16)
            for tt in range(TT):
                for k in range(8):
                    ch["gsc"].dma(g.indirect_dma_start(
                        out=o_map, out_offset=bass.IndirectOffsetOnAxis(sidx[:, tt * 8 + k: tt * 8 + k + 1], 0),
                        in_=stok[:, tt:tt + 1], in_offset=None))
            g.wait_ge(ch["gsc"].sem, ch["gsc"].n)
        block.gpsimd(scat)

        def outs(s):
            s.wait_ge(ch["dve"].sem, dnow)
            ch["sst"].dma(s.dma_start(out=o_s8, in_=sslot[:]))
            ch["sst"].dma(s.dma_start(out=o_w8, in_=sw8[:]))
            s.wait_ge(ch["sst"].sem, ch["sst"].n)
        block.sync(outs)
    return nc


def build_l3():
    nc = bass.Bass("TRN2", target_bir_lowering=False)
    def din(name, shape, dt=F32):
        return nc.dram_tensor(name, list(shape), dt, kind="ExternalInput").ap()
    maps = din("maps", (8, MAPN))
    ln2r = din("ln2r", (NC * T, D), BF16)
    myrows = din("myrows", (32, 1), I32)
    wg = din("wg", (NEL, D, FF))
    wu = din("wu", (NEL, D, FF))
    wd = din("wd", (NEL, FF, D))
    IDENT = din("IDENT", (128, 128))
    o_y = nc.dram_tensor("y", [NEL * CAP, D], BF16, kind="ExternalOutput").ap()
    mapd = nc.dram_tensor("mapd", [MAPN, 1], F32)

    with ExitStack() as ctx:
        E = ctx.enter_context
        ch = {k: Ch(ctx, nc, k) for k in ("sld", "gw", "gw2", "gg", "gg2", "pe", "dve", "act", "sst")}
        gwch = [ch["gw"], ch["gw2"]]
        ggx = [ch["gg"], ch["gg2"]]
        smaps = E(nc.sbuf_tensor("smaps", [128, 8 * 257], F32))
        identb = E(nc.sbuf_tensor("identb", [128, 128], BF16))
        identF = E(nc.sbuf_tensor("identF", [128, 128], F32))
        trtags = []
        trtags2 = []
        wfree = []
        smyr = E(nc.sbuf_tensor("smyr", [32, 1], I32))
        mext = E(nc.sbuf_tensor("mext", [32, 128], F32))
        mtr = E(nc.sbuf_tensor("mtr", [128, 32], F32))
        xidx = E(nc.sbuf_tensor("xidx", [128, 32], I32))
        xg = [E(nc.sbuf_tensor(f"xg{i}", [128, 1024], BF16)) for i in range(2)]
        xbT = E(nc.sbuf_tensor("xbT", [128, 8 * (NEL * CAP)], BF16))
        hbuf = E(nc.sbuf_tensor("hbuf", [128, 6 * CAP], BF16))
        sqs = E(nc.sbuf_tensor("sqs", [128, 512], BF16))
        yev = E(nc.sbuf_tensor("yev", [128, D], BF16))
        wgB = [E(nc.sbuf_tensor(f"wgB{i}", [128, 8 * FF], BF16)) for i in range(2)]
        wuB = [E(nc.sbuf_tensor(f"wuB{i}", [128, 8 * FF], BF16)) for i in range(2)]
        wdB = [E(nc.sbuf_tensor(f"wdB{i}", [128, 6 * D], BF16)) for i in range(2)]
        PS1 = E(nc.psum_tensor("PS1", [128, 1024], F32))
        PS2 = E(nc.psum_tensor("PS2", [128, 1024], F32))
        PSB = [E(nc.psum_tensor("PSBa", [128, 128], BF16)),
               E(nc.psum_tensor("PSBb", [128, 128], BF16))]
        block = E(nc.Block())

        def lds(s):
            ch["sld"].dma(s.dma_start(out=smaps[:].rearrange("p (c b) -> p c b", c=8),
                                      in_=maps.rearrange("c (p b) -> p c b", p=128)))
            ch["sld"].dma(s.dma_start(out=smyr[:], in_=myrows))
        block.sync(lds)
        sld0 = ch["sld"].n
        def wload(g, e, par):
            for dst, srcp, nch in ((wgB[par], wg, 8), (wuB[par], wu, 8), (wdB[par], wd, 6)):
                gwch[par].dma(g.dma_start(
                    out=dst[:].rearrange("p (c w) -> p c w", c=nch),
                    in_=srcp[e].rearrange("(c p) w -> p c w", p=128)))
        def ldg0(g):
            ch["gg"].dma(g.dma_start(out=identb[:], in_=IDENT))
            ch["gg"].dma(g.dma_start(out=identF[:], in_=IDENT))
            wload(g, 0, 0)
        block.gpsimd(ldg0)
        gw_id = ch["gg"].n

        def msum(v):
            v.wait_ge(ch["sld"].sem, sld0)
            for c in range(1, 8):
                ch["dve"].tag(v.tensor_add(out=smaps[:, 0:257], in0=smaps[:, 0:257],
                                           in1=smaps[:, c * 257:(c + 1) * 257]))
                v.wait_ge(ch["dve"].sem, ch["dve"].n)
        block.vector(msum)

        def mst(g, now=ch["dve"].n):
            g.wait_ge(ch["dve"].sem, now)
            ch["gg"].dma(g.dma_start(out=mapd.ap().rearrange("(p b) v -> p (b v)", p=128),
                                     in_=smaps[:, 0:257]))
            g.wait_ge(ch["gg"].sem, ch["gg"].n)
            ch["gg"].dma(g.indirect_dma_start(
                out=mext[:], out_offset=None,
                in_=mapd.ap()[0:32768, :].rearrange("(r q) v -> r (q v)", q=128),
                in_offset=bass.IndirectOffsetOnAxis(smyr[:, 0:1], 0)))
            g.wait_ge(ch["gg"].sem, ch["gg"].n)
        block.gpsimd(mst)
        gg_ext = ch["gg"].n

        def mtp(t):
            t.wait_ge(ch["gg"].sem, gg_ext)
            i = t.transpose(PS1[:, 0:32], mext[:], identF[0:32, 0:32])
            ch["pe"].tag(i)
        block.tensor(mtp)
        def mte(v):
            v.wait_ge(ch["pe"].sem, ch["pe"].n)
            ch["dve"].tag(v.tensor_copy(out=mtr[:], in_=PS1[:, 0:32]))
            v.wait_ge(ch["dve"].sem, ch["dve"].n)
            ch["dve"].tag(v.tensor_scalar(out=mtr[:], in0=mtr[:], scalar1=1.0, scalar2=None, op0=ALU.max))
            v.wait_ge(ch["dve"].sem, ch["dve"].n)
            ch["dve"].tag(v.tensor_scalar(out=mtr[:], in0=mtr[:], scalar1=1.0, scalar2=None, op0=ALU.subtract))
            v.wait_ge(ch["dve"].sem, ch["dve"].n)
            ch["dve"].tag(v.tensor_copy(out=xidx[:], in_=mtr[:]))
        block.vector(mte)
        dve_idx = ch["dve"].n

        for j in range(32):
            jp = j % 2
            def gat(g, j=j, jp=jp, dve_idx=dve_idx):
                g.wait_ge(ggx[jp].sem, ggx[jp].n)
                if j == 0:
                    g.wait_ge(ch["dve"].sem, dve_idx)
                    g.wait_ge(ch["gg"].sem, ch["gg"].n)
                elif j >= 2:
                    g.wait_ge(ch["pe"].sem, trtags[j - 2])
                ggx[jp].dma(g.indirect_dma_start(
                    out=xg[jp][:], out_offset=None, in_=ln2r,
                    in_offset=bass.IndirectOffsetOnAxis(xidx[:, j:j + 1], 0)))
            block.gpsimd(gat)
            gg_now = ggx[jp].n
            for dc in range(8):
                gi = j * 8 + dc
                def trp(t, j=j, jp=jp, dc=dc, gg_now=gg_now, gi=gi):
                    t.wait_ge(ggx[jp].sem, gg_now)
                    if gi >= 2:
                        t.wait_ge(ch["dve"].sem, trtags2[gi - 2])
                    i = t.transpose(PSB[gi % 2][:, 0:128],
                                    xg[jp][:, dc * 128:(dc + 1) * 128], identb[:])
                    ch["pe"].tag(i)
                block.tensor(trp)
                pe_now = ch["pe"].n
                def tre(v, j=j, dc=dc, pe_now=pe_now, gi=gi):
                    v.wait_ge(ch["pe"].sem, pe_now)
                    ch["dve"].tag(v.tensor_copy(
                        out=xbT[:, dc * (NEL * CAP) + j * 128: dc * (NEL * CAP) + j * 128 + 128],
                        in_=PSB[gi % 2][:, 0:128]))
                block.vector(tre)
                trtags2.append(ch["dve"].n)
            trtags.append(ch["pe"].n)

        gwtags = [ch["gw"].n]

        ysttags = []
        for e in range(NEL):
            par = e % 2
            if e + 1 < NEL:
                def wln(g, e=e, par=par):
                    if e >= 1:
                        g.wait_ge(ch["pe"].sem, wfree[e - 1])
                    wload(g, e + 1, 1 - par)
                block.gpsimd(wln)
                gwtags.append(gwch[(e + 1) % 2].n)
            base = e * CAP
            for fc in range(6):
                def gmm(t, e=e, par=par, fc=fc):
                    if fc == 0:
                        t.wait_ge(gwch[par].sem, gwtags[e])
                        wait_all(t, [ch["dve"]])
                    for ic in range(8):
                        t.matmul(PS1[:, 0:512],
                                 wgB[par][:, ic * FF + fc * 128: ic * FF + fc * 128 + 128],
                                 xbT[:, ic * (NEL * CAP) + base: ic * (NEL * CAP) + base + 512],
                                 start=(ic == 0), stop=(ic == 7))
                    for ic in range(8):
                        i = t.matmul(PS2[:, 0:512],
                                     wuB[par][:, ic * FF + fc * 128: ic * FF + fc * 128 + 128],
                                     xbT[:, ic * (NEL * CAP) + base: ic * (NEL * CAP) + base + 512],
                                     start=(ic == 0), stop=(ic == 7))
                    ch["pe"].tag(i)
                block.tensor(gmm)
                pe_now = ch["pe"].n
                def gsl(s, pe_now=pe_now):
                    s.wait_ge(ch["pe"].sem, pe_now)
                    ch["act"].tag(s.activation(out=sqs[:], in_=PS1[:, 0:512], func=AF.Silu))
                block.scalar(gsl)
                def gmu(v, fc=fc, now=ch["act"].n):
                    v.wait_ge(ch["act"].sem, now)
                    ch["dve"].tag(v.tensor_mul(out=hbuf[:, fc * CAP: fc * CAP + 512],
                                               in0=PS2[:, 0:512], in1=sqs[:]))
                block.vector(gmu)
                def ggate(t, nd=ch["dve"].n):
                    t.wait_ge(ch["dve"].sem, nd)
                block.tensor(ggate)
            for sc in range(4):
                for dp in range(2):
                    def dmm(t, e=e, par=par, sc=sc, dp=dp):
                        for ic in range(6):
                            i = t.matmul(PS1[0:128, dp * 512:dp * 512 + 512],
                                         hbuf[:, ic * CAP + sc * 128: ic * CAP + sc * 128 + 128],
                                         wdB[par][:, ic * D + dp * 512: ic * D + dp * 512 + 512],
                                         start=(ic == 0), stop=(ic == 5))
                        ch["pe"].tag(i)
                    block.tensor(dmm)
                pe_now = ch["pe"].n
                def yevf(v, pe_now=pe_now, snow=len(ysttags)):
                    v.wait_ge(ch["pe"].sem, pe_now)
                    if snow >= 1:
                        v.wait_ge(ch["sst"].sem, ysttags[snow - 1])
                    ch["dve"].tag(v.tensor_copy(out=yev[:], in_=PS1[:, 0:1024]))
                block.vector(yevf)
                def yst(s, e=e, sc=sc, now=ch["dve"].n):
                    s.wait_ge(ch["dve"].sem, now)
                    ch["sst"].dma(s.dma_start(out=o_y[e * CAP + sc * 128: e * CAP + sc * 128 + 128, :],
                                              in_=yev[:]))
                block.sync(yst)
                ysttags.append(ch["sst"].n)
                def dgate(t, nd=ch["dve"].n):
                    t.wait_ge(ch["dve"].sem, nd)
                block.tensor(dgate)
            wfree.append(ch["pe"].n)

        def fin(s):
            s.wait_ge(ch["sst"].sem, ch["sst"].n)
        block.sync(fin)
    return nc


def build_l4():
    nc = bass.Bass("TRN2", target_bir_lowering=False)
    def din(name, shape, dt=F32):
        return nc.dram_tensor(name, list(shape), dt, kind="ExternalInput").ap()
    yfull = din("yfull", (NE * CAP, D), BF16)
    slot8 = din("slot8", (128, TT * 8))
    w8v = din("w8v", (128, TT * 8))
    xsht = din("xsht", (T, D))
    o_out = nc.dram_tensor("out", [T, D], F32, kind="ExternalOutput").ap()

    with ExitStack() as ctx:
        E = ctx.enter_context
        ch = {k: Ch(ctx, nc, k) for k in ("sld", "sx", "gg", "gg2", "pe", "dve", "act", "sst")}
        ggch = [ch["gg"], ch["gg2"]]
        ss8 = E(nc.sbuf_tensor("ss8", [128, TT * 8], F32))
        sw8 = E(nc.sbuf_tensor("sw8", [128, TT * 8], F32))
        sidx = E(nc.sbuf_tensor("sidx", [128, TT * 8], I32))
        yg = [[E(nc.sbuf_tensor(f"yg{i}_{k}", [128, D], BF16)) for k in range(8)]
              for i in range(2)]
        sxsh = [E(nc.sbuf_tensor(f"sxsh{i}", [128, D], F32)) for i in range(2)]
        acc = E(nc.sbuf_tensor("acc", [128, D], F32))
        oval = [E(nc.sbuf_tensor(f"oval{i}", [128, D], F32)) for i in range(2)]
        block = E(nc.Block())

        def lds(s):
            ch["sld"].dma(s.dma_start(out=ss8[:], in_=slot8))
            ch["sld"].dma(s.dma_start(out=sw8[:], in_=w8v))
        block.sync(lds)
        sld0 = ch["sld"].n
        def cvt(v):
            v.wait_ge(ch["sld"].sem, sld0)
            ch["dve"].tag(v.tensor_copy(out=sidx[:], in_=ss8[:]))
        block.vector(cvt)
        dve_idx = ch["dve"].n

        gtags = []
        xtags = []
        dtags = []
        stags = []
        for tt in range(TT):
            par = tt % 2
            def gat(g, tt=tt, par=par):
                if tt == 0:
                    g.wait_ge(ch["dve"].sem, dve_idx)
                elif tt >= 2:
                    g.wait_ge(ch["dve"].sem, dtags[tt - 2][0])
                for k in range(8):
                    ggch[par].dma(g.indirect_dma_start(
                        out=yg[par][k][:], out_offset=None, in_=yfull,
                        in_offset=bass.IndirectOffsetOnAxis(sidx[:, tt * 8 + k: tt * 8 + k + 1], 0)))
            block.gpsimd(gat)
            gtags.append(ggch[par].n)
            def xld(s, tt=tt, par=par):
                if tt >= 2:
                    s.wait_ge(ch["dve"].sem, dtags[tt - 2][1])
                ch["sx"].dma(s.dma_start(out=sxsh[par][:], in_=xsht[tt * 128:(tt + 1) * 128, :]))
            block.sync(xld)
            xtags.append(ch["sx"].n)
            def comb(v, tt=tt, par=par, gnow=gtags[tt], xnow=xtags[tt], dprev=ch["dve"].n):
                v.wait_ge(ggch[par].sem, gnow)
                v.wait_ge(ch["dve"].sem, dprev)
                ch["dve"].tag(v.tensor_scalar(out=acc[:], in0=yg[par][0][:],
                                              scalar1=sw8[:, tt * 8: tt * 8 + 1],
                                              scalar2=None, op0=ALU.mult))
                for k in range(1, 8):
                    v.wait_ge(ch["dve"].sem, ch["dve"].n)
                    ch["dve"].tag(v.scalar_tensor_tensor(out=acc[:], in0=yg[par][k][:],
                                                         scalar=sw8[:, tt * 8 + k: tt * 8 + k + 1],
                                                         in1=acc[:], op0=ALU.mult, op1=ALU.add))
                v.wait_ge(ch["dve"].sem, ch["dve"].n)
                v.wait_ge(ch["sx"].sem, xnow)
                if tt >= 1:
                    v.wait_ge(ch["sst"].sem, stags[tt - 1])
                yg_free = ch["dve"].tag(v.tensor_add(out=oval[par][:], in0=acc[:], in1=sxsh[par][:]))
                dtags.append((yg_free, yg_free))
            block.vector(comb)
            def ost(s, tt=tt, par=par, now=ch["dve"].n):
                s.wait_ge(ch["dve"].sem, now)
                ch["sst"].dma(s.dma_start(out=o_out[tt * 128:(tt + 1) * 128, :], in_=oval[par][:]))
            block.sync(ost)
            stags.append(ch["sst"].n)

        def fin(s):
            s.wait_ge(ch["sst"].sem, ch["sst"].n)
        block.sync(fin)
    return nc


def l2_inputs(r1c, srowA, c, consts):
    cmask = np.zeros((8, 1), np.float32); cmask[:c, 0] = 1.0
    tokid1 = (c * T + 1 + np.arange(128, dtype=np.float32)[:, None]
              + 128.0 * np.arange(TT, dtype=np.float32)[None, :])
    e1024 = np.tile((np.arange(NE, dtype=np.float32) * 1024.0)[None, :], (128, TT))
    e8 = np.tile((np.arange(NE, dtype=np.float32) * 8.0)[None, :], (128, TT))
    trashw = np.tile((32768.0 + np.arange(TT * 8, dtype=np.float32))[None, :], (128, 1))
    return {"cum": r1c["cum"], "wnorm": r1c["wnorm"], "srowA": srowA,
            "cmask": cmask, "tokid1": np.ascontiguousarray(tokid1),
            "E1024W": np.ascontiguousarray(e1024), "E8W": np.ascontiguousarray(e8),
            "TRASHW": np.ascontiguousarray(trashw), "ONES": consts["ONES"]}


def kernel(**inputs):
    x = inputs["L0_attn_x"]
    consts = host_consts()
    consts["E1024"] = np.tile((np.arange(NE, dtype=np.float32) * 1024.0)[None, :], (128, 1))
    consts["E8"] = np.tile((np.arange(NE, dtype=np.float32) * 8.0)[None, :], (128, 1))
    cores = list(range(NC))

    nc1 = build_l1()
    in1 = [l1_inputs(np.ascontiguousarray(x[c]), inputs, consts) for c in cores]
    r1 = run_bass_kernel_spmd(nc1, in1, core_ids=cores).results

    srowA = np.ascontiguousarray(np.stack([r1[c]["srow"][0] for c in cores]))
    nc2 = build_l2()
    in2 = [l2_inputs(r1[c], srowA, c, consts) for c in cores]
    r2 = run_bass_kernel_spmd(nc2, in2, core_ids=cores).results

    maps = np.ascontiguousarray(np.stack([r2[c]["mapc"][:, 0] for c in cores]))
    ln2r = np.ascontiguousarray(
        np.concatenate([r1[c]["ln2b"].T for c in cores], axis=0))
    nc3 = build_l3()
    in3 = []
    for c in cores:
        in3.append({
            "maps": maps, "ln2r": ln2r,
            "myrows": np.arange(c * 32, c * 32 + 32, dtype=np.int32).reshape(32, 1),
            "wg": np.ascontiguousarray(inputs["L0_ffn_w_gate"][c * NEL:(c + 1) * NEL]),
            "wu": np.ascontiguousarray(inputs["L0_ffn_w_up"][c * NEL:(c + 1) * NEL]),
            "wd": np.ascontiguousarray(inputs["L0_ffn_w_down"][c * NEL:(c + 1) * NEL]),
            "IDENT": consts["IDENT"],
        })
    r3 = run_bass_kernel_spmd(nc3, in3, core_ids=cores).results

    yfull = np.ascontiguousarray(np.concatenate([r3[c]["y"] for c in cores], axis=0))
    nc4 = build_l4()
    in4 = []
    for c in cores:
        in4.append({
            "yfull": yfull, "slot8": r2[c]["slot8"], "w8v": r2[c]["w8v"],
            "xsht": np.ascontiguousarray(r1[c]["xsh"].T),
        })
    r4 = run_bass_kernel_spmd(nc4, in4, core_ids=cores).results
    return np.stack([r4[c]["out"] for c in cores]).astype(np.float32)


# version 08
# speedup vs baseline: 1.2777x; 1.0593x over previous
import numpy as np
import ml_dtypes
from contextlib import ExitStack

import concourse.bass as bass
import concourse.mybir as mybir
from concourse.bass_utils import run_bass_kernel_spmd

F32 = mybir.dt.float32
BF16 = mybir.dt.bfloat16
I32 = mybir.dt.int32
AF = mybir.ActivationFunctionType
ALU = mybir.AluOpType
AX = mybir.AxisListType

T = 2048
D = 1024
NH = 16
DH = 64
DR = 32
KVR = 256
QR = 512
NE = 64
NEL = 8
CAP = 512
FF = 768
SF = 768
NC = 8
TT = T // 128
NEG = -1.0e30
SCALE = 1.0 / float(np.sqrt(DH + DR))


class Ch:
    def __init__(self, ctx, nc, name):
        self.sem = ctx.enter_context(nc.semaphore(name))
        self.n = 0

    def tag(self, inst, k=1):
        inst.then_inc(self.sem, k)
        self.n += k
        return self.n

    def dma(self, inst):
        return self.tag(inst, 16)


def wait_all(eng, chans):
    for c in chans:
        if c.n > 0:
            eng.wait_ge(c.sem, c.n)



def rope_tables():
    inv = 1.0 / 10000.0 ** (np.arange(0, DR, 2, dtype=np.float32) / DR)
    t = np.arange(T, dtype=np.float32)
    emb = np.concatenate([np.outer(t, inv)] * 2, -1)
    cosf = np.ascontiguousarray(np.cos(emb).astype(np.float32).T)
    sinf = np.ascontiguousarray(np.sin(emb).astype(np.float32).T)
    sinf_signed = sinf.copy()
    sinf_signed[0:16, :] *= -1.0
    return (np.ascontiguousarray(np.tile(cosf, (4, 1))),
            np.ascontiguousarray(np.tile(sinf_signed, (4, 1))))


def rope_shuffle_cols(w):
    perm = np.arange(w.shape[1]).reshape(-1, 2, 16)[:, ::-1, :].reshape(-1)
    return np.ascontiguousarray(w[:, perm])


def host_consts():
    c = {}
    c["IDENT"] = np.eye(128, dtype=np.float32)
    c["TRICUM"] = np.triu(np.ones((128, 128), np.float32))
    c["ONES"] = np.ones((128, 128), np.float32)
    kk, qq = np.meshgrid(np.arange(128), np.arange(128), indexing="ij")
    c["MASKTRI"] = np.where(kk > qq, np.float32(NEG), np.float32(0.0)).astype(np.float32)
    c["COS"], c["SIN"] = rope_tables()
    return c



def fm_pair(dst, src, width):
    n = src.shape[0] // 128
    return (dst.rearrange("p (c w) -> p c w", w=width),
            src.rearrange("(c p) w -> p c w", p=128))


def build_l1():
    nc = bass.Bass("TRN2", target_bir_lowering=False)

    def din(name, shape, dt=F32):
        return nc.dram_tensor(name, list(shape), dt, kind="ExternalInput").ap()

    xT = din("xT", (D, T))
    wkv = din("wkv", (D, KVR))
    gkv = din("gkv", (128, 2))
    wkdec = din("wkdec", (KVR, D))
    wvdec = din("wvdec", (KVR, D))
    wq = din("wq", (D, QR))
    wqdec = din("wqdec", (QR, D))
    wkrope = din("wkrope", (D, QR))
    wkropeS = din("wkropeS", (D, QR))
    wqrope = din("wqrope", (QR, QR))
    wqropeS = din("wqropeS", (QR, QR))
    wo = din("wo", (D, D))
    routerT = din("routerT", (D, NE))
    wsgu = din("wsgu", (D, 2 * SF))
    wsdn = din("wsdn", (SF, D))
    IDENT = din("IDENT", (128, 128))
    TRICUM = din("TRICUM", (128, 128))
    ONESC = din("ONES", (128, 128))
    MASKTRI = din("MASKTRI", (128, 128))
    COS = din("COS", (128, T))
    SIN = din("SIN", (128, T))

    o_xsh = nc.dram_tensor("xsh", [D, T], F32, kind="ExternalOutput").ap()
    o_ln2 = nc.dram_tensor("ln2b", [D, T], BF16, kind="ExternalOutput").ap()
    o_cum = nc.dram_tensor("cum", [128, TT * NE], F32, kind="ExternalOutput").ap()
    o_wn = nc.dram_tensor("wnorm", [128, TT * NE], F32, kind="ExternalOutput").ap()
    o_srow = nc.dram_tensor("srow", [1, NE], F32, kind="ExternalOutput").ap()

    x1d = nc.dram_tensor("x1d", [D, T], F32)

    ctx = ExitStack()
    with ctx:
        E = ctx.enter_context
        ch = {k: Ch(ctx, nc, k) for k in ("sld", "sr", "gc0", "gw3", "gw4", "gwo", "gsh",
                                          "gx1", "pe", "dve", "act", "sst")}

        lnb = E(nc.sbuf_tensor("lnb", [128, 8 * T], BF16))
        rows = E(nc.sbuf_tensor("rows", [128, T], F32))
        rowsb = E(nc.sbuf_tensor("rowsb", [128, T], BF16))
        bcast = E(nc.sbuf_tensor("bcast", [128, T], BF16))
        sq = E(nc.sbuf_tensor("sq", [128, T], BF16))
        identb = E(nc.sbuf_tensor("identb", [128, 128], F32))
        trib = E(nc.sbuf_tensor("trib", [128, 128], BF16))
        onesb = E(nc.sbuf_tensor("onesb", [128, 128], BF16))
        maskt = E(nc.sbuf_tensor("maskt", [128, 128], F32))
        gkvb = E(nc.sbuf_tensor("gkvb", [128, 2], F32))
        onesF = E(nc.sbuf_tensor("onesF", [128, 128], F32))
        PS1 = E(nc.psum_tensor("PS1", [128, T], F32))
        PS2 = E(nc.psum_tensor("PS2", [128, T], F32))
        block = E(nc.Block())


        st1 = ExitStack()
        xbf = st1.enter_context(nc.sbuf_tensor("xbf", [128, 8 * T], BF16))

        def p0g(g):
            ch["gc0"].dma(g.dma_start(out=xbf[:].rearrange("p (c w) -> p c w", c=8), in_=xT.rearrange("(c p) w -> p c w", p=128)))
            ch["gc0"].dma(g.dma_start(out=trib[:], in_=TRICUM))
            ch["gc0"].dma(g.dma_start(out=onesb[:], in_=ONESC))
            ch["gc0"].dma(g.dma_start(out=identb[:], in_=IDENT))
            ch["gc0"].dma(g.dma_start(out=maskt[:], in_=MASKTRI))
            ch["gc0"].dma(g.dma_start(out=gkvb[:], in_=gkv))
            ch["gc0"].dma(g.dma_start(out=onesF[:], in_=ONESC))
        block.gpsimd(p0g)
        gld_x = ch["gc0"].n
        sld_p0 = 0

        def emit_ln(src_bf, eps):
            a0 = ch["act"].n
            dve_at = ch["dve"].n
            def summ(t):
                t.wait_ge(ch["gc0"].sem, gld_x)
                if dve_at:
                    t.wait_ge(ch["dve"].sem, dve_at)
                for dc in range(8):
                    for ps in range(4):
                        i = t.matmul(PS1[0:1, ps * 512:ps * 512 + 512], onesb[:, 0:1],
                                     src_bf[:, dc * T + ps * 512: dc * T + ps * 512 + 512],
                                     start=(dc == 0), stop=(dc == 7))
                ch["pe"].tag(i)
            block.tensor(summ)
            pe_cons = []
            for dc in range(8):
                def sqf(s, dc=dc):
                    if dc == 0:
                        s.wait_ge(ch["gc0"].sem, gld_x)
                        if dve_at:
                            s.wait_ge(ch["dve"].sem, dve_at)
                    else:
                        s.wait_ge(ch["pe"].sem, pe_cons[dc - 1])
                    ch["act"].tag(s.activation(out=sq[:], in_=src_bf[:, dc * T:(dc + 1) * T],
                                               func=AF.Square))
                block.scalar(sqf)
                def mmf(t, dc=dc, anow=ch["act"].n):
                    t.wait_ge(ch["act"].sem, anow)
                    for ps in range(4):
                        i = t.matmul(PS2[0:1, ps * 512:ps * 512 + 512], onesb[:, 0:1],
                                     sq[:, ps * 512:ps * 512 + 512],
                                     start=(dc == 0), stop=(dc == 7))
                    ch["pe"].tag(i)
                block.tensor(mmf)
                pe_cons.append(ch["pe"].n)

            def rowmath(v):
                v.wait_ge(ch["pe"].sem, ch["pe"].n)
                ch["dve"].tag(v.tensor_scalar(out=rows[0:1, :], in0=PS1[0:1, 0:T],
                                              scalar1=1.0 / D, scalar2=None, op0=ALU.mult))
                ch["dve"].tag(v.tensor_scalar(out=rows[32:33, :], in0=PS1[0:1, 0:T],
                                              scalar1=-1.0 / D, scalar2=None, op0=ALU.mult))
                ch["dve"].tag(v.tensor_scalar(out=rows[64:65, :], in0=PS2[0:1, 0:T],
                                              scalar1=1.0 / D, scalar2=None, op0=ALU.mult))
                v.wait_ge(ch["dve"].sem, ch["dve"].n)
                ch["dve"].tag(v.tensor_mul(out=PS1[0:1, 0:T], in0=rows[0:1, :], in1=rows[0:1, :]))
                v.wait_ge(ch["dve"].sem, ch["dve"].n)
                ch["dve"].tag(v.tensor_sub(out=rows[64:65, :], in0=rows[64:65, :], in1=PS1[0:1, 0:T]))
                v.wait_ge(ch["dve"].sem, ch["dve"].n)
                ch["dve"].tag(v.tensor_scalar(out=rows[64:65, :], in0=rows[64:65, :],
                                              scalar1=eps, scalar2=None, op0=ALU.add))
            block.vector(rowmath)
            emit_rsqrt_row(64)

        def emit_rsqrt_row(prow):
            def f(s):
                s.wait_ge(ch["dve"].sem, ch["dve"].n)
                ch["act"].tag(s.activation(out=rows[prow:prow + 1, :],
                                           in_=rows[prow:prow + 1, :], func=AF.Ln))
                s.wait_ge(ch["act"].sem, ch["act"].n)
                ch["act"].tag(s.activation(out=rows[prow:prow + 1, :],
                                           in_=rows[prow:prow + 1, :], func=AF.Exp,
                                           scale=-0.5))
            block.scalar(f)

        def emit_bcast(prow, nrows=128):
            def cp(v):
                v.wait_ge(ch["dve"].sem, ch["dve"].n)
                wait_all(v, [ch["act"]])
                ch["dve"].tag(v.tensor_copy(out=rowsb[prow:prow + 1, :],
                                            in_=rows[prow:prow + 1, :]))
            block.vector(cp)
            def mm(t, now=ch["dve"].n):
                t.wait_ge(ch["gc0"].sem, gld_x)
                t.wait_ge(ch["dve"].sem, now)
                for ps in range(4):
                    i = t.matmul(PS2[0:nrows, ps * 512:ps * 512 + 512],
                                 onesb[prow:prow + 1, 0:nrows],
                                 rowsb[prow:prow + 1, ps * 512:ps * 512 + 512],
                                 start=True, stop=True)
                ch["pe"].tag(i)
            block.tensor(mm)
            def ev(v):
                v.wait_ge(ch["pe"].sem, ch["pe"].n)
                ch["dve"].tag(v.tensor_copy(out=bcast[0:nrows, :], in_=PS2[0:nrows, 0:T]))
            block.vector(ev)

        def emit_ln_apply(src_bf, dst_bf):
            emit_bcast(32)
            def add(v):
                v.wait_ge(ch["dve"].sem, ch["dve"].n)
                for dc in range(8):
                    i = v.tensor_add(out=dst_bf[:, dc * T:(dc + 1) * T],
                                     in0=src_bf[:, dc * T:(dc + 1) * T], in1=bcast[:])
                ch["dve"].tag(i)
            block.vector(add)
            emit_bcast(64)
            def mul(v):
                v.wait_ge(ch["dve"].sem, ch["dve"].n)
                for dc in range(8):
                    i = v.tensor_mul(out=dst_bf[:, dc * T:(dc + 1) * T],
                                     in0=dst_bf[:, dc * T:(dc + 1) * T], in1=bcast[:])
                ch["dve"].tag(i)
            block.vector(mul)

        emit_ln(xbf, 1e-5)
        emit_ln_apply(xbf, lnb)
        st1.close()

        st2 = ExitStack()
        E2 = st2.enter_context
        qc = E2(nc.sbuf_tensor("qc", [128, 4 * T], BF16))
        kvn = E2(nc.sbuf_tensor("kvn", [128, 2 * T], BF16))
        krp = E2(nc.sbuf_tensor("krp", [128, 4 * T], BF16))
        qrp = E2(nc.sbuf_tensor("qrp", [128, 4 * T], BF16))
        vaug = E2(nc.sbuf_tensor("vaug", [128, TT * NH * 65], BF16))

        st2t = ExitStack()
        E2t = st2t.enter_context
        wkvB = E2t(nc.sbuf_tensor("wkvB", [128, 8 * KVR], BF16))
        wqB = E2t(nc.sbuf_tensor("wqB", [128, 8 * QR], BF16))
        wkrB = E2t(nc.sbuf_tensor("wkrB", [128, 8 * QR], BF16))
        wkrSB = E2t(nc.sbuf_tensor("wkrSB", [128, 8 * QR], BF16))
        wqrB = E2t(nc.sbuf_tensor("wqrB", [128, 4 * QR], BF16))
        wqrSB = E2t(nc.sbuf_tensor("wqrSB", [128, 4 * QR], BF16))
        wvdB = E2t(nc.sbuf_tensor("wvdB", [128, 2 * D], BF16))
        kvc32 = E2t(nc.sbuf_tensor("kvc32", [128, 2 * T], F32))
        cosb = E2t(nc.sbuf_tensor("cosb", [128, T], BF16))
        sinb = E2t(nc.sbuf_tensor("sinb", [128, T], BF16))

        def p3g(g):
            def cast(dst, src, pat):
                ch["gw3"].dma(g.dma_start(out=dst.rearrange("p (c w) -> p c w", c=src.shape[0] // 128), in_=src.rearrange("(c p) w -> p c w", p=128)))
            ch["gw3"].dma(g.dma_start(out=cosb[:], in_=COS))
            ch["gw3"].dma(g.dma_start(out=sinb[:], in_=SIN))
            cast(wkvB[:], wkv, "(dc p) r -> p (dc r)")
            cast(wqB[:], wq, "(dc p) r -> p (dc r)")
            cast(wkrB[:], wkrope, "(dc p) r -> p (dc r)")
            cast(wkrSB[:], wkropeS, "(dc p) r -> p (dc r)")
            cast(wqrB[:], wqrope, "(qc p) r -> p (qc r)")
            cast(wqrSB[:], wqropeS, "(qc p) r -> p (qc r)")
            cast(wvdB[:], wvdec, "(kc p) d -> p (kc d)")
        block.gpsimd(p3g)
        gld_w = ch["gw3"].n

        def proj(dst_bf, wB, n_in, n_out, src_bf, dst_f32=None, wSB=None, wait_ld=None):
            ev_tags = []
            for oc in range(n_out):
                for ps in range(4):
                    idx = oc * 4 + ps
                    def mm(t, oc=oc, ps=ps, idx=idx):
                        if idx == 0:
                            t.wait_ge(ch["gw3"].sem, wait_ld if wait_ld else ch["gw3"].n)
                            wait_all(t, [ch["dve"]])
                        elif idx >= 4:
                            t.wait_ge(ch["dve"].sem, ev_tags[idx - 4])
                        for ic in range(n_in):
                            i = t.matmul(PS1[:, ps * 512:ps * 512 + 512],
                                         wB[:, ic * n_out * 128 + oc * 128: ic * n_out * 128 + oc * 128 + 128],
                                         src_bf[:, ic * T + ps * 512: ic * T + ps * 512 + 512],
                                         start=(ic == 0), stop=(ic == n_in - 1))
                        if wSB is not None:
                            for ic in range(n_in):
                                i = t.matmul(PS2[:, ps * 512:ps * 512 + 512],
                                             wSB[:, ic * n_out * 128 + oc * 128: ic * n_out * 128 + oc * 128 + 128],
                                             src_bf[:, ic * T + ps * 512: ic * T + ps * 512 + 512],
                                             start=(ic == 0), stop=(ic == n_in - 1))
                        ch["pe"].tag(i)
                    block.tensor(mm)
                    pe_now = ch["pe"].n

                    def ev(v, oc=oc, ps=ps, pe_now=pe_now):
                        v.wait_ge(ch["pe"].sem, pe_now)
                        dsl = slice(oc * T + ps * 512, oc * T + ps * 512 + 512)
                        ssl = slice(ps * 512, ps * 512 + 512)
                        if wSB is None:
                            if dst_f32 is not None:
                                ch["dve"].tag(v.tensor_copy(out=dst_f32[:, dsl], in_=PS1[:, ssl]))
                            ch["dve"].tag(v.tensor_copy(out=dst_bf[:, dsl], in_=PS1[:, ssl]))
                        else:
                            ch["dve"].tag(v.tensor_mul(out=sq[:, ssl], in0=PS1[:, ssl], in1=cosb[:, ssl]))
                            ch["dve"].tag(v.tensor_mul(out=dst_bf[:, dsl], in0=PS2[:, ssl], in1=sinb[:, ssl]))
                            v.wait_ge(ch["dve"].sem, ch["dve"].n)
                            ch["dve"].tag(v.tensor_add(out=dst_bf[:, dsl], in0=dst_bf[:, dsl], in1=sq[:, ssl]))
                    block.vector(ev)
                    ev_tags.append(ch["dve"].n)

        proj(kvn, wkvB, 8, 2, lnb, dst_f32=kvc32, wait_ld=gld_w)
        kv_pe = []
        kv_dve = ch["dve"].n
        for c2 in range(2):
            def kvsq(s, c2=c2):
                if c2 == 0:
                    s.wait_ge(ch["dve"].sem, kv_dve)
                else:
                    s.wait_ge(ch["pe"].sem, kv_pe[c2 - 1])
                ch["act"].tag(s.activation(out=sq[:], in_=kvc32[:, c2 * T:(c2 + 1) * T],
                                           func=AF.Square))
            block.scalar(kvsq)
            def kvmm(t, c2=c2, anow=ch["act"].n):
                t.wait_ge(ch["act"].sem, anow)
                for ps in range(4):
                    i = t.matmul(PS1[0:1, ps * 512:ps * 512 + 512], onesb[:, 0:1],
                                 sq[:, ps * 512:ps * 512 + 512],
                                 start=(c2 == 0), stop=(c2 == 1))
                ch["pe"].tag(i)
            block.tensor(kvmm)
            kv_pe.append(ch["pe"].n)
        def kvrow(v):
            v.wait_ge(ch["pe"].sem, ch["pe"].n)
            ch["dve"].tag(v.tensor_scalar(out=rows[64:65, :], in0=PS1[0:1, 0:T],
                                          scalar1=1.0 / KVR, scalar2=None, op0=ALU.mult))
            v.wait_ge(ch["dve"].sem, ch["dve"].n)
            ch["dve"].tag(v.tensor_scalar(out=rows[64:65, :], in0=rows[64:65, :],
                                          scalar1=1e-6, scalar2=None, op0=ALU.add))
        block.vector(kvrow)
        emit_rsqrt_row(64)
        emit_bcast(64)
        def kvapply(v):
            v.wait_ge(ch["gc0"].sem, gld_x)
            v.wait_ge(ch["dve"].sem, ch["dve"].n)
            for c2 in range(2):
                ch["dve"].tag(v.tensor_mul(out=kvc32[:, c2 * T:(c2 + 1) * T],
                                           in0=kvc32[:, c2 * T:(c2 + 1) * T], in1=bcast[:]))
            v.wait_ge(ch["dve"].sem, ch["dve"].n)
            for c2 in range(2):
                ch["dve"].tag(v.tensor_scalar(out=kvn[:, c2 * T:(c2 + 1) * T],
                                              in0=kvc32[:, c2 * T:(c2 + 1) * T],
                                              scalar1=gkvb[:, c2:c2 + 1],
                                              scalar2=None, op0=ALU.mult))
        block.vector(kvapply)

        proj(qc, wqB, 8, 4, lnb)
        proj(krp, wkrB, 8, 4, lnb, wSB=wkrSB)
        proj(qrp, wqrB, 4, 4, qc, wSB=wqrSB)

        def vones(v):
            ch["dve"].tag(v.memset(vaug[:].rearrange("p (a b) -> p a b", b=65)[:, :, 64:65], 1.0))
        block.vector(vones)
        vev_tags = []
        for kt in range(TT):
            PSX = PS1 if kt % 2 == 0 else PS2
            def vmm(t, kt=kt, PSX=PSX):
                if kt <= 1:
                    wait_all(t, [ch["dve"], ch["gw3"]])
                else:
                    t.wait_ge(ch["dve"].sem, vev_tags[kt - 2])
                for half in range(2):
                    for c2 in range(2):
                        i = t.matmul(PSX[:, half * 512:half * 512 + 512],
                                     kvn[:, c2 * T + kt * 128: c2 * T + kt * 128 + 128],
                                     wvdB[:, c2 * D + half * 512: c2 * D + half * 512 + 512],
                                     start=(c2 == 0), stop=(c2 == 1))
                ch["pe"].tag(i)
            block.tensor(vmm)
            pe_now = ch["pe"].n
            def vev(v, kt=kt, pe_now=pe_now, PSX=PSX):
                v.wait_ge(ch["pe"].sem, pe_now)
                ch["dve"].tag(v.tensor_copy(
                    out=vaug[:, kt * NH * 65:(kt + 1) * NH * 65].rearrange("p (h c) -> p h c", c=65)[:, :, 0:64],
                    in_=PSX[:, 0:1024].rearrange("p (h c) -> p h c", c=64)))
            block.vector(vev)
            vev_tags.append(ch["dve"].n)

        st2t.close()

        st3a = ExitStack()
        abuf = st3a.enter_context(nc.sbuf_tensor("abuf", [128, 8 * T], BF16))
        st3b = ExitStack()
        E3 = st3b.enter_context
        wkdB = E3(nc.sbuf_tensor("wkdB", [128, 2 * D], BF16))
        wqdB = E3(nc.sbuf_tensor("wqdB", [128, 4 * D], BF16))
        pt = E3(nc.sbuf_tensor("pt", [128, 2 * 1024], BF16))
        qh = E3(nc.sbuf_tensor("qh", [128, T], BF16))
        kh = E3(nc.sbuf_tensor("kh", [128, T], BF16))

        def p4g(g):
            wait_all(g, [ch["pe"], ch["dve"], ch["act"], ch["gw3"]])
            def cast(dst, src, pat):
                ch["gw4"].dma(g.dma_start(out=dst.rearrange("p (c w) -> p c w", c=src.shape[0] // 128), in_=src.rearrange("(c p) w -> p c w", p=128)))
            cast(wkdB[:], wkdec, "(kc p) d -> p (kc d)")
            cast(wqdB[:], wqdec, "(qc p) d -> p (qc d)")
        block.gpsimd(p4g)
        gld_hw = ch["gw4"].n

        nrm_done = None
        for h in range(NH):
            pe_qbase = ch["pe"].n
            def qmm(t, h=h, nrm_prev=nrm_done):
                if h == 0:
                    t.wait_ge(ch["gw4"].sem, gld_hw)
                    wait_all(t, [ch["dve"]])
                else:
                    t.wait_ge(ch["dve"].sem, nrm_prev)
                for ps in range(4):
                    for ic in range(4):
                        t.matmul(PS2[0:64, ps * 512:ps * 512 + 512],
                                 wqdB[:, ic * D + h * 64: ic * D + h * 64 + 64],
                                 qc[:, ic * T + ps * 512: ic * T + ps * 512 + 512],
                                 start=(ic == 0), stop=(ic == 3))
                    for ic in range(2):
                        i = t.matmul(PS1[0:64, ps * 512:ps * 512 + 512],
                                     wkdB[:, ic * D + h * 64: ic * D + h * 64 + 64],
                                     kvn[:, ic * T + ps * 512: ic * T + ps * 512 + 512],
                                     start=(ic == 0), stop=(ic == 1))
                    ch["pe"].tag(i)
            block.tensor(qmm)
            assert ch["pe"].n == pe_qbase + 4
            def qevall(v, h=h, pe_qbase=pe_qbase):
                for ps in range(4):
                    v.wait_ge(ch["pe"].sem, pe_qbase + ps + 1)
                    ssl = slice(ps * 512, ps * 512 + 512)
                    ch["dve"].tag(v.tensor_copy(out=qh[0:64, ssl], in_=PS2[0:64, ssl]))
                    ch["dve"].tag(v.tensor_copy(out=kh[0:64, ssl], in_=PS1[0:64, ssl]))
                rb = (h % 4) * 32
                dc = h // 4
                ch["dve"].tag(v.tensor_copy(out=qh[64:96, :], in_=qrp[rb:rb + 32, dc * T:(dc + 1) * T]))
                ch["dve"].tag(v.tensor_copy(out=kh[64:96, :], in_=krp[rb:rb + 32, dc * T:(dc + 1) * T]))
            block.vector(qevall)
            dve_rope = ch["dve"].n

            secs = []
            for kt in range(TT):
                q0 = kt * 128
                secs.append((kt, q0, min(q0 + 1024, T)))
                if q0 + 1024 < T:
                    secs.append((kt, q0 + 1024, T))
            ns = len(secs)
            act_base = ch["act"].n
            act_sec = [act_base + si + 1 for si in range(ns)]
            pe_pv = [None] * ns
            exp_list = []
            diag_sis = [si for si, (kt, qlo, qhi) in enumerate(secs) if qlo == kt * 128]
            dve_mbase = ch["dve"].n
            dve_m_of = {si: dve_mbase + i + 1 for i, si in enumerate(diag_sis)}

            def pv_body(t, si):
                kt, qlo, qhi = secs[si]
                par = si % 2
                t.wait_ge(ch["act"].sem, act_sec[si])
                b0 = qlo
                while b0 < qhi:
                    b1 = min((b0 // 512 + 1) * 512, qhi)
                    lastkt = min(TT - 1, (b1 - 1) // 128)
                    i = t.matmul(PS2[0:65, b0:b1],
                                 vaug[:, kt * NH * 65 + h * 65: kt * NH * 65 + h * 65 + 65],
                                 pt[:, par * 1024 + (b0 - qlo): par * 1024 + (b1 - qlo)],
                                 start=(kt == 0), stop=(kt == lastkt))
                    b0 = b1
                ch["pe"].tag(i)
            pe_scs = {}
            for si, (kt, qlo, qhi) in enumerate(secs):
                par = si % 2
                ln = qhi - qlo
                pe_before = ch["pe"].n
                def tfrag(t, h=h, si=si, kt=kt, qlo=qlo, qhi=qhi, par=par, ln=ln):
                    if si >= 2:
                        t.wait_ge(ch["act"].sem, act_sec[si - 2])
                    if si == 0:
                        t.wait_ge(ch["dve"].sem, dve_rope)
                    nps = (ln + 511) // 512
                    for ps in range(nps):
                        w = min(512, ln - ps * 512)
                        i = t.matmul(PS1[:, par * 1024 + ps * 512: par * 1024 + ps * 512 + w],
                                     kh[0:96, kt * 128:kt * 128 + 128],
                                     qh[0:96, qlo + ps * 512: qlo + ps * 512 + w],
                                     start=True, stop=True)
                    ch["pe"].tag(i)
                    if si >= 1:
                        pv_body(t, si - 1)
                block.tensor(tfrag)
                pe_scs[si] = pe_before + 1
                if si >= 1:
                    pe_pv[si - 1] = pe_before + 2
                    assert ch["pe"].n == pe_before + 2
                exp_list.append((si, par, ln, pe_scs[si], dve_m_of.get(si),
                                 qlo == kt * 128))
            def lastpv(t):
                pv_body(t, ns - 1)
            block.tensor(lastpv)
            pe_pv[ns - 1] = ch["pe"].n

            def mskall(v, pe_scs=dict(pe_scs)):
                v.wait_ge(ch["gc0"].sem, gld_x)
                for si in diag_sis:
                    v.wait_ge(ch["pe"].sem, pe_scs[si])
                    par = si % 2
                    ch["dve"].tag(v.tensor_add(out=PS1[:, par * 1024: par * 1024 + 128],
                                               in0=PS1[:, par * 1024: par * 1024 + 128],
                                               in1=maskt[:]))
            block.vector(mskall)
            assert ch["dve"].n == dve_mbase + len(diag_sis)

            def expall(s, exp_list=tuple(exp_list), pe_pv=tuple(pe_pv)):
                for si, par, ln, pe_sc, dve_m, diag in exp_list:
                    if si >= 2 and pe_pv[si - 2] is not None:
                        s.wait_ge(ch["pe"].sem, pe_pv[si - 2])
                    if diag:
                        s.wait_ge(ch["dve"].sem, dve_m)
                    else:
                        s.wait_ge(ch["pe"].sem, pe_sc)
                    ch["act"].tag(s.activation(out=pt[:, par * 1024: par * 1024 + ln],
                                               in_=PS1[:, par * 1024: par * 1024 + ln],
                                               func=AF.Exp, scale=SCALE))
            block.scalar(expall)
            assert ch["act"].n == act_sec[-1], (ch["act"].n, act_sec[-1])

            def nrm_row(s, pv_done=pe_pv[ns - 1]):
                s.wait_ge(ch["pe"].sem, pv_done)
                ch["act"].tag(s.activation(out=rows[0:1, :], in_=PS2[64:65, 0:T], func=AF.Ln))
                s.wait_ge(ch["act"].sem, ch["act"].n)
                ch["act"].tag(s.activation(out=rowsb[0:1, :], in_=rows[0:1, :],
                                           func=AF.Exp, scale=-1.0))
            block.scalar(nrm_row)
            def nrm_bc(t):
                t.wait_ge(ch["act"].sem, ch["act"].n)
                t.wait_ge(ch["gc0"].sem, gld_x)
                for ps in range(4):
                    i = t.matmul(PS1[0:64, ps * 512:ps * 512 + 512], onesb[0:1, 0:64],
                                 rowsb[0:1, ps * 512:ps * 512 + 512], start=True, stop=True)
                ch["pe"].tag(i)
            block.tensor(nrm_bc)
            def nrm_ev(v, h=h):
                v.wait_ge(ch["pe"].sem, ch["pe"].n)
                ch["dve"].tag(v.tensor_copy(out=bcast[0:64, :], in_=PS1[0:64, 0:T]))
                v.wait_ge(ch["dve"].sem, ch["dve"].n)
                dc = h // 2
                rb = (h % 2) * 64
                ch["dve"].tag(v.tensor_mul(out=abuf[rb:rb + 64, dc * T:(dc + 1) * T],
                                           in0=PS2[0:64, 0:T], in1=bcast[0:64, :]))
            block.vector(nrm_ev)
            nrm_done = ch["dve"].n

        st3b.close()

        st4 = ExitStack()
        woB = st4.enter_context(nc.sbuf_tensor("woB", [128, 8 * D], BF16))
        xstr = st4.enter_context(nc.sbuf_tensor("xstr", [128, T], F32))
        x1f = st4.enter_context(nc.sbuf_tensor("x1f", [128, T], F32))

        def p5g(g):
            wait_all(g, [ch["pe"], ch["dve"], ch["act"], ch["gw4"]])
            ch["gwo"].dma(g.dma_start(out=woB[:].rearrange("p (c w) -> p c w", c=8),
                                      in_=wo.rearrange("(c p) w -> p c w", p=128)))
        block.gpsimd(p5g)
        gld_wo = ch["gwo"].n

        def woph(oc):
            def mm(t, oc=oc):
                if oc == 0:
                    wait_all(t, [ch["dve"]])
                    t.wait_ge(ch["gwo"].sem, gld_wo)
                for ps in range(4):
                    for ic in range(8):
                        i = t.matmul(PS1[:, ps * 512:ps * 512 + 512],
                                     woB[:, ic * D + oc * 128: ic * D + oc * 128 + 128],
                                     abuf[:, ic * T + ps * 512: ic * T + ps * 512 + 512],
                                     start=(ic == 0), stop=(ic == 7))
                ch["pe"].tag(i)
            block.tensor(mm)
            pe_now = ch["pe"].n
            def ld(s, oc=oc):
                if oc == 0:
                    wait_all(s, [ch["pe"], ch["dve"], ch["act"]])
                else:
                    s.wait_ge(ch["dve"].sem, x1tag[oc - 1][0])
                ch["sld"].dma(s.dma_start(out=xstr[:], in_=xT[oc * 128:(oc + 1) * 128, :]))
            block.sync(ld)
            sld_now = ch["sld"].n
            def ev(v, oc=oc, pe_now=pe_now, sld_now=sld_now):
                v.wait_ge(ch["pe"].sem, pe_now)
                v.wait_ge(ch["sld"].sem, sld_now)
                if oc >= 1:
                    v.wait_ge(ch["gx1"].sem, x1tag[oc - 1][1])
                ch["dve"].tag(v.tensor_add(out=x1f[:], in0=PS1[:, 0:T], in1=xstr[:]))
                v.wait_ge(ch["dve"].sem, ch["dve"].n)
                ch["dve"].tag(v.tensor_copy(out=lnb[:, oc * T:(oc + 1) * T], in_=x1f[:]))
            block.vector(ev)
            def st(g, oc=oc, now=ch["dve"].n):
                g.wait_ge(ch["dve"].sem, now)
                ch["gx1"].dma(g.dma_start(out=x1d.ap()[oc * 128:(oc + 1) * 128, :],
                                          in_=x1f[:]))
                x1tag.append((ch["dve"].n, ch["gx1"].n))
            block.gpsimd(st)
            def gate(t, now=ch["dve"].n):
                t.wait_ge(ch["dve"].sem, now)
            block.tensor(gate)
        x1tag = []
        for oc in range(8):
            woph(oc)
        st4.close()
        st3a.close()
        st2.close()

        emit_ln(lnb, 1e-5)
        st5 = ExitStack()
        E5 = st5.enter_context
        xs2 = E5(nc.sbuf_tensor("xs2", [128, T], F32))
        ln2f = E5(nc.sbuf_tensor("ln2f", [128, T], F32))
        routB = E5(nc.sbuf_tensor("routB", [128, 8 * NE], F32))
        nbc = E5(nc.sbuf_tensor("nbc", [128, T], F32))
        rbc = E5(nc.sbuf_tensor("rbc", [128, T], F32))
        logitsT = E5(nc.sbuf_tensor("logitsT", [128, TT * NE], F32))
        lgf = E5(nc.sbuf_tensor("lgf", [128, T], F32))

        def rload(s):
            wait_all(s, [ch["pe"], ch["dve"], ch["act"], ch["gwo"], ch["gx1"]])
            ch["sr"].dma(s.dma_start(out=routB[:].rearrange("p (c w) -> p c w", c=8), in_=routerT.rearrange("(c p) w -> p c w", p=128)))
        block.sync(rload)
        emit_bcast(32)
        def cpn(v):
            v.wait_ge(ch["dve"].sem, ch["dve"].n)
            ch["dve"].tag(v.tensor_copy(out=nbc[:], in_=bcast[:]))
        block.vector(cpn)
        emit_bcast(64)
        def cpr(v):
            v.wait_ge(ch["dve"].sem, ch["dve"].n)
            ch["dve"].tag(v.tensor_copy(out=rbc[:], in_=bcast[:]))
        block.vector(cpr)

        l2tags = []
        for dc in range(8):
            def l2ld(s, dc=dc):
                if dc == 0:
                    wait_all(s, [ch["dve"]])
                else:
                    s.wait_ge(ch["dve"].sem, l2tags[dc - 1])
                ch["sld"].dma(s.dma_start(out=xs2[:], in_=x1d.ap()[dc * 128:(dc + 1) * 128, :]))
            block.sync(l2ld)
            sld_now = ch["sld"].n
            def l2f(v, dc=dc, sld_now=sld_now):
                v.wait_ge(ch["sld"].sem, sld_now)
                if dc > 0:
                    v.wait_ge(ch["pe"].sem, ch["pe"].n)
                ch["dve"].tag(v.tensor_add(out=ln2f[:], in0=xs2[:], in1=nbc[:]))
                l2tags.append(ch["dve"].n)
                v.wait_ge(ch["dve"].sem, ch["dve"].n)
                ch["dve"].tag(v.tensor_mul(out=ln2f[:], in0=ln2f[:], in1=rbc[:]))
            block.vector(l2f)
            def l2mm(t, dc=dc, now=ch["dve"].n):
                t.wait_ge(ch["dve"].sem, now)
                if dc == 0:
                    t.wait_ge(ch["sr"].sem, 16)
                for ps in range(4):
                    i = t.matmul(PS2[0:NE, ps * 512:ps * 512 + 512],
                                 routB[:, dc * NE:(dc + 1) * NE],
                                 ln2f[:, ps * 512:ps * 512 + 512],
                                 start=(dc == 0), stop=(dc == 7))
                ch["pe"].tag(i)
            block.tensor(l2mm)
        def lgev2(v):
            v.wait_ge(ch["pe"].sem, ch["pe"].n)
            ch["dve"].tag(v.tensor_copy(out=lgf[0:NE, 0:T], in_=PS2[0:NE, 0:T]))
        block.vector(lgev2)
        for tt in range(TT):
            def ltr(t, tt=tt, now=ch["dve"].n):
                t.wait_ge(ch["dve"].sem, now)
                t.wait_ge(ch["gc0"].sem, gld_x)
                i = t.transpose(PS1[:, (tt % 8) * 128:(tt % 8) * 128 + NE],
                                lgf[0:NE, tt * 128:(tt + 1) * 128], identb[0:NE, 0:NE])
                ch["pe"].tag(i)
            block.tensor(ltr)
            pe_now = ch["pe"].n
            def lte(v, tt=tt, pe_now=pe_now):
                v.wait_ge(ch["pe"].sem, pe_now)
                ch["dve"].tag(v.tensor_copy(out=logitsT[:, tt * NE:(tt + 1) * NE],
                                            in_=PS1[:, (tt % 8) * 128:(tt % 8) * 128 + NE]))
            block.vector(lte)
            def ltg(t, now=ch["dve"].n):
                t.wait_ge(ch["dve"].sem, now)
            block.tensor(ltg)

        def l2a(v):
            for dc in range(8):
                ch["dve"].tag(v.tensor_add(out=lnb[:, dc * T:(dc + 1) * T],
                                           in0=lnb[:, dc * T:(dc + 1) * T], in1=nbc[:]))
            v.wait_ge(ch["dve"].sem, ch["dve"].n)
            for dc in range(8):
                ch["dve"].tag(v.tensor_mul(out=lnb[:, dc * T:(dc + 1) * T],
                                           in0=lnb[:, dc * T:(dc + 1) * T], in1=rbc[:]))
        block.vector(l2a)

        mx8 = E5(nc.sbuf_tensor("mx8", [128, TT * 8], F32))
        selb = E5(nc.sbuf_tensor("selb", [128, TT * NE], BF16))
        wn = E5(nc.sbuf_tensor("wn", [128, TT * NE], F32))
        wsum = E5(nc.sbuf_tensor("wsum", [128, 2], F32))
        cumo = E5(nc.sbuf_tensor("cumo", [128, TT * NE], F32))
        selw = E5(nc.sbuf_tensor("selw", [128, NE], F32))

        for tt in range(TT):
            lsl = slice(tt * NE, (tt + 1) * NE)
            def r1(v, tt=tt, lsl=lsl, dnow=ch["dve"].n):
                v.wait_ge(ch["dve"].sem, dnow)
                ch["dve"].tag(v.max(out=mx8[:, tt * 8:(tt + 1) * 8], in_=logitsT[:, lsl]))
                v.wait_ge(ch["dve"].sem, ch["dve"].n)
                ch["dve"].tag(v.tensor_scalar(out=selw[:], in0=logitsT[:, lsl],
                                              scalar1=mx8[:, tt * 8 + 7: tt * 8 + 8],
                                              scalar2=None, op0=ALU.is_ge))
                ch["dve"].tag(v.tensor_scalar(out=wsum[:, 1:2], in0=mx8[:, tt * 8: tt * 8 + 1],
                                              scalar1=-1.0, scalar2=None, op0=ALU.mult))
            block.vector(r1)
            def r2(s, tt=tt, lsl=lsl, now=ch["dve"].n):
                s.wait_ge(ch["dve"].sem, now)
                ch["act"].tag(s.activation(out=wn[:, lsl], in_=logitsT[:, lsl],
                                           func=AF.Exp, bias=wsum[:, 1:2]))
            block.scalar(r2)
            def r3(v, tt=tt, lsl=lsl, now=ch["act"].n):
                v.wait_ge(ch["act"].sem, now)
                ch["dve"].tag(v.scalar_tensor_tensor(out=wn[:, lsl], in0=wn[:, lsl],
                                                     scalar=1.0, in1=selw[:],
                                                     op0=ALU.mult, op1=ALU.mult,
                                                     accum_out=wsum[:, 0:1]))
                ch["dve"].tag(v.tensor_copy(out=selb[:, lsl], in_=selw[:]))
            block.vector(r3)
            def r4(s, now=ch["dve"].n):
                s.wait_ge(ch["dve"].sem, now)
                ch["act"].tag(s.activation(out=wsum[:, 1:2], in_=wsum[:, 0:1], func=AF.Ln))
                s.wait_ge(ch["act"].sem, ch["act"].n)
                ch["act"].tag(s.activation(out=wsum[:, 1:2], in_=wsum[:, 1:2],
                                           func=AF.Exp, scale=-1.0))
            block.scalar(r4)
            def r5(v, tt=tt, lsl=lsl, now=ch["act"].n):
                v.wait_ge(ch["act"].sem, now)
                ch["dve"].tag(v.tensor_scalar(out=wn[:, lsl], in0=wn[:, lsl],
                                              scalar1=wsum[:, 1:2], scalar2=None, op0=ALU.mult))
            block.vector(r5)

        def cums(t):
            wait_all(t, [ch["dve"]])
            for qt in range(TT):
                mms = [("tri", qt)] + [("ones", kt) for kt in range(qt)]
                for j, (kind, kt) in enumerate(mms):
                    lhs = trib[:] if kind == "tri" else onesb[:]
                    i = t.matmul(PS1[:, qt * NE:(qt + 1) * NE], lhs,
                                 selb[:, kt * NE:(kt + 1) * NE],
                                 start=(j == 0), stop=(j == len(mms) - 1))
            ch["pe"].tag(i)
        block.tensor(cums)
        def cume(v):
            v.wait_ge(ch["pe"].sem, ch["pe"].n)
            ch["dve"].tag(v.tensor_copy(out=cumo[:], in_=PS1[:, 0:TT * NE]))
        block.vector(cume)

        def outs(s, now=ch["dve"].n):
            s.wait_ge(ch["dve"].sem, now)
            ch["sst"].dma(s.dma_start(out=o_cum, in_=cumo[:]))
            ch["sst"].dma(s.dma_start(out=o_wn, in_=wn[:]))
            ch["sst"].dma(s.dma_start(out=o_srow, in_=cumo[127:128, 15 * NE:16 * NE]))
            ch["sst"].dma(s.dma_start(out=o_ln2.rearrange("(c p) w -> p c w", p=128), in_=lnb[:].rearrange("p (c w) -> p c w", c=8)))
        block.sync(outs)

        wsguB = E5(nc.sbuf_tensor("wsguB", [128, 8 * 2 * SF], BF16))
        wsdnB = E5(nc.sbuf_tensor("wsdnB", [128, 6 * D], BF16))
        hbuf = E5(nc.sbuf_tensor("hbuf", [128, 6 * T], BF16))

        def shg(g):
            wait_all(g, [ch["pe"], ch["dve"], ch["act"]])
            ch["gsh"].dma(g.dma_start(out=wsguB[:].rearrange("p (c w) -> p c w", c=8), in_=wsgu.rearrange("(c p) w -> p c w", p=128)))
            ch["gsh"].dma(g.dma_start(out=wsdnB[:].rearrange("p (c w) -> p c w", c=6), in_=wsdn.rearrange("(c p) w -> p c w", p=128)))
        block.gpsimd(shg)
        gld_sh = ch["gsh"].n

        gu_tags = []
        act_base7 = ch["act"].n
        pe_base7 = ch["pe"].n
        dve_base7 = ch["dve"].n
        for fc in range(6):
            for ps in range(4):
                idx = fc * 4 + ps
                def gmm(t, fc=fc, ps=ps, idx=idx):
                    if idx == 0:
                        t.wait_ge(ch["gsh"].sem, gld_sh)
                        wait_all(t, [ch["dve"]])
                    elif idx >= 4:
                        t.wait_ge(ch["dve"].sem, dve_base7 + (idx - 4) + 1)
                    for ic in range(8):
                        t.matmul(PS1[:, ps * 512:ps * 512 + 512],
                                 wsguB[:, ic * 2 * SF + fc * 128: ic * 2 * SF + fc * 128 + 128],
                                 lnb[:, ic * T + ps * 512: ic * T + ps * 512 + 512],
                                 start=(ic == 0), stop=(ic == 7))
                    for ic in range(8):
                        i = t.matmul(PS2[:, ps * 512:ps * 512 + 512],
                                     wsguB[:, ic * 2 * SF + SF + fc * 128: ic * 2 * SF + SF + fc * 128 + 128],
                                     lnb[:, ic * T + ps * 512: ic * T + ps * 512 + 512],
                                     start=(ic == 0), stop=(ic == 7))
                    ch["pe"].tag(i)
                block.tensor(gmm)
                def gmu(v, fc=fc, ps=ps, idx=idx):
                    v.wait_ge(ch["act"].sem, act_base7 + idx + 1)
                    ch["dve"].tag(v.tensor_mul(out=hbuf[:, fc * T + ps * 512: fc * T + ps * 512 + 512],
                                               in0=PS2[:, ps * 512:ps * 512 + 512],
                                               in1=sq[:, ps * 512:ps * 512 + 512]))
                block.vector(gmu)
                gu_tags.append(ch["dve"].n)
        def gsl_all(s):
            for fc in range(6):
                for ps in range(4):
                    idx = fc * 4 + ps
                    s.wait_ge(ch["pe"].sem, pe_base7 + idx + 1)
                    ch["act"].tag(s.activation(out=sq[:, ps * 512:ps * 512 + 512],
                                               in_=PS1[:, ps * 512:ps * 512 + 512],
                                               func=AF.Silu))
        block.scalar(gsl_all)
        assert ch["act"].n == act_base7 + 24
        assert ch["pe"].n == pe_base7 + 24

        devtags = []
        gu_last = ch["dve"].n
        for oc in range(8):
            def dmm(t, oc=oc):
                if oc == 0:
                    t.wait_ge(ch["dve"].sem, gu_last)
                for ps in range(4):
                    for ic in range(6):
                        i = t.matmul(PS1[:, ps * 512:ps * 512 + 512],
                                     wsdnB[:, ic * D + oc * 128: ic * D + oc * 128 + 128],
                                     hbuf[:, ic * T + ps * 512: ic * T + ps * 512 + 512],
                                     start=(ic == 0), stop=(ic == 5))
                ch["pe"].tag(i)
            block.tensor(dmm)
            pe_now = ch["pe"].n
            def dld(s, oc=oc):
                if oc >= 1:
                    s.wait_ge(ch["dve"].sem, devtags[oc - 1])
                ch["sld"].dma(s.dma_start(out=xs2[:], in_=x1d.ap()[oc * 128:(oc + 1) * 128, :]))
            block.sync(dld)
            sld_now = ch["sld"].n
            def dev(v, oc=oc, pe_now=pe_now, sld_now=sld_now):
                v.wait_ge(ch["pe"].sem, pe_now)
                v.wait_ge(ch["sld"].sem, sld_now)
                ch["dve"].tag(v.tensor_add(out=ln2f[:], in0=PS1[:, 0:T], in1=xs2[:]))
                devtags.append(ch["dve"].n)
            block.vector(dev)
            def dst(s, oc=oc, now=ch["dve"].n):
                s.wait_ge(ch["dve"].sem, now)
                ch["sst"].dma(s.dma_start(out=o_xsh[oc * 128:(oc + 1) * 128, :], in_=ln2f[:]))
            block.sync(dst)
            def dgate(t, now=ch["dve"].n):
                t.wait_ge(ch["dve"].sem, now)
            def dgate2(v, now=ch["sst"].n):
                v.wait_ge(ch["sst"].sem, now)
            block.tensor(dgate)
            block.vector(dgate2)

        st5.close()

        def fin(s):
            s.wait_ge(ch["sst"].sem, ch["sst"].n)
        block.sync(fin)

    return nc


def l1_inputs(x_core, w, consts):
    return {
        "xT": np.ascontiguousarray(x_core.T),
        "wkv": w["L0_attn_W_kv"], "gkv": np.ascontiguousarray(w["L0_attn_g_kv"].reshape(2, 128).T),
        "wkdec": w["L0_attn_W_kdec"], "wvdec": w["L0_attn_W_vdec"],
        "wq": w["L0_attn_W_q"], "wqdec": w["L0_attn_W_qdec"],
        "wkrope": w["L0_attn_W_krope"], "wkropeS": rope_shuffle_cols(w["L0_attn_W_krope"]),
        "wqrope": w["L0_attn_W_qrope"], "wqropeS": rope_shuffle_cols(w["L0_attn_W_qrope"]),
        "wo": w["L0_attn_W_o"],
        "routerT": np.ascontiguousarray(w["L0_ffn_router_w"].T),
        "wsgu": w["L0_ffn_ws_gate_up"], "wsdn": w["L0_ffn_ws_down"],
        "IDENT": consts["IDENT"], "TRICUM": consts["TRICUM"], "ONES": consts["ONES"],
        "MASKTRI": consts["MASKTRI"], "COS": consts["COS"], "SIN": consts["SIN"],
    }


MAPN = 32896


def build_l2():
    nc = bass.Bass("TRN2", target_bir_lowering=False)
    def din(name, shape, dt=F32):
        return nc.dram_tensor(name, list(shape), dt, kind="ExternalInput").ap()
    cum = din("cum", (128, TT * NE))
    wn = din("wnorm", (128, TT * NE))
    srowA = din("srowA", (8, NE))
    cmask = din("cmask", (8, 1))
    tokid1 = din("tokid1", (128, TT))
    E1024W = din("E1024W", (128, TT * NE))
    E8W = din("E8W", (128, TT * NE))
    TRASHW = din("TRASHW", (128, TT * 8))
    ONESC = din("ONES", (128, 128))

    o_map = nc.dram_tensor("mapc", [MAPN, 1], F32, kind="ExternalOutput").ap()
    o_s8 = nc.dram_tensor("slot8", [128, TT * 8], F32, kind="ExternalOutput").ap()
    o_w8 = nc.dram_tensor("w8v", [128, TT * 8], F32, kind="ExternalOutput").ap()

    with ExitStack() as ctx:
        E = ctx.enter_context
        ch = {k: Ch(ctx, nc, k) for k in ("sld", "gsc", "pe", "dve", "act", "sst")}
        scum = E(nc.sbuf_tensor("scum", [128, TT * NE], F32))
        swn = E(nc.sbuf_tensor("swn", [128, TT * NE], F32))
        ssrow = E(nc.sbuf_tensor("ssrow", [8, NE], F32))
        scm = E(nc.sbuf_tensor("scm", [8, 1], F32))
        stok = E(nc.sbuf_tensor("stok", [128, TT], F32))
        se1024 = E(nc.sbuf_tensor("se1024", [128, TT * NE], F32))
        se8 = E(nc.sbuf_tensor("se8", [128, TT * NE], F32))
        strash = E(nc.sbuf_tensor("strash", [128, TT * 8], F32))
        sonesF = E(nc.sbuf_tensor("sonesF", [128, 128], F32))
        obb = E(nc.sbuf_tensor("obb", [128, NE], F32))
        orow = E(nc.sbuf_tensor("orow", [1, NE], F32))
        zet = E(nc.sbuf_tensor("zet", [128, 257], F32))
        t1 = E(nc.sbuf_tensor("t1", [128, TT * NE], F32))
        t2 = E(nc.sbuf_tensor("t2", [128, TT * NE], F32))
        t3 = E(nc.sbuf_tensor("t3", [128, TT * NE], F32))
        sv8a = E(nc.sbuf_tensor("sv8a", [128, TT * 8], F32))
        su8a = E(nc.sbuf_tensor("su8a", [128, TT * 8], F32))
        svi = E(nc.sbuf_tensor("svi", [128, TT * 8], I32))
        svb = E(nc.sbuf_tensor("svb", [128, TT * 8], I32))
        validf = E(nc.sbuf_tensor("validf", [128, TT * 8], F32))
        sslot = E(nc.sbuf_tensor("sslot", [128, TT * 8], F32))
        sw8 = E(nc.sbuf_tensor("sw8", [128, TT * 8], F32))
        sidx = E(nc.sbuf_tensor("sidx", [128, TT * 8], I32))
        PS = E(nc.psum_tensor("PS", [128, 512], F32))
        block = E(nc.Block())

        def lds(s):
            for dst, srcp in ((scum[:], cum), (swn[:], wn), (ssrow[:], srowA),
                              (scm[:], cmask), (stok[:], tokid1), (se1024[:], E1024W),
                              (se8[:], E8W), (strash[:], TRASHW), (sonesF[:], ONESC)):
                ch["sld"].dma(s.dma_start(out=dst, in_=srcp))
        block.sync(lds)
        sld0 = ch["sld"].n

        def zeroV(v):
            ch["dve"].tag(v.memset(zet[:], 0.0))
        block.vector(zeroV)
        dve_z = ch["dve"].n
        def zero(g):
            g.wait_ge(ch["dve"].sem, dve_z)
            ch["gsc"].dma(g.dma_start(out=o_map.rearrange("(p b) v -> p (b v)", p=128), in_=zet[:]))
        block.gpsimd(zero)

        def off(t):
            t.wait_ge(ch["sld"].sem, sld0)
            i = t.matmul(PS[0:1, 0:NE], scm[:], ssrow[:], start=True, stop=True)
            ch["pe"].tag(i)
        block.tensor(off)
        def offe(v):
            v.wait_ge(ch["pe"].sem, ch["pe"].n)
            ch["dve"].tag(v.tensor_copy(out=orow[:], in_=PS[0:1, 0:NE]))
        block.vector(offe)
        def offb(t, now=ch["dve"].n):
            t.wait_ge(ch["dve"].sem, now)
            i = t.matmul(PS[:, 0:NE], sonesF[0:1, :], orow[:], start=True, stop=True)
            ch["pe"].tag(i)
        block.tensor(offb)
        def offbe(v):
            v.wait_ge(ch["pe"].sem, ch["pe"].n)
            ch["dve"].tag(v.tensor_copy(out=obb[:], in_=PS[:, 0:NE]))
        block.vector(offbe)

        def chain(v):
            C = ch["dve"]
            def step(i):
                C.tag(i)
                v.wait_ge(C.sem, C.n)
            v.wait_ge(C.sem, C.n)
            for tt in range(TT):
                C.tag(v.tensor_add(out=t1[:, tt * NE:(tt + 1) * NE],
                                   in0=scum[:, tt * NE:(tt + 1) * NE], in1=obb[:]))
            v.wait_ge(C.sem, C.n)
            step(v.tensor_scalar(out=t1[:], in0=t1[:], scalar1=1.0, scalar2=None, op0=ALU.subtract))
            step(v.tensor_scalar(out=t3[:], in0=swn[:], scalar1=0.0, scalar2=None, op0=ALU.is_gt))
            step(v.tensor_scalar(out=t2[:], in0=t1[:], scalar1=512.0, scalar2=None, op0=ALU.is_lt))
            step(v.tensor_scalar(out=t1[:], in0=t1[:], scalar1=511.0, scalar2=None, op0=ALU.min))
            step(v.tensor_scalar(out=t1[:], in0=t1[:], scalar1=0.0, scalar2=None, op0=ALU.max))
            step(v.tensor_scalar(out=t1[:], in0=t1[:], scalar1=2.0, scalar2=None, op0=ALU.mult))
            step(v.tensor_add(out=t1[:], in0=t1[:], in1=se1024[:]))
            step(v.tensor_add(out=t1[:], in0=t1[:], in1=t2[:]))
            step(v.tensor_scalar(out=t1[:], in0=t1[:], scalar1=65536.0, scalar2=None, op0=ALU.add))
            step(v.tensor_mul(out=t1[:], in0=t1[:], in1=t3[:]))
            step(v.tensor_scalar(out=t1[:], in0=t1[:], scalar1=65536.0, scalar2=None, op0=ALU.subtract))
            for tt in range(TT):
                C.tag(v.max(out=sv8a[:, tt * 8:(tt + 1) * 8], in_=t1[:, tt * NE:(tt + 1) * NE]))
            v.wait_ge(C.sem, C.n)
            step(v.tensor_scalar(out=t2[:], in0=swn[:], scalar1=4.0, scalar2=None, op0=ALU.mult))
            step(v.tensor_add(out=t2[:], in0=t2[:], in1=se8[:]))
            step(v.tensor_scalar(out=t2[:], in0=t2[:], scalar1=1024.0, scalar2=None, op0=ALU.add))
            step(v.tensor_mul(out=t2[:], in0=t2[:], in1=t3[:]))
            step(v.tensor_scalar(out=t2[:], in0=t2[:], scalar1=1024.0, scalar2=None, op0=ALU.subtract))
            for tt in range(TT):
                C.tag(v.max(out=su8a[:, tt * 8:(tt + 1) * 8], in_=t2[:, tt * NE:(tt + 1) * NE]))
            v.wait_ge(C.sem, C.n)
            step(v.tensor_copy(out=svi[:], in_=sv8a[:]))
            step(v.tensor_scalar(out=svb[:], in0=svi[:], scalar1=1, scalar2=None, op0=ALU.bitwise_and))
            step(v.tensor_scalar(out=svi[:], in0=svi[:], scalar1=1, scalar2=None, op0=ALU.arith_shift_right))
            step(v.tensor_copy(out=sslot[:], in_=svi[:]))
            step(v.tensor_copy(out=validf[:], in_=svb[:]))
            step(v.tensor_scalar(out=t1[:, 0:TT * 8], in0=su8a[:], scalar1=0.125, scalar2=None, op0=ALU.mult))
            step(v.tensor_copy(out=svi[:], in_=t1[:, 0:TT * 8]))
            step(v.tensor_copy(out=t1[:, 0:TT * 8], in_=svi[:]))
            step(v.tensor_scalar(out=t1[:, 0:TT * 8], in0=t1[:, 0:TT * 8], scalar1=-8.0, scalar2=None, op0=ALU.mult))
            step(v.tensor_add(out=t1[:, 0:TT * 8], in0=t1[:, 0:TT * 8], in1=su8a[:]))
            step(v.tensor_scalar(out=t1[:, 0:TT * 8], in0=t1[:, 0:TT * 8], scalar1=0.25, scalar2=None, op0=ALU.mult))
            step(v.tensor_scalar(out=t3[:, 0:TT * 8], in0=t1[:, 0:TT * 8], scalar1=0.0, scalar2=None, op0=ALU.is_lt))
            step(v.tensor_scalar(out=t3[:, 0:TT * 8], in0=t3[:, 0:TT * 8], scalar1=2.0, scalar2=None, op0=ALU.mult))
            step(v.tensor_add(out=t1[:, 0:TT * 8], in0=t1[:, 0:TT * 8], in1=t3[:, 0:TT * 8]))
            step(v.tensor_mul(out=sw8[:], in0=t1[:, 0:TT * 8], in1=validf[:]))
            step(v.tensor_scalar(out=t3[:, 0:TT * 8], in0=validf[:], scalar1=-1.0, scalar2=None, op0=ALU.mult))
            step(v.tensor_scalar(out=t3[:, 0:TT * 8], in0=t3[:, 0:TT * 8], scalar1=1.0, scalar2=None, op0=ALU.add))
            step(v.tensor_mul(out=t3[:, 0:TT * 8], in0=t3[:, 0:TT * 8], in1=strash[:]))
            step(v.tensor_mul(out=t2[:, 0:TT * 8], in0=sslot[:], in1=validf[:]))
            step(v.tensor_add(out=t2[:, 0:TT * 8], in0=t2[:, 0:TT * 8], in1=t3[:, 0:TT * 8]))
            step(v.tensor_copy(out=sidx[:], in_=t2[:, 0:TT * 8]))
        block.vector(chain)

        dnow = ch["dve"].n
        def scat(g):
            g.wait_ge(ch["dve"].sem, dnow)
            g.wait_ge(ch["gsc"].sem, 16)
            for tt in range(TT):
                for k in range(8):
                    ch["gsc"].dma(g.indirect_dma_start(
                        out=o_map, out_offset=bass.IndirectOffsetOnAxis(sidx[:, tt * 8 + k: tt * 8 + k + 1], 0),
                        in_=stok[:, tt:tt + 1], in_offset=None))
            g.wait_ge(ch["gsc"].sem, ch["gsc"].n)
        block.gpsimd(scat)

        def outs(s):
            s.wait_ge(ch["dve"].sem, dnow)
            ch["sst"].dma(s.dma_start(out=o_s8, in_=sslot[:]))
            ch["sst"].dma(s.dma_start(out=o_w8, in_=sw8[:]))
            s.wait_ge(ch["sst"].sem, ch["sst"].n)
        block.sync(outs)
    return nc


def build_l3():
    nc = bass.Bass("TRN2", target_bir_lowering=False)
    def din(name, shape, dt=F32):
        return nc.dram_tensor(name, list(shape), dt, kind="ExternalInput").ap()
    maps = din("maps", (8, MAPN))
    ln2r = din("ln2r", (NC * T, D), BF16)
    myrows = din("myrows", (32, 1), I32)
    wg = din("wg", (NEL, D, FF))
    wu = din("wu", (NEL, D, FF))
    wd = din("wd", (NEL, FF, D))
    IDENT = din("IDENT", (128, 128))
    o_y = nc.dram_tensor("y", [NEL * CAP, D], BF16, kind="ExternalOutput").ap()
    mapd = nc.dram_tensor("mapd", [MAPN, 1], F32)

    with ExitStack() as ctx:
        E = ctx.enter_context
        ch = {k: Ch(ctx, nc, k) for k in ("sld", "gw", "gw2", "gg", "gg2", "pe", "dve", "act", "sst")}
        gwch = [ch["gw"], ch["gw2"]]
        ggx = [ch["gg"], ch["gg2"]]
        smaps = E(nc.sbuf_tensor("smaps", [128, 8 * 257], F32))
        identb = E(nc.sbuf_tensor("identb", [128, 128], BF16))
        identF = E(nc.sbuf_tensor("identF", [128, 128], F32))
        trtags = []
        trtags2 = []
        wfree = []
        smyr = E(nc.sbuf_tensor("smyr", [32, 1], I32))
        mext = E(nc.sbuf_tensor("mext", [32, 128], F32))
        mtr = E(nc.sbuf_tensor("mtr", [128, 32], F32))
        xidx = E(nc.sbuf_tensor("xidx", [128, 32], I32))
        xg = [E(nc.sbuf_tensor(f"xg{i}", [128, 1024], BF16)) for i in range(2)]
        xbT = E(nc.sbuf_tensor("xbT", [128, 8 * (NEL * CAP)], BF16))
        hbuf = E(nc.sbuf_tensor("hbuf", [128, 6 * CAP], BF16))
        sqs = E(nc.sbuf_tensor("sqs", [128, 512], BF16))
        yev = E(nc.sbuf_tensor("yev", [128, D], BF16))
        wgB = [E(nc.sbuf_tensor(f"wgB{i}", [128, 8 * FF], BF16)) for i in range(2)]
        wuB = [E(nc.sbuf_tensor(f"wuB{i}", [128, 8 * FF], BF16)) for i in range(2)]
        wdB = [E(nc.sbuf_tensor(f"wdB{i}", [128, 6 * D], BF16)) for i in range(2)]
        PS1 = E(nc.psum_tensor("PS1", [128, 1024], F32))
        PS2 = E(nc.psum_tensor("PS2", [128, 1024], F32))
        PSB = [E(nc.psum_tensor("PSBa", [128, 128], BF16)),
               E(nc.psum_tensor("PSBb", [128, 128], BF16))]
        block = E(nc.Block())

        def lds(s):
            ch["sld"].dma(s.dma_start(out=smaps[:].rearrange("p (c b) -> p c b", c=8),
                                      in_=maps.rearrange("c (p b) -> p c b", p=128)))
            ch["sld"].dma(s.dma_start(out=smyr[:], in_=myrows))
        block.sync(lds)
        sld0 = ch["sld"].n
        def wload(g, e, par):
            for dst, srcp, nch in ((wgB[par], wg, 8), (wuB[par], wu, 8), (wdB[par], wd, 6)):
                gwch[par].dma(g.dma_start(
                    out=dst[:].rearrange("p (c w) -> p c w", c=nch),
                    in_=srcp[e].rearrange("(c p) w -> p c w", p=128)))
        def ldg0(g):
            ch["gg"].dma(g.dma_start(out=identb[:], in_=IDENT))
            ch["gg"].dma(g.dma_start(out=identF[:], in_=IDENT))
            wload(g, 0, 0)
        block.gpsimd(ldg0)
        gw_id = ch["gg"].n

        def msum(v):
            v.wait_ge(ch["sld"].sem, sld0)
            for c in range(1, 8):
                ch["dve"].tag(v.tensor_add(out=smaps[:, 0:257], in0=smaps[:, 0:257],
                                           in1=smaps[:, c * 257:(c + 1) * 257]))
                v.wait_ge(ch["dve"].sem, ch["dve"].n)
        block.vector(msum)

        def mst(g, now=ch["dve"].n):
            g.wait_ge(ch["dve"].sem, now)
            ch["gg"].dma(g.dma_start(out=mapd.ap().rearrange("(p b) v -> p (b v)", p=128),
                                     in_=smaps[:, 0:257]))
            g.wait_ge(ch["gg"].sem, ch["gg"].n)
            ch["gg"].dma(g.indirect_dma_start(
                out=mext[:], out_offset=None,
                in_=mapd.ap()[0:32768, :].rearrange("(r q) v -> r (q v)", q=128),
                in_offset=bass.IndirectOffsetOnAxis(smyr[:, 0:1], 0)))
            g.wait_ge(ch["gg"].sem, ch["gg"].n)
        block.gpsimd(mst)
        gg_ext = ch["gg"].n

        def mtp(t):
            t.wait_ge(ch["gg"].sem, gg_ext)
            i = t.transpose(PS1[:, 0:32], mext[:], identF[0:32, 0:32])
            ch["pe"].tag(i)
        block.tensor(mtp)
        def mte(v):
            v.wait_ge(ch["pe"].sem, ch["pe"].n)
            ch["dve"].tag(v.tensor_copy(out=mtr[:], in_=PS1[:, 0:32]))
            v.wait_ge(ch["dve"].sem, ch["dve"].n)
            ch["dve"].tag(v.tensor_scalar(out=mtr[:], in0=mtr[:], scalar1=1.0, scalar2=None, op0=ALU.max))
            v.wait_ge(ch["dve"].sem, ch["dve"].n)
            ch["dve"].tag(v.tensor_scalar(out=mtr[:], in0=mtr[:], scalar1=1.0, scalar2=None, op0=ALU.subtract))
            v.wait_ge(ch["dve"].sem, ch["dve"].n)
            ch["dve"].tag(v.tensor_copy(out=xidx[:], in_=mtr[:]))
        block.vector(mte)
        dve_idx = ch["dve"].n

        dve_tr0 = ch["dve"].n
        for j in range(32):
            jp = j % 2
            def gat(g, j=j, jp=jp, dve_idx=dve_idx):
                g.wait_ge(ggx[jp].sem, ggx[jp].n)
                if j == 0:
                    g.wait_ge(ch["dve"].sem, dve_idx)
                    g.wait_ge(ch["gg"].sem, ch["gg"].n)
                elif j >= 2:
                    g.wait_ge(ch["pe"].sem, trtags[j - 2])
                ggx[jp].dma(g.indirect_dma_start(
                    out=xg[jp][:], out_offset=None, in_=ln2r,
                    in_offset=bass.IndirectOffsetOnAxis(xidx[:, j:j + 1], 0)))
            block.gpsimd(gat)
            gg_now = ggx[jp].n
            pe_before = ch["pe"].n
            def trpall(t, j=j, jp=jp, gg_now=gg_now):
                t.wait_ge(ggx[jp].sem, gg_now)
                for dc in range(8):
                    gi = j * 8 + dc
                    if gi >= 2:
                        t.wait_ge(ch["dve"].sem, dve_tr0 + (gi - 2) + 1)
                    i = t.transpose(PSB[gi % 2][:, 0:128],
                                    xg[jp][:, dc * 128:(dc + 1) * 128], identb[:])
                    ch["pe"].tag(i)
            block.tensor(trpall)
            def treall(v, j=j, pe_before=pe_before):
                for dc in range(8):
                    gi = j * 8 + dc
                    v.wait_ge(ch["pe"].sem, pe_before + dc + 1)
                    ch["dve"].tag(v.tensor_copy(
                        out=xbT[:, dc * (NEL * CAP) + j * 128: dc * (NEL * CAP) + j * 128 + 128],
                        in_=PSB[gi % 2][:, 0:128]))
            block.vector(treall)
            assert ch["dve"].n == dve_tr0 + (j + 1) * 8
            trtags.append(ch["pe"].n)

        gwtags = [ch["gw"].n]

        ysttags = []
        for e in range(NEL):
            par = e % 2
            if e + 1 < NEL:
                def wln(g, e=e, par=par):
                    if e >= 1:
                        g.wait_ge(ch["pe"].sem, wfree[e - 1])
                    wload(g, e + 1, 1 - par)
                block.gpsimd(wln)
                gwtags.append(gwch[(e + 1) % 2].n)
            base = e * CAP
            for fc in range(6):
                def gmm(t, e=e, par=par, fc=fc):
                    if fc == 0:
                        t.wait_ge(gwch[par].sem, gwtags[e])
                        wait_all(t, [ch["dve"]])
                    for ic in range(8):
                        t.matmul(PS1[:, 0:512],
                                 wgB[par][:, ic * FF + fc * 128: ic * FF + fc * 128 + 128],
                                 xbT[:, ic * (NEL * CAP) + base: ic * (NEL * CAP) + base + 512],
                                 start=(ic == 0), stop=(ic == 7))
                    for ic in range(8):
                        i = t.matmul(PS2[:, 0:512],
                                     wuB[par][:, ic * FF + fc * 128: ic * FF + fc * 128 + 128],
                                     xbT[:, ic * (NEL * CAP) + base: ic * (NEL * CAP) + base + 512],
                                     start=(ic == 0), stop=(ic == 7))
                    ch["pe"].tag(i)
                block.tensor(gmm)
                pe_now = ch["pe"].n
                def gsl(s, pe_now=pe_now):
                    s.wait_ge(ch["pe"].sem, pe_now)
                    ch["act"].tag(s.activation(out=sqs[:], in_=PS1[:, 0:512], func=AF.Silu))
                block.scalar(gsl)
                def gmu(v, fc=fc, now=ch["act"].n):
                    v.wait_ge(ch["act"].sem, now)
                    ch["dve"].tag(v.tensor_mul(out=hbuf[:, fc * CAP: fc * CAP + 512],
                                               in0=PS2[:, 0:512], in1=sqs[:]))
                block.vector(gmu)
                def ggate(t, nd=ch["dve"].n):
                    t.wait_ge(ch["dve"].sem, nd)
                block.tensor(ggate)
            for sc in range(4):
                for dp in range(2):
                    def dmm(t, e=e, par=par, sc=sc, dp=dp):
                        for ic in range(6):
                            i = t.matmul(PS1[0:128, dp * 512:dp * 512 + 512],
                                         hbuf[:, ic * CAP + sc * 128: ic * CAP + sc * 128 + 128],
                                         wdB[par][:, ic * D + dp * 512: ic * D + dp * 512 + 512],
                                         start=(ic == 0), stop=(ic == 5))
                        ch["pe"].tag(i)
                    block.tensor(dmm)
                pe_now = ch["pe"].n
                def yevf(v, pe_now=pe_now, snow=len(ysttags)):
                    v.wait_ge(ch["pe"].sem, pe_now)
                    if snow >= 1:
                        v.wait_ge(ch["sst"].sem, ysttags[snow - 1])
                    ch["dve"].tag(v.tensor_copy(out=yev[:], in_=PS1[:, 0:1024]))
                block.vector(yevf)
                def yst(s, e=e, sc=sc, now=ch["dve"].n):
                    s.wait_ge(ch["dve"].sem, now)
                    ch["sst"].dma(s.dma_start(out=o_y[e * CAP + sc * 128: e * CAP + sc * 128 + 128, :],
                                              in_=yev[:]))
                block.sync(yst)
                ysttags.append(ch["sst"].n)
                def dgate(t, nd=ch["dve"].n):
                    t.wait_ge(ch["dve"].sem, nd)
                block.tensor(dgate)
            wfree.append(ch["pe"].n)

        def fin(s):
            s.wait_ge(ch["sst"].sem, ch["sst"].n)
        block.sync(fin)
    return nc


def build_l4():
    nc = bass.Bass("TRN2", target_bir_lowering=False)
    def din(name, shape, dt=F32):
        return nc.dram_tensor(name, list(shape), dt, kind="ExternalInput").ap()
    yfull = din("yfull", (NE * CAP, D), BF16)
    slot8 = din("slot8", (128, TT * 8))
    w8v = din("w8v", (128, TT * 8))
    xsht = din("xsht", (T, D))
    o_out = nc.dram_tensor("out", [T, D], F32, kind="ExternalOutput").ap()

    with ExitStack() as ctx:
        E = ctx.enter_context
        ch = {k: Ch(ctx, nc, k) for k in ("sld", "sx", "gg", "gg2", "pe", "dve", "act", "sst")}
        ggch = [ch["gg"], ch["gg2"]]
        ss8 = E(nc.sbuf_tensor("ss8", [128, TT * 8], F32))
        sw8 = E(nc.sbuf_tensor("sw8", [128, TT * 8], F32))
        sidx = E(nc.sbuf_tensor("sidx", [128, TT * 8], I32))
        yg = [[E(nc.sbuf_tensor(f"yg{i}_{k}", [128, D], BF16)) for k in range(8)]
              for i in range(2)]
        sxsh = [E(nc.sbuf_tensor(f"sxsh{i}", [128, D], F32)) for i in range(2)]
        acc = E(nc.sbuf_tensor("acc", [128, D], F32))
        oval = [E(nc.sbuf_tensor(f"oval{i}", [128, D], F32)) for i in range(2)]
        block = E(nc.Block())

        def lds(s):
            ch["sld"].dma(s.dma_start(out=ss8[:], in_=slot8))
            ch["sld"].dma(s.dma_start(out=sw8[:], in_=w8v))
        block.sync(lds)
        sld0 = ch["sld"].n
        def cvt(v):
            v.wait_ge(ch["sld"].sem, sld0)
            ch["dve"].tag(v.tensor_copy(out=sidx[:], in_=ss8[:]))
        block.vector(cvt)
        dve_idx = ch["dve"].n

        gtags = []
        xtags = []
        dtags = []
        stags = []
        for tt in range(TT):
            par = tt % 2
            def gat(g, tt=tt, par=par):
                if tt == 0:
                    g.wait_ge(ch["dve"].sem, dve_idx)
                elif tt >= 2:
                    g.wait_ge(ch["dve"].sem, dtags[tt - 2][0])
                for k in range(8):
                    ggch[par].dma(g.indirect_dma_start(
                        out=yg[par][k][:], out_offset=None, in_=yfull,
                        in_offset=bass.IndirectOffsetOnAxis(sidx[:, tt * 8 + k: tt * 8 + k + 1], 0)))
            block.gpsimd(gat)
            gtags.append(ggch[par].n)
            def xld(s, tt=tt, par=par):
                if tt >= 2:
                    s.wait_ge(ch["dve"].sem, dtags[tt - 2][1])
                ch["sx"].dma(s.dma_start(out=sxsh[par][:], in_=xsht[tt * 128:(tt + 1) * 128, :]))
            block.sync(xld)
            xtags.append(ch["sx"].n)
            def comb(v, tt=tt, par=par, gnow=gtags[tt], xnow=xtags[tt], dprev=ch["dve"].n):
                v.wait_ge(ggch[par].sem, gnow)
                v.wait_ge(ch["dve"].sem, dprev)
                ch["dve"].tag(v.tensor_scalar(out=acc[:], in0=yg[par][0][:],
                                              scalar1=sw8[:, tt * 8: tt * 8 + 1],
                                              scalar2=None, op0=ALU.mult))
                for k in range(1, 8):
                    v.wait_ge(ch["dve"].sem, ch["dve"].n)
                    ch["dve"].tag(v.scalar_tensor_tensor(out=acc[:], in0=yg[par][k][:],
                                                         scalar=sw8[:, tt * 8 + k: tt * 8 + k + 1],
                                                         in1=acc[:], op0=ALU.mult, op1=ALU.add))
                v.wait_ge(ch["dve"].sem, ch["dve"].n)
                v.wait_ge(ch["sx"].sem, xnow)
                if tt >= 1:
                    v.wait_ge(ch["sst"].sem, stags[tt - 1])
                yg_free = ch["dve"].tag(v.tensor_add(out=oval[par][:], in0=acc[:], in1=sxsh[par][:]))
                dtags.append((yg_free, yg_free))
            block.vector(comb)
            def ost(s, tt=tt, par=par, now=ch["dve"].n):
                s.wait_ge(ch["dve"].sem, now)
                ch["sst"].dma(s.dma_start(out=o_out[tt * 128:(tt + 1) * 128, :], in_=oval[par][:]))
            block.sync(ost)
            stags.append(ch["sst"].n)

        def fin(s):
            s.wait_ge(ch["sst"].sem, ch["sst"].n)
        block.sync(fin)
    return nc


def l2_inputs(r1c, srowA, c, consts):
    cmask = np.zeros((8, 1), np.float32); cmask[:c, 0] = 1.0
    tokid1 = (c * T + 1 + np.arange(128, dtype=np.float32)[:, None]
              + 128.0 * np.arange(TT, dtype=np.float32)[None, :])
    e1024 = np.tile((np.arange(NE, dtype=np.float32) * 1024.0)[None, :], (128, TT))
    e8 = np.tile((np.arange(NE, dtype=np.float32) * 8.0)[None, :], (128, TT))
    trashw = np.tile((32768.0 + np.arange(TT * 8, dtype=np.float32))[None, :], (128, 1))
    return {"cum": r1c["cum"], "wnorm": r1c["wnorm"], "srowA": srowA,
            "cmask": cmask, "tokid1": np.ascontiguousarray(tokid1),
            "E1024W": np.ascontiguousarray(e1024), "E8W": np.ascontiguousarray(e8),
            "TRASHW": np.ascontiguousarray(trashw), "ONES": consts["ONES"]}


def kernel(**inputs):
    x = inputs["L0_attn_x"]
    consts = host_consts()
    consts["E1024"] = np.tile((np.arange(NE, dtype=np.float32) * 1024.0)[None, :], (128, 1))
    consts["E8"] = np.tile((np.arange(NE, dtype=np.float32) * 8.0)[None, :], (128, 1))
    cores = list(range(NC))

    nc1 = build_l1()
    in1 = [l1_inputs(np.ascontiguousarray(x[c]), inputs, consts) for c in cores]
    r1 = run_bass_kernel_spmd(nc1, in1, core_ids=cores).results

    srowA = np.ascontiguousarray(np.stack([r1[c]["srow"][0] for c in cores]))
    nc2 = build_l2()
    in2 = [l2_inputs(r1[c], srowA, c, consts) for c in cores]
    r2 = run_bass_kernel_spmd(nc2, in2, core_ids=cores).results

    maps = np.ascontiguousarray(np.stack([r2[c]["mapc"][:, 0] for c in cores]))
    ln2r = np.ascontiguousarray(
        np.concatenate([r1[c]["ln2b"].T for c in cores], axis=0))
    nc3 = build_l3()
    in3 = []
    for c in cores:
        in3.append({
            "maps": maps, "ln2r": ln2r,
            "myrows": np.arange(c * 32, c * 32 + 32, dtype=np.int32).reshape(32, 1),
            "wg": np.ascontiguousarray(inputs["L0_ffn_w_gate"][c * NEL:(c + 1) * NEL]),
            "wu": np.ascontiguousarray(inputs["L0_ffn_w_up"][c * NEL:(c + 1) * NEL]),
            "wd": np.ascontiguousarray(inputs["L0_ffn_w_down"][c * NEL:(c + 1) * NEL]),
            "IDENT": consts["IDENT"],
        })
    r3 = run_bass_kernel_spmd(nc3, in3, core_ids=cores).results

    yfull = np.ascontiguousarray(np.concatenate([r3[c]["y"] for c in cores], axis=0))
    nc4 = build_l4()
    in4 = []
    for c in cores:
        in4.append({
            "yfull": yfull, "slot8": r2[c]["slot8"], "w8v": r2[c]["w8v"],
            "xsht": np.ascontiguousarray(r1[c]["xsh"].T),
        })
    r4 = run_bass_kernel_spmd(nc4, in4, core_ids=cores).results
    return np.stack([r4[c]["out"] for c in cores]).astype(np.float32)
